# version 07
# baseline (speedup 1.0000x reference)
import numpy as np
from contextlib import ExitStack

import concourse.bass as bass
import concourse.mybir as mybir
from concourse.bass_utils import run_bass_kernel_spmd
from concourse.bass_core_rust import IndirectOffsetOnAxis

F32 = mybir.dt.float32
F32R = mybir.dt.float32r
BF16 = mybir.dt.bfloat16
I32 = mybir.dt.int32
AF = mybir.ActivationFunctionType
ALU = mybir.AluOpType
AX = mybir.AxisListType

NCORE = 8
DM = 1024
NST = 16
DTR = 64
B = 2
LQ = 8192
T = B * LQ
TPC = T // NCORE
CPC = DM // NCORE
CK = 512
NCK = T // CK
CKPC = NCK // NCORE
NE = 32
EI = 512
NEX = NE + 2
TB = 1024
SUB = 256
NSUB = TB // SUB
EG = 2
NSLOT = 4

ENGINES = ("sync", "tensor", "vector", "scalar", "gpsimd")


class Sem:
    def __init__(self, ctx, nc, name):
        self.h = ctx.enter_context(nc.semaphore(name))
        self.c = 0


class Plan:
    def __init__(self, self_sems=None):
        self.ops = {e: [] for e in ENGINES}
        self.self_sems = self_sems or {}

    def op(self, eng, emit, waits=(), inc=None, inc_n=1):
        w = [(s.h, v) for (s, v) in waits if v > 0]
        ss = self.self_sems.get(eng)
        if ss is not None and ss.c > 0:
            w.append((ss.h, ss.c))
        if inc is not None:
            inc.c += inc_n
            self.ops[eng].append((w, emit, inc.h, inc_n))
            return inc.c
        self.ops[eng].append((w, emit, None, 0))
        return None

    def emit(self, block):
        for eng_name in ENGINES:
            lst = self.ops[eng_name]
            if not lst:
                continue

            def body(eng, lst=lst):
                for waits, emit, sem_h, n in lst:
                    for (h, v) in waits:
                        eng.wait_ge(h, v)
                    ins = emit(eng)
                    if sem_h is not None:
                        if ins is None:
                            eng.sem_inc(sem_h, n)
                        else:
                            ins.then_inc(sem_h, n)
            getattr(block, eng_name)(body)


def _build(nc):
    dram = {}

    def din(name, shape, dt=F32):
        dram[name] = nc.dram_tensor(name, list(shape), dt, kind="ExternalInput")
        return dram[name]

    u_tokT = din("u_tokT", (DM, TPC + 3), F32R)
    u_chanT = din("u_chanT", (CPC, T), F32R)
    dtwT = din("dtwT", (DTR, CPC), F32R)
    dtb = din("dtb", (CPC, 1), F32)
    alog = din("alog", (128, 16), F32)
    dcol = din("dcol", (CPC, 1), F32)
    xpwT = din("xpwT", (DM, 96), F32)
    cw = din("cw", (DM, 4), F32)
    cb = din("cb", (DM, 1), F32)
    gatewT = din("gatewT", (DM, NE), F32)
    gateb = din("gateb", (1, NE), F32)
    selrep = din("selrep", (16, 128, 128), F32R)
    selcon = din("selcon", (16, 128, 128), F32R)
    selbc = din("selbc", (NST, 128), F32R)
    ones_row = din("ones_row", (1, 128), F32R)
    ones_col = din("ones_col", (128, 1), F32R)
    ident = din("ident", (128, 128), F32)
    ckidx = din("ckidx", (128, 8 * CKPC), I32)
    wgT = din("wgT", (NEX, DM, EI), F32)
    wuT = din("wuT", (NEX, DM, EI), F32)
    wdT = din("wdT", (NEX, EI, DM), F32)
    out_h = nc.dram_tensor("out", [TPC, DM], F32, kind="ExternalOutput")

    cc_prep_in = nc.dram_tensor("cc_prep_in", [96, TPC], F32R, kind="Internal")
    cc_prep_out = nc.dram_tensor("cc_prep_out", [96 * NCORE, TPC], F32R,
                                 kind="Internal")
    cc_y_in = nc.dram_tensor("cc_y_in", [4, CPC, 4096], F32R, kind="Internal")
    cc_y_out = nc.dram_tensor("cc_y_out", [4, DM, 4096], F32R, kind="Internal")
    xres_spill = nc.dram_tensor("xres_spill", [DM, TPC], F32, kind="Internal")
    hb_spill = nc.dram_tensor("hb_spill", [DM, TPC], BF16, kind="Internal")
    pre2_spill = nc.dram_tensor("pre2_spill", [DM, TPC], F32, kind="Internal")
    cflat_spill = nc.dram_tensor("cflat_spill", [NE, TPC], F32R, kind="Internal")

    octx = ExitStack()
    with octx:
        def osb(name, shape, dt=F32):
            return octx.enter_context(nc.sbuf_tensor(name, list(shape), dt))

        def sem(name):
            return Sem(octx, nc, name)

        CONSTS = [0.0, 1.0, -1.0, -0.5, 1e-5, 2.5]
        s_const = osb("s_const", (128, len(CONSTS)))
        for i, v in enumerate(CONSTS):
            nc.const_aps.register(F32, v, s_const[:, i:i + 1])
        s_acol = osb("s_acol", (128, 16))
        s_ainv = osb("s_ainv", (128, 16))
        s_dp1 = osb("s_dp1", (CPC, 1))
        s_dtb = osb("s_dtb", (CPC, 1))
        s_carry = osb("s_carry", (128, 16))
        s_zero = osb("s_zero", (128, 1))
        s_onr = osb("s_onr", (1, 128), F32R)
        s_onc = osb("s_onc", (128, 1), F32R)
        s_onc32 = osb("s_onc32", (128, 1))
        s_onr32 = osb("s_onr32", (1, 128))
        s_id = osb("s_id", (128, 128))
        s_gw = osb("s_gw", (128, 8 * NE))
        s_gbb = osb("s_gbb", (128, NE))
        s_gbr = osb("s_gbr", (1, NE), F32)
        s_ckidx = osb("s_ckidx", (128, 8 * CKPC), I32)
        s_tmp16 = osb("s_tmp16", (128, 16))

        ld0 = sem("ld0")
        ld1 = sem("ld1")
        lu = [sem("lu0"), sem("lu1")]
        ldt = [sem("ldt0"), sem("ldt1")]
        lbc = [sem("lbc0"), sem("lbc1")]
        ys = [sem("ys0"), sem("ys1")]
        lyg = [sem("lyg0"), sem("lyg1")]
        ssp = [sem("ssp0"), sem("ssp1")]
        shb = sem("shb")
        scf = sem("scf")
        stp = [sem("stp0"), sem("stp1")]
        sw = [sem(f"sw{i}") for i in range(NSLOT)]
        scr = [sem("scr0"), sem("scr1")]
        lhb = sem("lhb")
        lxr = [sem("lxr0"), sem("lxr1")]
        sp2 = sem("sp2")
        lp2 = [sem("lp20"), sem("lp21")]
        so = [sem("so0"), sem("so1")]
        cc = sem("cc")
        pe = sem("pe")
        act = sem("act")
        dve = sem("dve")
        pool = sem("pool")
        bar = sem("bar")

        phases = []

        def new_phase():
            P = Plan(self_sems={"vector": dve, "scalar": act, "gpsimd": pool})
            phases.append(P)
            return P

        ALL_DMA = ([ld0, ld1, lu[0], lu[1], ldt[0], ldt[1], lbc[0], lbc[1],
                    ys[0], ys[1], lyg[0], lyg[1], ssp[0], ssp[1], shb, scf,
                    stp[0], stp[1]] + sw + scr + [lhb, lxr[0], lxr[1], sp2,
                    lp2[0], lp2[1], so[0], so[1]])

        def barrier(P):
            for e in ENGINES:
                P.op(e, lambda eng: None,
                     waits=[(s, s.c) for s in ALL_DMA], inc=bar)
            tgt = bar.c
            return tgt

        def phase_wait(P, tgt):
            for e in ENGINES:
                P.op(e, lambda eng: None, waits=[(bar, tgt)])

        def ld16(P, q, s, dst_fn, src_fn, waits=()):
            return P.op(q, lambda e: e.dma_start(out=dst_fn(), in_=src_fn()),
                        waits=waits, inc=s, inc_n=16)

        P = new_phase()
        p1ctx = ExitStack()
        with p1ctx:
            def sb(name, shape, dt=F32):
                return p1ctx.enter_context(nc.sbuf_tensor(name, list(shape), dt))

            for i, v in enumerate(CONSTS):
                P.op("vector", lambda e, i=i, v=v: e.memset(s_const[:, i:i+1], v),
                     inc=dve)
            P.op("vector", lambda e: e.memset(s_zero[:, :], 0.0), inc=dve)
            P.op("vector", lambda e: e.memset(s_carry[:, :], 0.0), inc=dve)
            P.op("vector", lambda e: e.memset(s_onc32[:, :], 1.0), inc=dve)
            P.op("vector", lambda e: e.memset(s_onr32[:, :], 1.0), inc=dve)
            const_ready = dve.c

            ld16(P, "sync", ld0, lambda: s_onr[:, :], lambda: ones_row.ap())
            ld16(P, "sync", ld0, lambda: s_onc[:, :], lambda: ones_col.ap())
            ld16(P, "sync", ld0, lambda: s_id[:, :], lambda: ident.ap())
            ld16(P, "sync", ld0, lambda: s_ckidx[:, :], lambda: ckidx.ap())
            ld16(P, "sync", ld0, lambda: s_gbr[:, :], lambda: gateb.ap())
            for d in range(8):
                ld16(P, "sync", ld0, lambda d=d: s_gw[:, NE*d:NE*(d+1)],
                     lambda d=d: gatewT.ap()[128*d:128*(d+1), :])
            ld16(P, "sync", ld0, lambda: s_tmp16[:, :], lambda: alog.ap())
            s_dcol_t = sb("s_dcol_t", (CPC, 1))
            ld16(P, "sync", ld0, lambda: s_dcol_t[:, :], lambda: dcol.ap())
            ld16(P, "sync", ld0, lambda: s_dtb[:, :], lambda: dtb.ap())
            c_small = ld0.c

            a1 = P.op("scalar", lambda e: e.activation(
                s_acol[:, :], s_tmp16[:, :], AF.Exp),
                waits=[(ld0, c_small), (dve, const_ready)], inc=act)
            a2 = P.op("scalar", lambda e: e.activation(
                s_ainv[:, :], s_tmp16[:, :], AF.Exp, scale=-1.0), inc=act)
            P.op("vector", lambda e: e.tensor_scalar(
                s_acol[:, :], s_acol[:, :], -1.0, None, op0=ALU.mult),
                waits=[(act, a2)], inc=dve)
            ainv_ready = P.op("vector", lambda e: e.tensor_scalar(
                s_ainv[:, :], s_ainv[:, :], -1.0, None, op0=ALU.mult), inc=dve)
            acol_ready = ainv_ready
            dp1_ready = P.op("vector", lambda e: e.tensor_scalar(
                s_dp1[:, :], s_dcol_t[:, :], 1.0, None, op0=ALU.add),
                waits=[(ld0, c_small)], inc=dve)

            UW = TPC + 3
            s_utok = sb("s_utok", (128, 8 * UW), F32R)
            s_xpw = sb("s_xpw", (128, 8 * 96))
            s_cw = sb("s_cw", (128, 8 * 4))
            s_cb = sb("s_cb", (128, 8))
            s_wk = sb("s_wk", (128, 8 * 4 * 96), F32R)
            s_c96 = sb("s_c96", (96, 1))
            s_proj = sb("s_proj", (96, 2 * CK), F32R)
            pp = p1ctx.enter_context(nc.psum_tensor("pp", [96, CK], F32))
            pc96 = p1ctx.enter_context(nc.psum_tensor("pc96", [96, 1], F32))

            for d in range(8):
                ld16(P, "sync", ld1, lambda d=d: s_utok[:, UW*d:UW*(d+1)],
                     lambda d=d: u_tokT.ap()[128*d:128*(d+1), :])
                ld16(P, "sync", ld1, lambda d=d: s_xpw[:, 96*d:96*(d+1)],
                     lambda d=d: xpwT.ap()[128*d:128*(d+1), :])
                ld16(P, "sync", ld1, lambda d=d: s_cw[:, 4*d:4*(d+1)],
                     lambda d=d: cw.ap()[128*d:128*(d+1), :])
                ld16(P, "sync", ld1, lambda d=d: s_cb[:, d:d+1],
                     lambda d=d: cb.ap()[128*d:128*(d+1), :])
            c_p1 = ld1.c

            for d in range(8):
                for k in range(4):
                    P.op("vector", lambda e, d=d, k=k: e.tensor_scalar(
                        s_wk[:, d*384 + k*96: d*384 + (k+1)*96],
                        s_xpw[:, 96*d:96*(d+1)], s_cw[:, 4*d+k:4*d+k+1], None,
                        op0=ALU.mult),
                        waits=[(ld1, c_p1)] if (d == 0 and k == 0) else (),
                        inc=dve)
            wk_ready = dve.c

            for d in range(8):
                P.op("tensor", lambda e, d=d: e.matmul(
                    pc96[:, :], s_xpw[:, 96*d:96*(d+1)], s_cb[:, d:d+1],
                    start=(d == 0), stop=(d == 7)),
                    waits=[(dve, wk_ready)] if d == 0 else (),
                    inc=pe if d == 7 else None)
            c96_mm = pe.c
            c96_ready = P.op("scalar", lambda e: e.copy(s_c96[:, :], pc96[:, :]),
                             waits=[(pe, c96_mm)], inc=act)

            prev_evac = 0
            for ck_i in range(CKPC):
                buf = ck_i % 2
                for d in range(8):
                    for k in range(4):
                        col0 = UW*d + ck_i*CK + k
                        w = [(dve, prev_evac)] if (d == 0 and k == 0) else ()
                        P.op("tensor", lambda e, d=d, k=k, col0=col0: e.matmul(
                            pp[:, :], s_wk[:, d*384 + k*96: d*384 + (k+1)*96],
                            s_utok[:, col0:col0+CK],
                            start=(d == 0 and k == 0), stop=(d == 7 and k == 3)),
                            waits=w, inc=pe if (d == 7 and k == 3) else None)
                mm_done = pe.c
                prev_evac = P.op("vector", lambda e, buf=buf: e.tensor_scalar(
                    s_proj[:, CK*buf:CK*(buf+1)], pp[:, :], s_c96[:, :1], None,
                    op0=ALU.add),
                    waits=[(pe, mm_done), (act, c96_ready),
                           (stp[buf], stp[buf].c)], inc=dve)
                P.op("sync", lambda e, ck_i=ck_i, buf=buf: e.dma_start(
                    out=cc_prep_in.ap()[:, CK*ck_i:CK*(ck_i+1)],
                    in_=s_proj[:, CK*buf:CK*(buf+1)]),
                    waits=[(dve, prev_evac)], inc=stp[buf], inc_n=16)

            cc_prep_done = P.op("gpsimd", lambda e: e.collective_compute(
                "AllGather", ALU.bypass, replica_groups=[list(range(NCORE))],
                ins=[cc_prep_in.ap().opt()], outs=[cc_prep_out.ap().opt()]),
                waits=[(stp[0], stp[0].c), (stp[1], stp[1].c)], inc=cc)
            bar1 = barrier(P)
            with nc.Block("ph1") as blk:
                P.emit(blk)

        P = new_phase()
        phase_wait(P, bar1)
        p3ctx = ExitStack()
        with p3ctx:
            def sb(name, shape, dt=F32):
                return p3ctx.enter_context(nc.sbuf_tensor(name, list(shape), dt))

            s_u3 = sb("s_u3", (128, 2 * CK), F32R)
            s_dt3 = sb("s_dt3", (64, 2 * CK), F32R)
            s_bt3 = sb("s_bt3", (NST, 2 * CK), F32R)
            s_ct3 = sb("s_ct3", (NST, 2 * CK), F32R)
            s_selr = sb("s_selr", (128, 16 * 128), F32R)
            s_selc = sb("s_selc", (128, 16 * 128), F32R)
            s_selb = sb("s_selb", (NST, 128), F32R)
            s_dtw = sb("s_dtw", (DTR, CPC), F32R)
            s_sp = sb("s_sp", (128, CK))
            s_delta = sb("s_delta", (128, 2 * CK), F32R)
            RDA, RW, RT = 6, 4, 8
            s_dA = sb("s_dA", (128, RDA * CK))
            s_w1 = sb("s_w1", (128, RW * CK))
            s_w2 = sb("s_w2", (128, RW * CK))
            s_hh = sb("s_hh", (128, 32 * CK))
            s_t1 = sb("s_t1", (128, RT * CK), F32R)
            s_brep = sb("s_brep", (128, 2 * CK))
            s_crep = sb("s_crep", (128, 2 * CK))
            s_y3 = sb("s_y3", (128, 2 * CK), F32R)
            s_tp = sb("s_tp", (128, 2 * CK))
            psh = p3ctx.enter_context(nc.psum_tensor("psh", [128, CK], F32))
            pdr = [p3ctx.enter_context(nc.psum_tensor(f"pdr{i}", [128, CK], F32))
                   for i in range(2)]
            pur = [p3ctx.enter_context(nc.psum_tensor(f"pur{i}", [128, CK], F32))
                   for i in range(3)]
            py3 = [p3ctx.enter_context(nc.psum_tensor(f"py3{i}", [128, CK], F32))
                   for i in range(2)]

            for g in range(16):
                ld16(P, "sync", ld1, lambda g=g: s_selr[:, 128*g:128*(g+1)],
                     lambda g=g: selrep.ap()[g, :, :])
                ld16(P, "sync", ld1, lambda g=g: s_selc[:, 128*g:128*(g+1)],
                     lambda g=g: selcon.ap()[g, :, :])
            ld16(P, "sync", ld1, lambda: s_selb[:, :], lambda: selbc.ap())
            ld16(P, "sync", ld1, lambda: s_dtw[:, :], lambda: dtwT.ap())
            sel_ready = ld1.c

            ev = {
                "w2_free": [0] * RW,
                "t1_free": [0] * RT,
                "u_free": [0, 0],
                "dt_free": [0, 0],
                "bc_free": [0, 0],
                "delta_free": [0, 0],
                "brep_free": [0, 0],
                "crep_free": [(0, 0), (0, 0)],
                "psh_free_act": 0,
                "pdr_free": [0] * 2,
                "pur_free": [0] * 3,
                "py_free": [0, 0],
                "y_free": [0, 0],
                "w1s_free": [0] * RW,
                "loads": {},
                "spq": [],
                "a": {},
            }
            dA_free = [0] * RDA
            CLAG = 4
            pend_con = []

            def hh_col(ck, g):
                return ((ck % 2) * 16 + g) * CK

            def emit_loads(Pp, ck):
                buf = ck % 2
                rk, cko = ck // CKPC, ck % CKPC
                c_u = Pp.op("sync", lambda e, ck=ck, buf=buf: e.dma_start(
                    out=s_u3[:, CK*buf:CK*(buf+1)],
                    in_=u_chanT.ap()[:, CK*ck:CK*(ck+1)]),
                    waits=[(dve, ev["u_free"][buf])], inc=lu[buf], inc_n=16)
                c_dt = Pp.op("sync", lambda e, rk=rk, cko=cko, buf=buf:
                             e.dma_start(
                    out=s_dt3[:, CK*buf:CK*(buf+1)],
                    in_=cc_prep_out.ap()[96*rk:96*rk+64, CK*cko:CK*(cko+1)]),
                    waits=[(cc, cc_prep_done), (pe, ev["dt_free"][buf])],
                    inc=ldt[buf], inc_n=16)
                Pp.op("sync", lambda e, rk=rk, cko=cko, buf=buf: e.dma_start(
                    out=s_bt3[:, CK*buf:CK*(buf+1)],
                    in_=cc_prep_out.ap()[96*rk+64:96*rk+80, CK*cko:CK*(cko+1)]),
                    waits=[(pe, ev["bc_free"][buf])], inc=lbc[buf], inc_n=16)
                c_c = Pp.op("sync", lambda e, rk=rk, cko=cko, buf=buf:
                            e.dma_start(
                    out=s_ct3[:, CK*buf:CK*(buf+1)],
                    in_=cc_prep_out.ap()[96*rk+80:96*rk+96, CK*cko:CK*(cko+1)]),
                    inc=lbc[buf], inc_n=16)
                ev["loads"][ck] = (c_u, c_dt, c_c)

            def emit_prep1(Pp, ck):
                buf = ck % 2
                (c_u, c_dt, c_c) = ev["loads"][ck]
                pe_d = Pp.op("tensor", lambda e, buf=buf: e.matmul(
                    psh[:, :], s_dtw[:, :], s_dt3[:, CK*buf:CK*(buf+1)],
                    start=True, stop=True),
                    waits=[(ldt[buf], c_dt), (ld1, sel_ready),
                           (act, ev["psh_free_act"])], inc=pe)
                ev["dt_free"][buf] = pe.c
                a_sp = Pp.op("scalar", lambda e: e.activation(
                    s_sp[:, :], psh[:, :], AF.Exp, bias=s_dtb[:, :1]),
                    waits=[(pe, pe_d)], inc=act)
                ev["a"].setdefault(ck, {})["a_sp"] = a_sp

            def emit_prep2(Pp, ck):
                buf = ck % 2
                (c_u, c_dt, c_c) = ev["loads"][ck]
                a_sp = ev["a"][ck]["a_sp"]
                a_dl = Pp.op("scalar", lambda e, buf=buf: e.activation(
                    s_delta[:, CK*buf:CK*(buf+1)], s_sp[:, :], AF.Ln, bias=1.0),
                    waits=[(pe, ev["delta_free"][buf])], inc=act)
                pe_b = Pp.op("tensor", lambda e, buf=buf: e.matmul(
                    psh[:, :], s_selb[:, :], s_bt3[:, CK*buf:CK*(buf+1)],
                    start=True, stop=True),
                    waits=[(lbc[buf], c_c), (act, a_sp)], inc=pe)
                a_b = Pp.op("scalar", lambda e, buf=buf: e.activation(
                    s_brep[:, CK*buf:CK*(buf+1)], psh[:, :], AF.Copy),
                    waits=[(pe, pe_b), (pool, ev["brep_free"][buf])], inc=act)
                ev["a"][ck].update(a_dl=a_dl, a_b=a_b)

            def emit_prep3(Pp, ck):
                buf = ck % 2
                a_b = ev["a"][ck]["a_b"]
                pe_c = Pp.op("tensor", lambda e, buf=buf: e.matmul(
                    psh[:, :], s_selb[:, :], s_ct3[:, CK*buf:CK*(buf+1)],
                    start=True, stop=True),
                    waits=[(act, a_b)], inc=pe)
                ev["bc_free"][buf] = pe.c
                a_c = Pp.op("scalar", lambda e, buf=buf: e.activation(
                    s_crep[:, CK*buf:CK*(buf+1)], psh[:, :], AF.Copy),
                    waits=[(pe, pe_c), (dve, ev["crep_free"][buf][0]),
                           (pool, ev["crep_free"][buf][1])], inc=act)
                ev["psh_free_act"] = a_c
                ev["a"][ck]["a_c"] = a_c

            def emit_con(Pp, item):
                g, gt, ck, d_t1, t1_sem = item
                pyb = ck % 2
                pe_con = Pp.op("tensor", lambda e, g=g, gt=gt, pyb=pyb: e.matmul(
                    py3[pyb][:, :], s_selc[:, 128*g:128*(g+1)],
                    s_t1[:, CK*gt:CK*(gt+1)],
                    start=(g == 0), stop=(g == 15)),
                    waits=[(t1_sem, d_t1)] +
                          ([(dve, ev["py_free"][pyb])] if g == 0 else []),
                    inc=pe)
                ev["t1_free"][gt] = pe_con
                if g == 15:
                    buf = ck % 2
                    d_y = Pp.op("vector", lambda e, buf=buf, pyb=pyb:
                                e.scalar_tensor_tensor(
                        s_y3[:, CK*buf:CK*(buf+1)],
                        s_u3[:, CK*buf:CK*(buf+1)].bitcast(F32),
                        s_dp1[:, :1], py3[pyb][:, :],
                        op0=ALU.mult, op1=ALU.add),
                        waits=[(pe, pe_con), (dve, dp1_ready),
                               (ys[buf], ev["y_free"][buf])], inc=dve)
                    ev["u_free"][buf] = d_y
                    ev["py_free"][pyb] = d_y
                    Pp.op("gpsimd", lambda e, ck=ck, buf=buf: e.dma_start(
                        out=cc_y_in.ap()[ck // 8][:, (ck % 8)*CK:
                                                  (ck % 8 + 1)*CK],
                        in_=s_y3[:, CK*buf:CK*(buf+1)]),
                        waits=[(dve, d_y)], inc=ys[buf], inc_n=16)
                    ev["y_free"][buf] = ys[buf].c
                    if ck % 8 == 7:
                        q = ck // 8
                        Pp.op("gpsimd", lambda e, q=q: e.collective_compute(
                            "AllGather", ALU.bypass,
                            replica_groups=[list(range(NCORE))],
                            ins=[cc_y_in.ap()[q].opt()],
                            outs=[cc_y_out.ap()[q].opt()]),
                            waits=[(ys[0], ys[0].c), (ys[1], ys[1].c)], inc=cc)

            emit_loads(P, 0)
            emit_prep1(P, 0)
            emit_prep2(P, 0)
            emit_prep3(P, 0)
            for ckk in range(NCK):
                buf = ckk % 2
                a_dl = ev["a"][ckk]["a_dl"]
                a_c = ev["a"][ckk]["a_c"]
                (c_u, c_dt, c_c) = ev["loads"][ckk]
                for g in range(16):
                    gw = g % RW
                    gda = g % RDA
                    gt = (ckk * 16 + g) % RT
                    gp2 = g % 2
                    gp3 = g % 3
                    if len(pend_con) > CLAG:
                        emit_con(P, pend_con.pop(0))
                    if ckk + 1 < NCK:
                        if g == 6:
                            emit_loads(P, ckk + 1)
                        elif g == 8:
                            emit_prep1(P, ckk + 1)
                        elif g == 10:
                            emit_prep2(P, ckk + 1)
                        elif g == 12:
                            emit_prep3(P, ckk + 1)
                    pe_dr = P.op("tensor", lambda e, g=g, buf=buf, gp2=gp2:
                                 e.matmul(
                        pdr[gp2][:, :], s_selr[:, 128*g:128*(g+1)],
                        s_delta[:, CK*buf:CK*(buf+1)], start=True, stop=True),
                        waits=[(act, a_dl), (act, ev["pdr_free"][gp2])], inc=pe)
                    if g == 15:
                        ev["delta_free"][buf] = pe.c
                    pe_ur = P.op("tensor", lambda e, g=g, buf=buf, gp3=gp3:
                                 e.matmul(
                        pur[gp3][:, :], s_selr[:, 128*g:128*(g+1)],
                        s_u3[:, CK*buf:CK*(buf+1)], start=True, stop=True),
                        waits=[(lu[buf], c_u), (dve, ev["pur_free"][gp3])],
                        inc=pe)
                    a_dA = P.op("scalar", lambda e, g=g, gp2=gp2, gda=gda:
                                e.activation(
                        s_dA[:, CK*gda:CK*(gda+1)], pdr[gp2][:, :], AF.Exp,
                        scale=s_acol[:, g:g+1]),
                        waits=[(pe, pe_dr), (dve, dA_free[gda]),
                               (dve, acol_ready)], inc=act)
                    ev["pdr_free"][gp2] = a_dA
                    d_w1 = P.op("vector", lambda e, gw=gw, gda=gda, gp3=gp3:
                                e.scalar_tensor_tensor(
                        s_w1[:, CK*gw:CK*(gw+1)], s_dA[:, CK*gda:CK*(gda+1)],
                        -1.0, pur[gp3][:, :], op0=ALU.add, op1=ALU.mult),
                        waits=[(act, a_dA), (pe, pe_ur),
                               (pool, ev["w1s_free"][gw])], inc=dve)
                    ev["pur_free"][gp3] = d_w1
                    p_w2 = P.op("gpsimd", lambda e, gw=gw, buf=buf:
                                e.tensor_tensor(
                        s_w2[:, CK*gw:CK*(gw+1)], s_w1[:, CK*gw:CK*(gw+1)],
                        s_brep[:, CK*buf:CK*(buf+1)], op=ALU.mult),
                        waits=[(dve, d_w1), (act, ev["a"][ckk]["a_b"]),
                               (dve, ev["w2_free"][gw])], inc=pool)
                    ev["w1s_free"][gw] = p_w2
                    if g >= 14:
                        ev["brep_free"][buf] = p_w2
                    if ckk in (0, NCK // 2):
                        init_ap = s_zero[:, 0:1]
                    else:
                        pc = hh_col(ckk - 1, g)
                        init_ap = s_hh[:, pc + CK - 1: pc + CK]
                    hc = hh_col(ckk, g)
                    d_sc = P.op("vector", lambda e, hc=hc, gda=gda, gw=gw,
                                init_ap=init_ap: e.tensor_tensor_scan(
                        s_hh[:, hc:hc + CK], s_dA[:, CK*gda:CK*(gda+1)],
                        s_w2[:, CK*gw:CK*gw+CK],
                        init_ap, op0=ALU.mult, op1=ALU.add),
                        waits=[(pool, p_w2)], inc=dve)
                    ev["w2_free"][gw] = d_sc
                    dA_free[gda] = d_sc
                    t1_eng = "vector" if g % 2 == 0 else "gpsimd"
                    t1_sem = dve if g % 2 == 0 else pool
                    if g % 2 == 0:
                        d_t1 = P.op(t1_eng, lambda e, hc=hc, gt=gt, g=g,
                                    buf=buf: e.scalar_tensor_tensor(
                            s_t1[:, CK*gt:CK*(gt+1)], s_hh[:, hc:hc + CK],
                            s_ainv[:, g:g+1], s_crep[:, CK*buf:CK*(buf+1)],
                            op0=ALU.mult, op1=ALU.mult),
                            waits=[(act, a_c), (pe, ev["t1_free"][gt]),
                                   (dve, d_sc)],
                            inc=t1_sem)
                    else:
                        tps = (g // 2) % 2
                        P.op(t1_eng, lambda e, hc=hc, g=g, tps=tps:
                             e.tensor_scalar(
                            s_tp[:, CK*tps:CK*(tps+1)], s_hh[:, hc:hc + CK],
                            s_ainv[:, g:g+1], None, op0=ALU.mult),
                            waits=[(dve, d_sc)], inc=t1_sem)
                        d_t1 = P.op(t1_eng, lambda e, gt=gt, buf=buf, tps=tps:
                                    e.tensor_tensor(
                            s_t1[:, CK*gt:CK*(gt+1)],
                            s_tp[:, CK*tps:CK*(tps+1)],
                            s_crep[:, CK*buf:CK*(buf+1)], op=ALU.mult),
                            waits=[(act, a_c), (pe, ev["t1_free"][gt])],
                            inc=t1_sem)
                    if g >= 14:
                        cf = list(ev["crep_free"][buf])
                        if g % 2 == 0:
                            cf[0] = d_t1
                        else:
                            cf[1] = d_t1
                        ev["crep_free"][buf] = tuple(cf)
                    pend_con.append((g, gt, ckk, d_t1, t1_sem))
            while pend_con:
                emit_con(P, pend_con.pop(0))
            cc_y_done = cc.c
            bar3 = barrier(P)
            with nc.Block("ph3") as blk:
                P.emit(blk)

        P = new_phase()
        phase_wait(P, bar3)
        p5ctx = ExitStack()
        with p5ctx:
            def sb(name, shape, dt=F32):
                return p5ctx.enter_context(nc.sbuf_tensor(name, list(shape), dt))

            s_yg = sb("s_yg", (128, 2 * 8 * CK), F32R)
            s_ysq = sb("s_ysq", (128, 8 * CK), F32R)
            s_h5 = sb("s_h5", (128, 2 * 8 * CK))
            s_hb5 = sb("s_hb5", (128, 8 * CK), BF16)
            s_r1 = sb("s_r1", (1, CK))
            s_mu = sb("s_mu", (1, CK))
            s_m2 = sb("s_m2", (1, CK))
            s_var = sb("s_var", (1, CK))
            s_rstd = sb("s_rstd", (1, CK), F32R)
            s_nmu = sb("s_nmu", (1, CK), F32R)
            s_sc5 = sb("s_sc5", (128, NE))
            s_sc5b = sb("s_sc5b", (128, NE))
            s_gsc = sb("s_gsc", (128, 8))
            s_gtmp = sb("s_gtmp", (128, 8))
            s_gm = sb("s_gm", (128, 1))
            s_msk = sb("s_msk", (128, NE))
            s_mskd = sb("s_mskd", (128, NE))
            s_thr = sb("s_thr", (128, 1))
            s_wsum = sb("s_wsum", (128, 1))
            s_comb = sb("s_comb", (128, NE))
            s_combT = sb("s_combT", (NE, CK), F32R)
            p_s1 = p5ctx.enter_context(nc.psum_tensor("p_s1", [1, CK], F32))
            p_s2 = p5ctx.enter_context(nc.psum_tensor("p_s2", [1, CK], F32))
            p_bc1 = p5ctx.enter_context(nc.psum_tensor("p_bc1", [128, CK], F32))
            p_bc2 = p5ctx.enter_context(nc.psum_tensor("p_bc2", [128, CK], F32))
            p_lg = p5ctx.enter_context(nc.psum_tensor("p_lg", [128, NE], F32))
            p_tr5 = p5ctx.enter_context(nc.psum_tensor("p_tr5", [NE, 128], F32))

            pe_gb = P.op("tensor", lambda e: e.matmul(
                p_bc1[:, 0:NE], s_onr32[:, :], s_gbr[:, :], start=True,
                stop=True), inc=pe)
            gbb_ready = P.op("scalar", lambda e: e.activation(
                s_gbb[:, :], p_bc1[:, 0:NE], AF.Copy),
                waits=[(pe, pe_gb)], inc=act)

            p5 = {"yg_free": [0, 0], "h5_free": [0, 0], "hb_free": 0,
                  "ps1_free": 0, "ps2_free": 0, "bc1_free": gbb_ready,
                  "bc2_free": 0, "lg_free": 0, "tr_free": 0, "combT_free": 0}
            for cko in range(CKPC):
                buf = cko % 2
                c_yg = None
                for r in range(8):
                    c_yg = P.op("gpsimd",
                                lambda e, r=r, cko=cko, buf=buf: e.indirect_dma_start(
                        out=s_yg[:, (buf*8 + r)*CK:(buf*8 + r + 1)*CK],
                        out_offset=None,
                        in_=cc_y_out.ap().rearrange(
                            "q p (k f) -> (q p k) f", f=CK),
                        in_offset=IndirectOffsetOnAxis(
                            s_ckidx[:, r*CKPC+cko:r*CKPC+cko+1], 0)),
                        waits=[(cc, cc_y_done)] +
                              ([(dve, p5["yg_free"][buf])] if r == 0 else []),
                        inc=lyg[buf], inc_n=16)
                a_sq = None
                for d in range(8):
                    a_sq = P.op("scalar", lambda e, d=d, buf=buf: e.activation(
                        s_ysq[:, d*CK:(d+1)*CK],
                        s_yg[:, (buf*8 + d)*CK:(buf*8 + d + 1)*CK].bitcast(F32),
                        AF.Square),
                        waits=[(lyg[buf], c_yg)] if d == 0 else (), inc=act)
                for d in range(8):
                    P.op("tensor", lambda e, d=d, buf=buf: e.matmul(
                        p_s1[:, :], s_onc[:, :],
                        s_yg[:, (buf*8 + d)*CK:(buf*8 + d + 1)*CK],
                        start=(d == 0), stop=(d == 7)),
                        waits=([(lyg[buf], c_yg), (dve, p5["ps1_free"])]
                               if d == 0 else ()),
                        inc=pe if d == 7 else None)
                pe_s1 = pe.c
                for d in range(8):
                    P.op("tensor", lambda e, d=d: e.matmul(
                        p_s2[:, :], s_onc[:, :], s_ysq[:, d*CK:(d+1)*CK],
                        start=(d == 0), stop=(d == 7)),
                        waits=([(act, a_sq), (dve, p5["ps2_free"])]
                               if d == 0 else ()),
                        inc=pe if d == 7 else None)
                pe_s2 = pe.c
                d_mu = P.op("vector", lambda e: e.tensor_scalar(
                    s_mu[:, :], p_s1[0:1, :], 1.0 / DM, None, op0=ALU.mult),
                    waits=[(pe, pe_s1)], inc=dve)
                p5["ps1_free"] = d_mu
                P.op("vector", lambda e: e.tensor_tensor(
                    s_m2[:, :], s_mu[:, :], s_mu[:, :], op=ALU.mult), inc=dve)
                d_var = P.op("vector", lambda e: e.scalar_tensor_tensor(
                    s_var[:, :], p_s2[0:1, :], 1.0 / DM, s_m2[:, :],
                    op0=ALU.mult, op1=ALU.subtract),
                    waits=[(pe, pe_s2)], inc=dve)
                p5["ps2_free"] = d_var
                P.op("scalar", lambda e: e.activation(
                    s_r1[:, :], s_var[:, :], AF.Ln, bias=1e-5),
                    waits=[(dve, d_var)], inc=act)
                a_rstd = P.op("scalar", lambda e: e.activation(
                    s_rstd[:, :], s_r1[:, :], AF.Exp, scale=-0.5), inc=act)
                d_nmu = P.op("vector", lambda e: e.scalar_tensor_tensor(
                    s_nmu[:, :], s_mu[:, :], -1.0, s_rstd[:, :].bitcast(F32),
                    op0=ALU.mult, op1=ALU.mult),
                    waits=[(act, a_rstd)], inc=dve)
                pe_b1 = P.op("tensor", lambda e: e.matmul(
                    p_bc1[:, :], s_onr[:, :], s_rstd[:, :], start=True, stop=True),
                    waits=[(act, a_rstd), (act, p5["bc1_free"] if cko == 0 else 0),
                           (dve, p5["bc1_free"] if cko > 0 else 0)], inc=pe)
                pe_b2 = P.op("tensor", lambda e: e.matmul(
                    p_bc2[:, :], s_onr[:, :], s_nmu[:, :], start=True, stop=True),
                    waits=[(dve, d_nmu), (dve, p5["bc2_free"])], inc=pe)
                d_h = None
                for d in range(8):
                    P.op("vector", lambda e, d=d, buf=buf: e.tensor_tensor(
                        s_h5[:, (buf*8 + d)*CK:(buf*8 + d + 1)*CK],
                        s_yg[:, (buf*8 + d)*CK:(buf*8 + d + 1)*CK].bitcast(F32),
                        p_bc1[:, :], op=ALU.mult),
                        waits=[(pe, pe_b1)] +
                              ([(ssp[buf], p5["h5_free"][buf])]
                               if d == 0 else []), inc=dve)
                    d_h = P.op("vector", lambda e, d=d, buf=buf: e.tensor_tensor(
                        s_h5[:, (buf*8 + d)*CK:(buf*8 + d + 1)*CK],
                        s_h5[:, (buf*8 + d)*CK:(buf*8 + d + 1)*CK],
                        p_bc2[:, :], op=ALU.add),
                        waits=[(pe, pe_b2)] if d == 0 else (), inc=dve)
                p5["yg_free"][buf] = d_h
                p5["bc1_free"] = d_h
                p5["bc2_free"] = d_h
                d_hb = None
                for d in range(8):
                    d_hb = P.op("vector", lambda e, d=d, buf=buf: e.tensor_copy(
                        s_hb5[:, d*CK:(d+1)*CK],
                        s_h5[:, (buf*8 + d)*CK:(buf*8 + d + 1)*CK]),
                        waits=[(shb, p5["hb_free"])] if d == 0 else (),
                        inc=dve)
                sp1 = P.op("scalar", lambda e, cko=cko, buf=buf: e.dma_start(
                    out=xres_spill.ap()[:, CK*cko:CK*(cko+1)].rearrange(
                        "(dt p) f -> p dt f", p=128),
                    in_=s_h5[:, buf*8*CK:(buf*8 + 8)*CK]),
                    waits=[(dve, d_h)], inc=ssp[buf], inc_n=16)
                P.op("scalar", lambda e, cko=cko: e.dma_start(
                    out=hb_spill.ap()[:, CK*cko:CK*(cko+1)].rearrange(
                        "(dt p) f -> p dt f", p=128),
                    in_=s_hb5[:, :]),
                    waits=[(dve, d_hb)], inc=shb, inc_n=16)
                p5["hb_free"] = shb.c
                p5["h5_free"][buf] = ssp[buf].c

                for tt in range(4):
                    for d in range(8):
                        P.op("tensor", lambda e, d=d, tt=tt, buf=buf: e.matmul(
                            p_lg[:, :],
                            s_h5[:, (buf*8 + d)*CK + tt*128:
                                 (buf*8 + d)*CK + (tt+1)*128],
                            s_gw[:, NE*d:NE*(d+1)], start=(d == 0), stop=(d == 7)),
                            waits=([(dve, d_h), (act, p5["lg_free"])]
                                   if d == 0 else ()),
                            inc=pe if d == 7 else None)
                    pe_lg = pe.c
                    a_e = P.op("scalar", lambda e: e.activation(
                        s_sc5[:, :], p_lg[:, :], AF.Exp, scale=-1.0),
                        waits=[(pe, pe_lg), (dve, 0)], inc=act)
                    p5["lg_free"] = a_e
                    P.op("vector", lambda e: e.tensor_scalar(
                        s_sc5[:, :], s_sc5[:, :], 1.0, None, op0=ALU.add),
                        waits=[(act, a_e)], inc=dve)
                    P.op("vector", lambda e: e.reciprocal(
                        s_sc5[:, :], s_sc5[:, :]), inc=dve)
                    P.op("vector", lambda e: e.tensor_tensor(
                        s_sc5b[:, :], s_sc5[:, :], s_gbb[:, :], op=ALU.add),
                        waits=[(act, gbb_ready)], inc=dve)
                    pairs = [(0, 1), (0, 2), (0, 3), (1, 2), (1, 3), (2, 3)]
                    for pi, (i1, i2) in enumerate(pairs):
                        dst = s_gsc if pi == 0 else s_gtmp
                        P.op("vector", lambda e, i1=i1, i2=i2, dst=dst:
                             e.tensor_tensor(
                            dst[:, :], s_sc5b[:, i1::4], s_sc5b[:, i2::4],
                            op=ALU.add), inc=dve)
                        if pi > 0:
                            P.op("vector", lambda e: e.tensor_tensor(
                                s_gsc[:, :], s_gsc[:, :], s_gtmp[:, :],
                                op=ALU.max), inc=dve)
                    P.op("vector", lambda e: e.tensor_copy(
                        s_gtmp[:, :], s_gsc[:, :]), inc=dve)
                    for it in range(4):
                        P.op("vector", lambda e: e.reduce_max(
                            s_gm[:, :], s_gtmp[:, :], axis=AX.X), inc=dve)
                        if it < 3:
                            P.op("vector", lambda e: e.tensor_scalar(
                                s_msk[:, 0:8], s_gtmp[:, :], s_gm[:, :1], None,
                                op0=ALU.is_ge), inc=dve)
                            P.op("vector", lambda e: e.scalar_tensor_tensor(
                                s_gtmp[:, :], s_msk[:, 0:8], -1e30, s_gtmp[:, :],
                                op0=ALU.mult, op1=ALU.add), inc=dve)
                    P.op("vector", lambda e: e.tensor_scalar(
                        s_gtmp[:, :], s_gsc[:, :], s_gm[:, :1], None,
                        op0=ALU.is_ge), inc=dve)
                    for j in range(4):
                        P.op("vector", lambda e, j=j: e.tensor_copy(
                            s_msk[:, j::4], s_gtmp[:, :]), inc=dve)
                    P.op("vector", lambda e: e.tensor_tensor(
                        s_mskd[:, :], s_sc5b[:, :], s_msk[:, :], op=ALU.mult),
                        inc=dve)
                    P.op("vector", lambda e: e.tensor_copy(
                        s_comb[:, :], s_mskd[:, :]), inc=dve)
                    for it in range(8):
                        P.op("vector", lambda e: e.reduce_max(
                            s_thr[:, :], s_comb[:, :], axis=AX.X), inc=dve)
                        if it < 7:
                            P.op("vector", lambda e: e.tensor_scalar(
                                s_msk[:, :], s_comb[:, :], s_thr[:, :1], None,
                                op0=ALU.is_ge), inc=dve)
                            P.op("vector", lambda e: e.scalar_tensor_tensor(
                                s_comb[:, :], s_msk[:, :], -1e30, s_comb[:, :],
                                op0=ALU.mult, op1=ALU.add), inc=dve)
                    P.op("vector", lambda e: e.tensor_scalar(
                        s_msk[:, :], s_mskd[:, :], s_thr[:, :1], None,
                        op0=ALU.is_ge), inc=dve)
                    P.op("vector", lambda e: e.tensor_tensor(
                        s_comb[:, :], s_msk[:, :], s_sc5[:, :], op=ALU.mult),
                        inc=dve)
                    P.op("vector", lambda e: e.reduce_sum(
                        s_wsum[:, :], s_comb[:, :], axis=AX.X), inc=dve)
                    P.op("vector", lambda e: e.tensor_scalar(
                        s_wsum[:, :], s_wsum[:, :], 1e-20, None, op0=ALU.add),
                        inc=dve)
                    P.op("vector", lambda e: e.reciprocal(
                        s_wsum[:, :], s_wsum[:, :]), inc=dve)
                    P.op("vector", lambda e: e.tensor_scalar(
                        s_wsum[:, :], s_wsum[:, :], 2.5, None, op0=ALU.mult),
                        inc=dve)
                    d_cb = P.op("vector", lambda e: e.tensor_scalar(
                        s_comb[:, :], s_comb[:, :], s_wsum[:, :1], None,
                        op0=ALU.mult), inc=dve)
                    pe_tr = P.op("tensor", lambda e: e.transpose(
                        p_tr5[:, :], s_comb[:, :], s_id[:, :]),
                        waits=[(dve, d_cb), (act, p5["tr_free"])], inc=pe)
                    p5["tr_free"] = P.op("scalar", lambda e, tt=tt: e.activation(
                        s_combT[:, tt*128:(tt+1)*128], p_tr5[:, :], AF.Copy),
                        waits=[(pe, pe_tr), (scf, p5["combT_free"])], inc=act)
                a_combT = act.c
                P.op("scalar", lambda e, cko=cko: e.dma_start(
                    out=cflat_spill.ap()[:, CK*cko:CK*(cko+1)],
                    in_=s_combT[:, :]),
                    waits=[(act, a_combT)], inc=scf, inc_n=16)
                p5["combT_free"] = scf.c
            cflat_ready = scf.c
            hb_all = shb.c
            xres_all = [ssp[0].c, ssp[1].c]
            bar5 = barrier(P)
            with nc.Block("ph5") as blk:
                P.emit(blk)

        P = new_phase()
        phase_wait(P, bar5)
        p6ctx = ExitStack()
        with p6ctx:
            def sb(name, shape, dt=F32):
                return p6ctx.enter_context(nc.sbuf_tensor(name, list(shape), dt))

            s_wg = sb("s_wg", (128, NSLOT * 8 * EI), BF16)
            s_wu = sb("s_wu", (128, NSLOT * 8 * EI), BF16)
            s_wd = sb("s_wd", (128, NSLOT * 4 * DM), BF16)
            s_hb6 = sb("s_hb6", (128, 8 * TB), BF16)
            s_routed = sb("s_routed", (128, 8 * TB))
            s_cbc = sb("s_cbc", (128, EG * TB))
            s_crow = sb("s_crow", (1, EG * TB), F32R)
            s_sg6 = sb("s_sg6", (128, 4 * SUB))
            s_x6 = sb("s_x6", (128, 2 * 4 * SUB), BF16)
            s_xr6 = sb("s_xr6", (128, 2 * TB))
            pg = [p6ctx.enter_context(nc.psum_tensor(f"pg{i}", [128, 512], F32))
                  for i in range(2)]
            pu = [p6ctx.enter_context(nc.psum_tensor(f"pu{i}", [128, 512], F32))
                  for i in range(2)]
            pr6 = [p6ctx.enter_context(nc.psum_tensor(f"pr{i}", [128, 512], F32))
                   for i in range(4)]

            def wslot(base, slot, n):
                return base[:, slot * n:(slot + 1) * n]

            p6 = {"crow_free": [0, 0],
                  "slot_free": [0] * NSLOT, "slot_ready": [0] * NSLOT,
                  "cbc_free": 0, "sg_free": 0, "x_free": [0, 0],
                  "routed_free": 0, "pr_free": 0, "pg_free": [0, 0],
                  "pu_free": [0, 0], "hb_free": 0, "xr_free": [0, 0]}

            for tb in range(2):
                c_hb = P.op("sync", lambda e, tb=tb: e.dma_start(
                    out=s_hb6[:, :],
                    in_=hb_spill.ap()[:, TB*tb:TB*(tb+1)].rearrange(
                        "(kd p) t -> p kd t", p=128)),
                    waits=[(shb, hb_all), (pe, p6["hb_free"])],
                    inc=lhb, inc_n=16)
                d_ms = None
                for ht in range(8):
                    d_ms = P.op("vector", lambda e, ht=ht: e.memset(
                        s_routed[:, ht*TB:(ht+1)*TB], 0.0),
                        waits=[(sp2, p6["routed_free"])] if ht == 0 else (),
                        inc=dve)

                for eg_i in range(NEX // EG):
                    egs = [eg_i * EG + j for j in range(EG)]
                    slots = [ei % NSLOT for ei in egs]
                    for ei, sl in zip(egs, slots):
                        P.op("gpsimd", lambda e, ei=ei, sl=sl: e.dma_start(
                            out=wslot(s_wg, sl, 8 * EI),
                            in_=wgT.ap()[ei].rearrange("(k p) i -> p k i",
                                                       p=128)),
                            waits=[(pe, p6["slot_free"][sl])],
                            inc=sw[sl], inc_n=16)
                        P.op("gpsimd", lambda e, ei=ei, sl=sl: e.dma_start(
                            out=wslot(s_wu, sl, 8 * EI),
                            in_=wuT.ap()[ei].rearrange("(k p) i -> p k i",
                                                       p=128)),
                            inc=sw[sl], inc_n=16)
                        P.op("gpsimd", lambda e, ei=ei, sl=sl: e.dma_start(
                            out=wslot(s_wd, sl, 4 * DM),
                            in_=wdT.ap()[ei].rearrange("(k p) h -> p k h",
                                                       p=128)),
                            inc=sw[sl], inc_n=16)
                        p6["slot_ready"][sl] = sw[sl].c
                    cbc_dve = 0
                    cbc_act = 0
                    for j, ei in enumerate(egs):
                        if ei < NE:
                            c_cr = P.op("scalar", lambda e, ei=ei, tb=tb, j=j:
                                        e.dma_start(
                                out=s_crow[:, j*TB:(j+1)*TB],
                                in_=cflat_spill.ap()[ei:ei+1, TB*tb:TB*(tb+1)]),
                                waits=[(scf, cflat_ready),
                                       (pe, p6["crow_free"][j])],
                                inc=scr[j], inc_n=16)
                            pe_o1 = P.op("tensor", lambda e, j=j: e.matmul(
                                pg[0][:, :], s_onr[:, :],
                                s_crow[:, j*TB:j*TB + 512],
                                start=True, stop=True),
                                waits=[(scr[j], c_cr),
                                       (act, p6["pg_free"][0]),
                                       (dve, p6["cbc_free"])], inc=pe)
                            pe_o2 = P.op("tensor", lambda e, j=j: e.matmul(
                                pg[1][:, :], s_onr[:, :],
                                s_crow[:, j*TB + 512:(j+1)*TB],
                                start=True, stop=True),
                                waits=[(act, p6["pg_free"][1])], inc=pe)
                            p6["crow_free"][j] = pe_o2
                            a1 = P.op("scalar", lambda e, j=j: e.activation(
                                s_cbc[:, j*TB:j*TB + 512], pg[0][:, :], AF.Copy),
                                waits=[(pe, pe_o1)], inc=act)
                            a2 = P.op("scalar", lambda e, j=j: e.activation(
                                s_cbc[:, j*TB + 512:(j+1)*TB], pg[1][:, :],
                                AF.Copy),
                                waits=[(pe, pe_o2)], inc=act)
                            p6["pg_free"][0] = a1
                            p6["pg_free"][1] = a2
                            cbc_act = a2
                        else:
                            cbc_dve = P.op("vector", lambda e, j=j: e.memset(
                                s_cbc[:, j*TB:(j+1)*TB], 1.0),
                                waits=[(dve, p6["cbc_free"])], inc=dve)

                    pend_dn = []

                    def emit_down(Pp, item):
                        (j_, sub_, sl_, xslot_, d_x_) = item
                        w_extra = []
                        if j_ == 0:
                            w_extra.append((dve, p6["pr_free"]))
                            if eg_i == 0 and sub_ == 0:
                                w_extra.append((dve, d_ms))
                        for ht in range(8):
                            for ki in range(4):
                                w = []
                                if ht == 0 and ki == 0:
                                    w = [(dve, d_x_)] + w_extra
                                Pp.op("tensor",
                                      lambda e, ht=ht, ki=ki, sl=sl_, j=j_,
                                      xs=xslot_: e.matmul(
                                    pr6[ht // 2][:, (ht % 2)*SUB:
                                                 (ht % 2 + 1)*SUB],
                                    wslot(s_wd, sl, 4*DM)[
                                        :, ki*DM + ht*128:
                                        ki*DM + (ht+1)*128],
                                    s_x6[:, xs*TB + ki*SUB:
                                         xs*TB + (ki+1)*SUB],
                                    start=(ki == 0 and j == 0
                                           and ht % 2 == 0),
                                    stop=(ki == 3 and j == EG - 1
                                          and ht % 2 == 1)),
                                    waits=w,
                                    inc=pe if (ki == 3 and ht == 7)
                                    else None)
                        dn_done = pe.c
                        p6["x_free"][xslot_] = dn_done
                        if j_ == EG - 1:
                            for sl2 in set(slots):
                                p6["slot_free"][sl2] = dn_done
                            for q in range(4):
                                P.op("vector", lambda e, q=q, sub=sub_:
                                     e.tensor_tensor(
                                    s_routed[:, (2*q)*TB + sub*SUB:
                                             (2*q)*TB + sub*SUB + SUB],
                                    pr6[q][:, 0:SUB],
                                    s_routed[:, (2*q)*TB + sub*SUB:
                                             (2*q)*TB + sub*SUB + SUB],
                                    op=ALU.add),
                                    waits=[(pe, dn_done)], inc=dve)
                                P.op("vector", lambda e, q=q, sub=sub_:
                                     e.tensor_tensor(
                                    s_routed[:, (2*q+1)*TB + sub*SUB:
                                             (2*q+1)*TB + sub*SUB + SUB],
                                    pr6[q][:, SUB:2*SUB],
                                    s_routed[:, (2*q+1)*TB + sub*SUB:
                                             (2*q+1)*TB + sub*SUB + SUB],
                                    op=ALU.add), inc=dve)
                            p6["pr_free"] = dve.c

                    for sub in range(NSUB):
                        for j, ei in enumerate(egs):
                            sl = slots[j]
                            xslot = (sub * EG + j) % 2
                            for it in range(4):
                                for kd in range(8):
                                    w = []
                                    if it == 0 and kd == 0:
                                        w = [(sw[sl], p6["slot_ready"][sl]),
                                             (lhb, c_hb),
                                             (act, p6["pg_free"][0]),
                                             (act, p6["pg_free"][1])]
                                        if cbc_act:
                                            w.append((act, cbc_act))
                                        if cbc_dve:
                                            w.append((dve, cbc_dve))
                                    P.op("tensor",
                                         lambda e, it=it, kd=kd, sl=sl, sub=sub:
                                         e.matmul(
                                        pg[it // 2][:, (it % 2)*SUB:
                                                    (it % 2 + 1)*SUB],
                                        wslot(s_wg, sl, 8*EI)[
                                            :, kd*EI + it*128: kd*EI + (it+1)*128],
                                        s_hb6[:, kd*TB + sub*SUB:
                                              kd*TB + (sub+1)*SUB],
                                        start=(kd == 0 and it % 2 == 0),
                                        stop=(kd == 7 and it % 2 == 1)),
                                        waits=w,
                                        inc=pe if (kd == 7 and it % 2 == 1)
                                        else None)
                            pe_g = pe.c
                            for it in range(4):
                                for kd in range(8):
                                    w = []
                                    if it == 0 and kd == 0:
                                        w = [(dve, p6["pu_free"][0]),
                                             (dve, p6["pu_free"][1])]
                                    P.op("tensor",
                                         lambda e, it=it, kd=kd, sl=sl, sub=sub:
                                         e.matmul(
                                        pu[it // 2][:, (it % 2)*SUB:
                                                    (it % 2 + 1)*SUB],
                                        wslot(s_wu, sl, 8*EI)[
                                            :, kd*EI + it*128: kd*EI + (it+1)*128],
                                        s_hb6[:, kd*TB + sub*SUB:
                                              kd*TB + (sub+1)*SUB],
                                        start=(kd == 0 and it % 2 == 0),
                                        stop=(kd == 7 and it % 2 == 1)),
                                        waits=w,
                                        inc=pe if (kd == 7 and it % 2 == 1)
                                        else None)
                            pe_u = pe.c
                            if pend_dn:
                                emit_down(P, pend_dn.pop(0))
                            a_sg1 = P.op("scalar", lambda e: e.activation(
                                s_sg6[:, 0:512], pg[0][:, :], AF.Silu),
                                waits=[(pe, pe_g), (dve, p6["sg_free"])],
                                inc=act)
                            a_sg2 = P.op("scalar", lambda e: e.activation(
                                s_sg6[:, 512:1024], pg[1][:, :], AF.Silu),
                                inc=act)
                            p6["pg_free"][0] = a_sg1
                            p6["pg_free"][1] = a_sg2
                            d_t1 = P.op("vector", lambda e: e.tensor_tensor(
                                s_sg6[:, 0:512], pu[0][:, :], s_sg6[:, 0:512],
                                op=ALU.mult),
                                waits=[(pe, pe_u), (act, a_sg1)], inc=dve)
                            d_t2 = P.op("vector", lambda e: e.tensor_tensor(
                                s_sg6[:, 512:1024], pu[1][:, :],
                                s_sg6[:, 512:1024], op=ALU.mult),
                                waits=[(act, a_sg2)], inc=dve)
                            p6["pu_free"][0] = d_t1
                            p6["pu_free"][1] = d_t2
                            d_x = None
                            for it in range(4):
                                d_x = P.op("vector",
                                           lambda e, it=it, j=j, sub=sub,
                                           xs=xslot: e.tensor_tensor(
                                    s_x6[:, xs*TB + it*SUB:
                                         xs*TB + (it+1)*SUB],
                                    s_sg6[:, it*SUB:(it+1)*SUB],
                                    s_cbc[:, j*TB + sub*SUB:
                                          j*TB + sub*SUB + SUB],
                                    op=ALU.mult),
                                    waits=([(dve, d_t2),
                                            (pe, p6["x_free"][xslot])]
                                           if it == 0 else ()), inc=dve)
                            p6["sg_free"] = d_x
                            pend_dn.append((j, sub, sl, xslot, d_x))
                    while pend_dn:
                        emit_down(P, pend_dn.pop(0))
                    p6["cbc_free"] = dve.c

                for ht in range(8):
                    hbuf = ht % 2
                    c_xr = P.op("sync", lambda e, ht=ht, tb=tb, hbuf=hbuf:
                                e.dma_start(
                        out=s_xr6[:, hbuf*TB:(hbuf+1)*TB],
                        in_=xres_spill.ap()[128*ht:128*(ht+1), TB*tb:TB*(tb+1)]),
                        waits=[(dve, p6["xr_free"][hbuf]),
                               (ssp[0], xres_all[0]), (ssp[1], xres_all[1])],
                        inc=lxr[hbuf], inc_n=16)
                    d_p2 = P.op("vector", lambda e, ht=ht, hbuf=hbuf:
                                e.tensor_tensor(
                        s_routed[:, ht*TB:(ht+1)*TB],
                        s_routed[:, ht*TB:(ht+1)*TB],
                        s_xr6[:, hbuf*TB:(hbuf+1)*TB], op=ALU.add),
                        waits=[(lxr[hbuf], c_xr)], inc=dve)
                    p6["xr_free"][hbuf] = d_p2
                    P.op("scalar", lambda e, ht=ht, tb=tb: e.dma_start(
                        out=pre2_spill.ap()[128*ht:128*(ht+1), TB*tb:TB*(tb+1)],
                        in_=s_routed[:, ht*TB:(ht+1)*TB]),
                        waits=[(dve, d_p2)], inc=sp2, inc_n=16)
                p6["routed_free"] = sp2.c
                p6["hb_free"] = pe.c

            pre2_done = sp2.c
            bar6 = barrier(P)
            with nc.Block("ph6") as blk:
                P.emit(blk)

        P = new_phase()
        phase_wait(P, bar6)
        p7ctx = ExitStack()
        with p7ctx:
            def sb(name, shape, dt=F32):
                return p7ctx.enter_context(nc.sbuf_tensor(name, list(shape), dt))

            s_p7 = sb("s_p7", (128, 2 * 8 * CK))
            s_q7 = sb("s_q7", (128, 8 * CK))
            s_out7 = sb("s_out7", (128, 2 * DM))
            s_rs7 = sb("s_rs7", (1, CK))
            s_nm7 = sb("s_nm7", (1, CK))
            s_sclr = sb("s_sclr", (128, 1))
            s_sclm = sb("s_sclm", (128, 1))
            s_r17 = sb("s_r17", (1, CK))
            s_mu7 = sb("s_mu7", (1, CK))
            s_m27 = sb("s_m27", (1, CK))
            s_v7 = sb("s_v7", (1, CK))
            p_t7 = [p7ctx.enter_context(nc.psum_tensor(f"p_t7{i}",
                                                       [128, 128], F32))
                    for i in range(2)]
            p_s17 = p7ctx.enter_context(nc.psum_tensor("p_s17", [1, CK], F32))
            p_s27 = p7ctx.enter_context(nc.psum_tensor("p_s27", [1, CK], F32))
            p_sc7 = p7ctx.enter_context(nc.psum_tensor("p_sc7", [128, 2], F32))

            p7 = {"p7_free": [0, 0], "out_free": [0, 0], "q_free": 0,
                  "tr_free": [0, 0], "scl_free": 0, "ps1_free": 0,
                  "ps2_free": 0}
            for cko in range(CKPC):
                buf = cko % 2
                c_p2 = P.op("sync", lambda e, cko=cko, buf=buf: e.dma_start(
                    out=s_p7[:, buf*8*CK:(buf + 1)*8*CK],
                    in_=pre2_spill.ap()[:, CK*cko:CK*(cko+1)].rearrange(
                        "(dt p) t -> p dt t", p=128)),
                    waits=[(sp2, pre2_done), (pe, p7["p7_free"][buf])],
                    inc=lp2[buf], inc_n=16)
                a_sq = None
                for d in range(8):
                    a_sq = P.op("scalar", lambda e, d=d, buf=buf: e.activation(
                        s_q7[:, d*CK:(d+1)*CK],
                        s_p7[:, (buf*8 + d)*CK:(buf*8 + d+1)*CK], AF.Square),
                        waits=([(lp2[buf], c_p2), (pe, p7["q_free"])]
                               if d == 0 else ()), inc=act)
                for d in range(8):
                    P.op("tensor", lambda e, d=d, buf=buf: e.matmul(
                        p_s17[:, :], s_onc32[:, :],
                        s_p7[:, (buf*8 + d)*CK:(buf*8 + d+1)*CK],
                        start=(d == 0), stop=(d == 7)),
                        waits=([(lp2[buf], c_p2), (dve, p7["ps1_free"])]
                               if d == 0 else ()),
                        inc=pe if d == 7 else None)
                pe_s1 = pe.c
                for d in range(8):
                    P.op("tensor", lambda e, d=d: e.matmul(
                        p_s27[:, :], s_onc32[:, :], s_q7[:, d*CK:(d+1)*CK],
                        start=(d == 0), stop=(d == 7)),
                        waits=([(act, a_sq), (dve, p7["ps2_free"])]
                               if d == 0 else ()),
                        inc=pe if d == 7 else None)
                pe_s2 = pe.c
                p7["q_free"] = pe_s2
                d_mu = P.op("vector", lambda e: e.tensor_scalar(
                    s_mu7[:, :], p_s17[0:1, :], 1.0 / DM, None, op0=ALU.mult),
                    waits=[(pe, pe_s1)], inc=dve)
                p7["ps1_free"] = d_mu
                P.op("vector", lambda e: e.tensor_tensor(
                    s_m27[:, :], s_mu7[:, :], s_mu7[:, :], op=ALU.mult), inc=dve)
                d_v = P.op("vector", lambda e: e.scalar_tensor_tensor(
                    s_v7[:, :], p_s27[0:1, :], 1.0 / DM, s_m27[:, :],
                    op0=ALU.mult, op1=ALU.subtract),
                    waits=[(pe, pe_s2)], inc=dve)
                p7["ps2_free"] = d_v
                P.op("scalar", lambda e: e.activation(
                    s_r17[:, :], s_v7[:, :], AF.Ln, bias=1e-5),
                    waits=[(dve, d_v)], inc=act)
                a_rs = P.op("scalar", lambda e: e.activation(
                    s_rs7[:, :], s_r17[:, :], AF.Exp, scale=-0.5), inc=act)
                d_nm = P.op("vector", lambda e: e.scalar_tensor_tensor(
                    s_nm7[:, :], s_mu7[:, :], -1.0, s_rs7[:, :],
                    op0=ALU.mult, op1=ALU.mult),
                    waits=[(act, a_rs)], inc=dve)
                for tt in range(4):
                    obuf = tt % 2
                    pe_sc1 = P.op("tensor", lambda e, tt=tt: e.transpose(
                        p_sc7[:, 0:1], s_rs7[:, tt*128:(tt+1)*128], s_id[0:1, 0:1]),
                        waits=[(act, a_rs), (dve, d_nm),
                               (dve, p7["scl_free"])], inc=pe)
                    pe_sc2 = P.op("tensor", lambda e, tt=tt: e.transpose(
                        p_sc7[:, 1:2], s_nm7[:, tt*128:(tt+1)*128], s_id[0:1, 0:1]),
                        inc=pe)
                    a_sc1 = P.op("scalar", lambda e: e.activation(
                        s_sclr[:, :], p_sc7[:, 0:1], AF.Copy),
                        waits=[(pe, pe_sc2)], inc=act)
                    a_scl = P.op("scalar", lambda e: e.activation(
                        s_sclm[:, :], p_sc7[:, 1:2], AF.Copy), inc=act)
                    p7["scl_free"] = a_scl
                    for d in range(8):
                        dbuf = d % 2
                        pe_tr = P.op("tensor",
                                     lambda e, d=d, tt=tt, buf=buf, dbuf=dbuf:
                                     e.transpose(
                            p_t7[dbuf][:, :],
                            s_p7[:, (buf*8 + d)*CK + tt*128:
                                 (buf*8 + d)*CK + (tt+1)*128],
                            s_id[:, :]),
                            waits=[(dve, p7["tr_free"][dbuf])], inc=pe)
                        a_o = P.op("vector",
                                   lambda e, d=d, dbuf=dbuf, obuf=obuf:
                                   e.tensor_scalar(
                            s_out7[:, obuf*DM + d*128: obuf*DM + (d+1)*128],
                            p_t7[dbuf][:, :], s_sclr[:, 0:1], s_sclm[:, 0:1],
                            op0=ALU.mult, op1=ALU.add),
                            waits=[(pe, pe_tr), (act, a_scl)] +
                                  ([(so[obuf], p7["out_free"][obuf])]
                                   if d == 0 else []),
                            inc=dve)
                        p7["tr_free"][dbuf] = a_o
                    p7["scl_free"] = dve.c
                    if tt == 3:
                        p7["p7_free"][buf] = pe.c
                    a_last = dve.c
                    P.op("sync", lambda e, cko=cko, tt=tt, obuf=obuf:
                         e.dma_start(
                        out=out_h.ap()[cko*CK + tt*128: cko*CK + (tt+1)*128, :],
                        in_=s_out7[:, obuf*DM:(obuf+1)*DM]),
                        waits=[(dve, a_last)], inc=so[obuf], inc_n=16)
                    p7["out_free"][obuf] = so[obuf].c

            P.op("sync", lambda e: None,
                 waits=[(so[0], so[0].c), (so[1], so[1].c)])
            with nc.Block("ph7") as blk:
                P.emit(blk)

    return nc


def kernel(**inputs):
    f = lambda k: np.asarray(inputs[k], np.float32)
    u = np.ascontiguousarray(f("L0_attn_u").reshape(T, DM))
    conv_w = f("L0_attn_conv_w")[:, 0, :]
    conv_b = f("L0_attn_conv_b").reshape(DM, 1)
    x_proj_w = f("L0_attn_x_proj_w")
    dt_proj_w = f("L0_attn_dt_proj_w")
    dt_proj_b = f("L0_attn_dt_proj_b")
    A_log = f("L0_attn_A_log")
    D_skip = f("L0_attn_D_skip")
    gate_w = f("L0_ffn_gate_w")
    gate_bias = f("L0_ffn_gate_bias")
    w_gate = f("L0_ffn_w_gate")
    w_up = f("L0_ffn_w_up")
    w_down = f("L0_ffn_w_down")
    sh_gate = f("L0_ffn_sh_gate")
    sh_up = f("L0_ffn_sh_up")
    sh_down = f("L0_ffn_sh_down")

    wgT = np.empty((NEX, DM, EI), np.float32)
    wuT = np.empty((NEX, DM, EI), np.float32)
    wdT = np.empty((NEX, EI, DM), np.float32)
    wgT[:NE] = w_gate.transpose(0, 2, 1)
    wuT[:NE] = w_up.transpose(0, 2, 1)
    wdT[:NE] = w_down.transpose(0, 2, 1)
    shgT = sh_gate.T
    shuT = sh_up.T
    shdT = sh_down.T
    wgT[NE] = shgT[:, :EI]; wgT[NE+1] = shgT[:, EI:]
    wuT[NE] = shuT[:, :EI]; wuT[NE+1] = shuT[:, EI:]
    wdT[NE] = shdT[:EI, :]; wdT[NE+1] = shdT[EI:, :]
    wgT = np.ascontiguousarray(wgT)
    wuT = np.ascontiguousarray(wuT)
    wdT = np.ascontiguousarray(wdT)

    selrep = np.zeros((16, 128, 128), np.float32)
    selcon = np.zeros((16, 128, 128), np.float32)
    for g in range(16):
        for n in range(16):
            for j in range(8):
                selrep[g, 8 * g + j, n * 8 + j] = 1.0
                selcon[g, n * 8 + j, 8 * g + j] = 1.0
    selbc = np.zeros((NST, 128), np.float32)
    for n in range(16):
        selbc[n, n * 8:(n + 1) * 8] = 1.0

    shared = {
        "xpwT": np.ascontiguousarray(x_proj_w.T),
        "cw": np.ascontiguousarray(conv_w),
        "cb": conv_b,
        "gatewT": np.ascontiguousarray(gate_w.T),
        "gateb": gate_bias.reshape(1, NE),
        "selrep": selrep, "selcon": selcon, "selbc": selbc,
        "ones_row": np.ones((1, 128), np.float32),
        "ones_col": np.ones((128, 1), np.float32),
        "ident": np.eye(128, dtype=np.float32),
        "wgT": wgT, "wuT": wuT, "wdT": wdT,
    }

    in_maps = []
    for c in range(NCORE):
        t0 = TPC * c
        halo = (np.zeros((3, DM), np.float32) if (t0 % LQ) == 0
                else u[t0-3:t0])
        u_tokT_a = np.ascontiguousarray(
            np.concatenate([halo, u[t0:t0+TPC]], axis=0).T)
        ch0 = CPC * c
        u_chanT_a = np.ascontiguousarray(u[:, ch0:ch0+CPC].T)
        pvec = np.arange(128)
        gvec = np.arange(16)
        alog_a = A_log[ch0 + 8 * gvec[None, :] + (pvec[:, None] % 8),
                       pvec[:, None] // 8].astype(np.float32)
        rr = np.arange(8)
        jj = np.arange(CKPC)
        kk = CKPC * c + jj
        qq = kk // 8
        kin = kk % 8
        idx = ((qq[None, None, :] * 1024 + 128 * rr[None, :, None]
                + pvec[:, None, None]) * 8 + kin[None, None, :]
               ).reshape(128, 8 * CKPC).astype(np.int32)
        m = dict(shared)
        m.update({
            "u_tokT": u_tokT_a,
            "u_chanT": u_chanT_a,
            "dtwT": np.ascontiguousarray(dt_proj_w[ch0:ch0+CPC].T),
            "dtb": dt_proj_b[ch0:ch0+CPC].reshape(CPC, 1),
            "alog": np.ascontiguousarray(alog_a),
            "dcol": D_skip[ch0:ch0+CPC].reshape(CPC, 1),
            "ckidx": np.ascontiguousarray(idx),
        })
        in_maps.append(m)

    nc = bass.Bass("TRN2", target_bir_lowering=False, debug=False)
    nc = _build(nc)
    res = run_bass_kernel_spmd(nc, in_maps, core_ids=list(range(NCORE)))
    out = np.concatenate([np.asarray(res.results[c]["out"], np.float32)
                          for c in range(NCORE)], axis=0)
    return out.reshape(B, LQ, DM)


# version 09
# speedup vs baseline: 1.2309x; 1.2309x over previous
import numpy as np
from contextlib import ExitStack

import concourse.bass as bass
import concourse.mybir as mybir
from concourse.bass_utils import run_bass_kernel_spmd
from concourse.bass_core_rust import IndirectOffsetOnAxis

F32 = mybir.dt.float32
F32R = mybir.dt.float32r
BF16 = mybir.dt.bfloat16
I32 = mybir.dt.int32
AF = mybir.ActivationFunctionType
ALU = mybir.AluOpType
AX = mybir.AxisListType

NCORE = 8
DM = 1024
NST = 16
DTR = 64
B = 2
LQ = 8192
T = B * LQ
TPC = T // NCORE
CPC = DM // NCORE
CK = 512
NCK = T // CK
CKPC = NCK // NCORE
NE = 32
EI = 512
NEX = NE + 2
TB = 1024
SUB = 256
NSUB = TB // SUB
EG = 2
NSLOT = 4

ENGINES = ("sync", "tensor", "vector", "scalar", "gpsimd")


class Sem:
    def __init__(self, ctx, nc, name):
        self.h = ctx.enter_context(nc.semaphore(name))
        self.c = 0


class Plan:
    def __init__(self, self_sems=None):
        self.ops = {e: [] for e in ENGINES}
        self.self_sems = self_sems or {}

    def op(self, eng, emit, waits=(), inc=None, inc_n=1):
        w = [(s.h, v) for (s, v) in waits if v > 0]
        ss = self.self_sems.get(eng)
        if ss is not None and ss.c > 0:
            w.append((ss.h, ss.c))
        if inc is not None:
            inc.c += inc_n
            self.ops[eng].append((w, emit, inc.h, inc_n))
            return inc.c
        self.ops[eng].append((w, emit, None, 0))
        return None

    def emit(self, block):
        for eng_name in ENGINES:
            lst = self.ops[eng_name]
            if not lst:
                continue

            def body(eng, lst=lst):
                for waits, emit, sem_h, n in lst:
                    for (h, v) in waits:
                        eng.wait_ge(h, v)
                    ins = emit(eng)
                    if sem_h is not None:
                        if ins is None:
                            eng.sem_inc(sem_h, n)
                        else:
                            ins.then_inc(sem_h, n)
            getattr(block, eng_name)(body)


def _build(nc):
    dram = {}

    def din(name, shape, dt=F32):
        dram[name] = nc.dram_tensor(name, list(shape), dt, kind="ExternalInput")
        return dram[name]

    u_tokT = din("u_tokT", (DM, TPC + 3), F32R)
    u_chanT = din("u_chanT", (CPC, T), F32R)
    dtwT = din("dtwT", (DTR, CPC), F32R)
    dtb = din("dtb", (CPC, 1), F32)
    alog = din("alog", (128, 16), F32)
    dcol = din("dcol", (CPC, 1), F32)
    xpwT = din("xpwT", (DM, 96), F32)
    cw = din("cw", (DM, 4), F32)
    cb = din("cb", (DM, 1), F32)
    gatewT = din("gatewT", (DM, NE), F32)
    gateb = din("gateb", (1, NE), F32)
    selrep = din("selrep", (16, 128, 128), F32R)
    selcon = din("selcon", (16, 128, 128), F32R)
    selbc = din("selbc", (NST, 128), F32R)
    ones_row = din("ones_row", (1, 128), F32R)
    ones_col = din("ones_col", (128, 1), F32R)
    ident = din("ident", (128, 128), F32)
    ckidx = din("ckidx", (128, 8 * CKPC), I32)
    wgT = din("wgT", (NEX, DM, EI), F32)
    wuT = din("wuT", (NEX, DM, EI), F32)
    wdT = din("wdT", (NEX, EI, DM), F32)
    out_h = nc.dram_tensor("out", [TPC, DM], F32, kind="ExternalOutput")

    cc_prep_in = nc.dram_tensor("cc_prep_in", [96, TPC], F32R, kind="Internal")
    cc_prep_out = nc.dram_tensor("cc_prep_out", [96 * NCORE, TPC], F32R,
                                 kind="Internal")
    cc_y_in = nc.dram_tensor("cc_y_in", [4, CPC, 4096], F32R, kind="Internal")
    cc_y_out = nc.dram_tensor("cc_y_out", [4, DM, 4096], F32R, kind="Internal")
    xres_spill = nc.dram_tensor("xres_spill", [DM, TPC], F32, kind="Internal")
    hb_spill = nc.dram_tensor("hb_spill", [DM, TPC], BF16, kind="Internal")
    pre2_spill = nc.dram_tensor("pre2_spill", [DM, TPC], F32, kind="Internal")
    cflat_spill = nc.dram_tensor("cflat_spill", [NE, TPC], F32R, kind="Internal")

    octx = ExitStack()
    with octx:
        def osb(name, shape, dt=F32):
            return octx.enter_context(nc.sbuf_tensor(name, list(shape), dt))

        def sem(name):
            return Sem(octx, nc, name)

        CONSTS = [0.0, 1.0, -1.0, -0.5, 1e-5, 2.5]
        s_const = osb("s_const", (128, len(CONSTS)))
        for i, v in enumerate(CONSTS):
            nc.const_aps.register(F32, v, s_const[:, i:i + 1])
        s_acol = osb("s_acol", (128, 16))
        s_ainv = osb("s_ainv", (128, 16))
        s_dp1 = osb("s_dp1", (CPC, 1))
        s_dtb = osb("s_dtb", (CPC, 1))
        s_carry = osb("s_carry", (128, 16))
        s_zero = osb("s_zero", (128, 1))
        s_onr = osb("s_onr", (1, 128), F32R)
        s_onc = osb("s_onc", (128, 1), F32R)
        s_onc32 = osb("s_onc32", (128, 1))
        s_onr32 = osb("s_onr32", (1, 128))
        s_id = osb("s_id", (128, 128))
        s_gw = osb("s_gw", (128, 8 * NE))
        s_gbb = osb("s_gbb", (128, NE))
        s_gbr = osb("s_gbr", (1, NE), F32)
        s_ckidx = osb("s_ckidx", (128, 8 * CKPC), I32)
        s_tmp16 = osb("s_tmp16", (128, 16))

        ld0 = sem("ld0")
        ld1 = sem("ld1")
        lu = [sem("lu0"), sem("lu1")]
        ldt = [sem("ldt0"), sem("ldt1")]
        lbc = [sem("lbc0"), sem("lbc1")]
        ys = [sem("ys0"), sem("ys1")]
        lyg = [sem("lyg0"), sem("lyg1")]
        ssp = [sem("ssp0"), sem("ssp1")]
        shb = sem("shb")
        scf = sem("scf")
        stp = [sem("stp0"), sem("stp1")]
        sw = [sem(f"sw{i}") for i in range(NSLOT)]
        scr = [sem("scr0"), sem("scr1")]
        lhb = sem("lhb")
        lxr = [sem("lxr0"), sem("lxr1")]
        sp2 = sem("sp2")
        lp2 = [sem("lp20"), sem("lp21")]
        so = [sem("so0"), sem("so1")]
        cc = sem("cc")
        pe = sem("pe")
        act = sem("act")
        dve = sem("dve")
        pool = sem("pool")
        bar = sem("bar")

        phases = []

        def new_phase():
            P = Plan(self_sems={"vector": dve, "scalar": act, "gpsimd": pool})
            phases.append(P)
            return P

        ALL_DMA = ([ld0, ld1, lu[0], lu[1], ldt[0], ldt[1], lbc[0], lbc[1],
                    ys[0], ys[1], lyg[0], lyg[1], ssp[0], ssp[1], shb, scf,
                    stp[0], stp[1]] + sw + scr + [lhb, lxr[0], lxr[1], sp2,
                    lp2[0], lp2[1], so[0], so[1]])

        def barrier(P):
            for e in ENGINES:
                P.op(e, lambda eng: None,
                     waits=[(s, s.c) for s in ALL_DMA], inc=bar)
            tgt = bar.c
            return tgt

        def phase_wait(P, tgt):
            for e in ENGINES:
                P.op(e, lambda eng: None, waits=[(bar, tgt)])

        def ld16(P, q, s, dst_fn, src_fn, waits=()):
            return P.op(q, lambda e: e.dma_start(out=dst_fn(), in_=src_fn()),
                        waits=waits, inc=s, inc_n=16)

        P = new_phase()
        p1ctx = ExitStack()
        with p1ctx:
            def sb(name, shape, dt=F32):
                return p1ctx.enter_context(nc.sbuf_tensor(name, list(shape), dt))

            for i, v in enumerate(CONSTS):
                P.op("vector", lambda e, i=i, v=v: e.memset(s_const[:, i:i+1], v),
                     inc=dve)
            P.op("vector", lambda e: e.memset(s_zero[:, :], 0.0), inc=dve)
            P.op("vector", lambda e: e.memset(s_carry[:, :], 0.0), inc=dve)
            P.op("vector", lambda e: e.memset(s_onc32[:, :], 1.0), inc=dve)
            P.op("vector", lambda e: e.memset(s_onr32[:, :], 1.0), inc=dve)
            const_ready = dve.c

            ld16(P, "sync", ld0, lambda: s_onr[:, :], lambda: ones_row.ap())
            ld16(P, "sync", ld0, lambda: s_onc[:, :], lambda: ones_col.ap())
            ld16(P, "sync", ld0, lambda: s_id[:, :], lambda: ident.ap())
            ld16(P, "sync", ld0, lambda: s_ckidx[:, :], lambda: ckidx.ap())
            ld16(P, "sync", ld0, lambda: s_gbr[:, :], lambda: gateb.ap())
            for d in range(8):
                ld16(P, "sync", ld0, lambda d=d: s_gw[:, NE*d:NE*(d+1)],
                     lambda d=d: gatewT.ap()[128*d:128*(d+1), :])
            ld16(P, "sync", ld0, lambda: s_tmp16[:, :], lambda: alog.ap())
            s_dcol_t = sb("s_dcol_t", (CPC, 1))
            ld16(P, "sync", ld0, lambda: s_dcol_t[:, :], lambda: dcol.ap())
            ld16(P, "sync", ld0, lambda: s_dtb[:, :], lambda: dtb.ap())
            c_small = ld0.c

            a1 = P.op("scalar", lambda e: e.activation(
                s_acol[:, :], s_tmp16[:, :], AF.Exp),
                waits=[(ld0, c_small), (dve, const_ready)], inc=act)
            a2 = P.op("scalar", lambda e: e.activation(
                s_ainv[:, :], s_tmp16[:, :], AF.Exp, scale=-1.0), inc=act)
            P.op("vector", lambda e: e.tensor_scalar(
                s_acol[:, :], s_acol[:, :], -1.0, None, op0=ALU.mult),
                waits=[(act, a2)], inc=dve)
            ainv_ready = P.op("vector", lambda e: e.tensor_scalar(
                s_ainv[:, :], s_ainv[:, :], -1.0, None, op0=ALU.mult), inc=dve)
            acol_ready = ainv_ready
            dp1_ready = P.op("vector", lambda e: e.tensor_scalar(
                s_dp1[:, :], s_dcol_t[:, :], 1.0, None, op0=ALU.add),
                waits=[(ld0, c_small)], inc=dve)

            UW = TPC + 3
            s_utok = sb("s_utok", (128, 8 * UW), F32R)
            s_xpw = sb("s_xpw", (128, 8 * 96))
            s_cw = sb("s_cw", (128, 8 * 4))
            s_cb = sb("s_cb", (128, 8))
            s_wk = sb("s_wk", (128, 8 * 4 * 96), F32R)
            s_c96 = sb("s_c96", (96, 1))
            s_proj = sb("s_proj", (96, 2 * CK), F32R)
            pp = p1ctx.enter_context(nc.psum_tensor("pp", [96, CK], F32))
            pc96 = p1ctx.enter_context(nc.psum_tensor("pc96", [96, 1], F32))

            for d in range(8):
                ld16(P, "sync", ld1, lambda d=d: s_utok[:, UW*d:UW*(d+1)],
                     lambda d=d: u_tokT.ap()[128*d:128*(d+1), :])
                ld16(P, "sync", ld1, lambda d=d: s_xpw[:, 96*d:96*(d+1)],
                     lambda d=d: xpwT.ap()[128*d:128*(d+1), :])
                ld16(P, "sync", ld1, lambda d=d: s_cw[:, 4*d:4*(d+1)],
                     lambda d=d: cw.ap()[128*d:128*(d+1), :])
                ld16(P, "sync", ld1, lambda d=d: s_cb[:, d:d+1],
                     lambda d=d: cb.ap()[128*d:128*(d+1), :])
            c_p1 = ld1.c

            for d in range(8):
                for k in range(4):
                    P.op("vector", lambda e, d=d, k=k: e.tensor_scalar(
                        s_wk[:, d*384 + k*96: d*384 + (k+1)*96],
                        s_xpw[:, 96*d:96*(d+1)], s_cw[:, 4*d+k:4*d+k+1], None,
                        op0=ALU.mult),
                        waits=[(ld1, c_p1)] if (d == 0 and k == 0) else (),
                        inc=dve)
            wk_ready = dve.c

            for d in range(8):
                P.op("tensor", lambda e, d=d: e.matmul(
                    pc96[:, :], s_xpw[:, 96*d:96*(d+1)], s_cb[:, d:d+1],
                    start=(d == 0), stop=(d == 7)),
                    waits=[(dve, wk_ready)] if d == 0 else (),
                    inc=pe if d == 7 else None)
            c96_mm = pe.c
            c96_ready = P.op("scalar", lambda e: e.copy(s_c96[:, :], pc96[:, :]),
                             waits=[(pe, c96_mm)], inc=act)

            prev_evac = 0
            for ck_i in range(CKPC):
                buf = ck_i % 2
                for d in range(8):
                    for k in range(4):
                        col0 = UW*d + ck_i*CK + k
                        w = [(dve, prev_evac)] if (d == 0 and k == 0) else ()
                        P.op("tensor", lambda e, d=d, k=k, col0=col0: e.matmul(
                            pp[:, :], s_wk[:, d*384 + k*96: d*384 + (k+1)*96],
                            s_utok[:, col0:col0+CK],
                            start=(d == 0 and k == 0), stop=(d == 7 and k == 3)),
                            waits=w, inc=pe if (d == 7 and k == 3) else None)
                mm_done = pe.c
                prev_evac = P.op("vector", lambda e, buf=buf: e.tensor_scalar(
                    s_proj[:, CK*buf:CK*(buf+1)], pp[:, :], s_c96[:, :1], None,
                    op0=ALU.add),
                    waits=[(pe, mm_done), (act, c96_ready),
                           (stp[buf], stp[buf].c)], inc=dve)
                P.op("sync", lambda e, ck_i=ck_i, buf=buf: e.dma_start(
                    out=cc_prep_in.ap()[:, CK*ck_i:CK*(ck_i+1)],
                    in_=s_proj[:, CK*buf:CK*(buf+1)]),
                    waits=[(dve, prev_evac)], inc=stp[buf], inc_n=16)

            cc_prep_done = P.op("gpsimd", lambda e: e.collective_compute(
                "AllGather", ALU.bypass, replica_groups=[list(range(NCORE))],
                ins=[cc_prep_in.ap().opt()], outs=[cc_prep_out.ap().opt()]),
                waits=[(stp[0], stp[0].c), (stp[1], stp[1].c)], inc=cc)
            bar1 = barrier(P)
            with nc.Block("ph1") as blk:
                P.emit(blk)

        P = new_phase()
        phase_wait(P, bar1)
        p3ctx = ExitStack()
        with p3ctx:
            def sb(name, shape, dt=F32):
                return p3ctx.enter_context(nc.sbuf_tensor(name, list(shape), dt))

            s_u3 = sb("s_u3", (128, 2 * CK), F32R)
            s_dt3 = sb("s_dt3", (64, 2 * CK), F32R)
            s_bt3 = sb("s_bt3", (NST, 2 * CK), F32R)
            s_ct3 = sb("s_ct3", (NST, 2 * CK), F32R)
            s_selr = sb("s_selr", (128, 16 * 128), F32R)
            s_selc = sb("s_selc", (128, 16 * 128), F32R)
            s_selb = sb("s_selb", (NST, 128), F32R)
            s_dtw = sb("s_dtw", (DTR, CPC), F32R)
            s_sp = sb("s_sp", (128, CK))
            s_delta = sb("s_delta", (128, 2 * CK), F32R)
            RDA, RW, RT = 6, 4, 8
            s_dA = sb("s_dA", (128, RDA * CK))
            s_w1 = sb("s_w1", (128, RW * CK))
            s_w2 = sb("s_w2", (128, RW * CK))
            s_hh = sb("s_hh", (128, 32 * CK))
            s_t1 = sb("s_t1", (128, RT * CK), F32R)
            s_brep = sb("s_brep", (128, 2 * CK))
            s_crep = sb("s_crep", (128, 2 * CK))
            s_y3 = sb("s_y3", (128, 2 * CK), F32R)
            s_tp = sb("s_tp", (128, 2 * CK))
            psh = p3ctx.enter_context(nc.psum_tensor("psh", [128, CK], F32))
            pdr = [p3ctx.enter_context(nc.psum_tensor(f"pdr{i}", [128, CK], F32))
                   for i in range(2)]
            pur = [p3ctx.enter_context(nc.psum_tensor(f"pur{i}", [128, CK], F32))
                   for i in range(3)]
            py3 = [p3ctx.enter_context(nc.psum_tensor(f"py3{i}", [128, CK], F32))
                   for i in range(2)]

            for g in range(16):
                ld16(P, "sync", ld1, lambda g=g: s_selr[:, 128*g:128*(g+1)],
                     lambda g=g: selrep.ap()[g, :, :])
                ld16(P, "sync", ld1, lambda g=g: s_selc[:, 128*g:128*(g+1)],
                     lambda g=g: selcon.ap()[g, :, :])
            ld16(P, "sync", ld1, lambda: s_selb[:, :], lambda: selbc.ap())
            ld16(P, "sync", ld1, lambda: s_dtw[:, :], lambda: dtwT.ap())
            sel_ready = ld1.c

            ev = {
                "w2_free": [0] * RW,
                "t1_free": [0] * RT,
                "u_free": [0, 0],
                "dt_free": [0, 0],
                "bc_free": [0, 0],
                "delta_free": [0, 0],
                "brep_free": [0, 0],
                "crep_free": [(0, 0), (0, 0)],
                "psh_free_act": 0,
                "pdr_free": [0] * 2,
                "pur_free": [0] * 3,
                "py_free": [0, 0],
                "y_free": [0, 0],
                "w1s_free": [0] * RW,
                "loads": {},
                "spq": [],
                "a": {},
            }
            dA_free = [0] * RDA
            CLAG = 4
            pend_con = []

            def hh_col(ck, g):
                return ((ck % 2) * 16 + g) * CK

            def emit_loads(Pp, ck):
                buf = ck % 2
                rk, cko = ck // CKPC, ck % CKPC
                c_u = Pp.op("sync", lambda e, ck=ck, buf=buf: e.dma_start(
                    out=s_u3[:, CK*buf:CK*(buf+1)],
                    in_=u_chanT.ap()[:, CK*ck:CK*(ck+1)]),
                    waits=[(dve, ev["u_free"][buf])], inc=lu[buf], inc_n=16)
                c_dt = Pp.op("sync", lambda e, rk=rk, cko=cko, buf=buf:
                             e.dma_start(
                    out=s_dt3[:, CK*buf:CK*(buf+1)],
                    in_=cc_prep_out.ap()[96*rk:96*rk+64, CK*cko:CK*(cko+1)]),
                    waits=[(cc, cc_prep_done), (pe, ev["dt_free"][buf])],
                    inc=ldt[buf], inc_n=16)
                Pp.op("sync", lambda e, rk=rk, cko=cko, buf=buf: e.dma_start(
                    out=s_bt3[:, CK*buf:CK*(buf+1)],
                    in_=cc_prep_out.ap()[96*rk+64:96*rk+80, CK*cko:CK*(cko+1)]),
                    waits=[(pe, ev["bc_free"][buf])], inc=lbc[buf], inc_n=16)
                c_c = Pp.op("sync", lambda e, rk=rk, cko=cko, buf=buf:
                            e.dma_start(
                    out=s_ct3[:, CK*buf:CK*(buf+1)],
                    in_=cc_prep_out.ap()[96*rk+80:96*rk+96, CK*cko:CK*(cko+1)]),
                    inc=lbc[buf], inc_n=16)
                ev["loads"][ck] = (c_u, c_dt, c_c)

            def emit_prep1(Pp, ck):
                buf = ck % 2
                (c_u, c_dt, c_c) = ev["loads"][ck]
                pe_d = Pp.op("tensor", lambda e, buf=buf: e.matmul(
                    psh[:, :], s_dtw[:, :], s_dt3[:, CK*buf:CK*(buf+1)],
                    start=True, stop=True),
                    waits=[(ldt[buf], c_dt), (ld1, sel_ready),
                           (act, ev["psh_free_act"])], inc=pe)
                ev["dt_free"][buf] = pe.c
                a_sp = Pp.op("scalar", lambda e: e.activation(
                    s_sp[:, :], psh[:, :], AF.Exp, bias=s_dtb[:, :1]),
                    waits=[(pe, pe_d)], inc=act)
                ev["a"].setdefault(ck, {})["a_sp"] = a_sp

            def emit_prep2(Pp, ck):
                buf = ck % 2
                (c_u, c_dt, c_c) = ev["loads"][ck]
                a_sp = ev["a"][ck]["a_sp"]
                a_dl = Pp.op("scalar", lambda e, buf=buf: e.activation(
                    s_delta[:, CK*buf:CK*(buf+1)], s_sp[:, :], AF.Ln, bias=1.0),
                    waits=[(pe, ev["delta_free"][buf])], inc=act)
                pe_b = Pp.op("tensor", lambda e, buf=buf: e.matmul(
                    psh[:, :], s_selb[:, :], s_bt3[:, CK*buf:CK*(buf+1)],
                    start=True, stop=True),
                    waits=[(lbc[buf], c_c), (act, a_sp)], inc=pe)
                a_b = Pp.op("scalar", lambda e, buf=buf: e.activation(
                    s_brep[:, CK*buf:CK*(buf+1)], psh[:, :], AF.Copy),
                    waits=[(pe, pe_b), (pool, ev["brep_free"][buf])], inc=act)
                ev["a"][ck].update(a_dl=a_dl, a_b=a_b)

            def emit_prep3(Pp, ck):
                buf = ck % 2
                a_b = ev["a"][ck]["a_b"]
                pe_c = Pp.op("tensor", lambda e, buf=buf: e.matmul(
                    psh[:, :], s_selb[:, :], s_ct3[:, CK*buf:CK*(buf+1)],
                    start=True, stop=True),
                    waits=[(act, a_b)], inc=pe)
                ev["bc_free"][buf] = pe.c
                a_c = Pp.op("scalar", lambda e, buf=buf: e.activation(
                    s_crep[:, CK*buf:CK*(buf+1)], psh[:, :], AF.Copy),
                    waits=[(pe, pe_c), (dve, ev["crep_free"][buf][0]),
                           (pool, ev["crep_free"][buf][1])], inc=act)
                ev["psh_free_act"] = a_c
                ev["a"][ck]["a_c"] = a_c

            def emit_con(Pp, item):
                g, gt, ck, d_t1, t1_sem = item
                pyb = ck % 2
                pe_con = Pp.op("tensor", lambda e, g=g, gt=gt, pyb=pyb: e.matmul(
                    py3[pyb][:, :], s_selc[:, 128*g:128*(g+1)],
                    s_t1[:, CK*gt:CK*(gt+1)],
                    start=(g == 0), stop=(g == 15)),
                    waits=[(t1_sem, d_t1)] +
                          ([(dve, ev["py_free"][pyb])] if g == 0 else []),
                    inc=pe)
                ev["t1_free"][gt] = pe_con
                if g == 15:
                    buf = ck % 2
                    d_y = Pp.op("vector", lambda e, buf=buf, pyb=pyb:
                                e.scalar_tensor_tensor(
                        s_y3[:, CK*buf:CK*(buf+1)],
                        s_u3[:, CK*buf:CK*(buf+1)].bitcast(F32),
                        s_dp1[:, :1], py3[pyb][:, :],
                        op0=ALU.mult, op1=ALU.add),
                        waits=[(pe, pe_con), (dve, dp1_ready),
                               (ys[buf], ev["y_free"][buf])], inc=dve)
                    ev["u_free"][buf] = d_y
                    ev["py_free"][pyb] = d_y
                    Pp.op("gpsimd", lambda e, ck=ck, buf=buf: e.dma_start(
                        out=cc_y_in.ap()[ck // 8][:, (ck % 8)*CK:
                                                  (ck % 8 + 1)*CK],
                        in_=s_y3[:, CK*buf:CK*(buf+1)]),
                        waits=[(dve, d_y)], inc=ys[buf], inc_n=16)
                    ev["y_free"][buf] = ys[buf].c
                    if ck % 8 == 7:
                        q = ck // 8
                        Pp.op("gpsimd", lambda e, q=q: e.collective_compute(
                            "AllGather", ALU.bypass,
                            replica_groups=[list(range(NCORE))],
                            ins=[cc_y_in.ap()[q].opt()],
                            outs=[cc_y_out.ap()[q].opt()]),
                            waits=[(ys[0], ys[0].c), (ys[1], ys[1].c)], inc=cc)

            emit_loads(P, 0)
            emit_prep1(P, 0)
            emit_prep2(P, 0)
            emit_prep3(P, 0)
            for ckk in range(NCK):
                buf = ckk % 2
                a_dl = ev["a"][ckk]["a_dl"]
                a_c = ev["a"][ckk]["a_c"]
                (c_u, c_dt, c_c) = ev["loads"][ckk]
                for g in range(16):
                    gw = g % RW
                    gda = g % RDA
                    gt = (ckk * 16 + g) % RT
                    gp2 = g % 2
                    gp3 = g % 3
                    if len(pend_con) > CLAG:
                        emit_con(P, pend_con.pop(0))
                    if ckk + 1 < NCK:
                        if g == 6:
                            emit_loads(P, ckk + 1)
                        elif g == 8:
                            emit_prep1(P, ckk + 1)
                        elif g == 10:
                            emit_prep2(P, ckk + 1)
                        elif g == 12:
                            emit_prep3(P, ckk + 1)
                    pe_dr = P.op("tensor", lambda e, g=g, buf=buf, gp2=gp2:
                                 e.matmul(
                        pdr[gp2][:, :], s_selr[:, 128*g:128*(g+1)],
                        s_delta[:, CK*buf:CK*(buf+1)], start=True, stop=True),
                        waits=[(act, a_dl), (act, ev["pdr_free"][gp2])], inc=pe)
                    if g == 15:
                        ev["delta_free"][buf] = pe.c
                    pe_ur = P.op("tensor", lambda e, g=g, buf=buf, gp3=gp3:
                                 e.matmul(
                        pur[gp3][:, :], s_selr[:, 128*g:128*(g+1)],
                        s_u3[:, CK*buf:CK*(buf+1)], start=True, stop=True),
                        waits=[(lu[buf], c_u), (dve, ev["pur_free"][gp3])],
                        inc=pe)
                    a_dA = P.op("scalar", lambda e, g=g, gp2=gp2, gda=gda:
                                e.activation(
                        s_dA[:, CK*gda:CK*(gda+1)], pdr[gp2][:, :], AF.Exp,
                        scale=s_acol[:, g:g+1]),
                        waits=[(pe, pe_dr), (dve, dA_free[gda]),
                               (dve, acol_ready)], inc=act)
                    ev["pdr_free"][gp2] = a_dA
                    d_w1 = P.op("vector", lambda e, gw=gw, gda=gda, gp3=gp3:
                                e.scalar_tensor_tensor(
                        s_w1[:, CK*gw:CK*(gw+1)], s_dA[:, CK*gda:CK*(gda+1)],
                        -1.0, pur[gp3][:, :], op0=ALU.add, op1=ALU.mult),
                        waits=[(act, a_dA), (pe, pe_ur),
                               (pool, ev["w1s_free"][gw])], inc=dve)
                    ev["pur_free"][gp3] = d_w1
                    p_w2 = P.op("gpsimd", lambda e, gw=gw, buf=buf:
                                e.tensor_tensor(
                        s_w2[:, CK*gw:CK*(gw+1)], s_w1[:, CK*gw:CK*(gw+1)],
                        s_brep[:, CK*buf:CK*(buf+1)], op=ALU.mult),
                        waits=[(dve, d_w1), (act, ev["a"][ckk]["a_b"]),
                               (dve, ev["w2_free"][gw])], inc=pool)
                    ev["w1s_free"][gw] = p_w2
                    if g >= 14:
                        ev["brep_free"][buf] = p_w2
                    if ckk in (0, NCK // 2):
                        init_ap = s_zero[:, 0:1]
                    else:
                        pc = hh_col(ckk - 1, g)
                        init_ap = s_hh[:, pc + CK - 1: pc + CK]
                    hc = hh_col(ckk, g)
                    d_sc = P.op("vector", lambda e, hc=hc, gda=gda, gw=gw,
                                init_ap=init_ap: e.tensor_tensor_scan(
                        s_hh[:, hc:hc + CK], s_dA[:, CK*gda:CK*(gda+1)],
                        s_w2[:, CK*gw:CK*gw+CK],
                        init_ap, op0=ALU.mult, op1=ALU.add),
                        waits=[(pool, p_w2)], inc=dve)
                    ev["w2_free"][gw] = d_sc
                    dA_free[gda] = d_sc
                    t1_sem = dve
                    d_t1 = P.op("vector", lambda e, hc=hc, gt=gt, g=g,
                                buf=buf: e.scalar_tensor_tensor(
                        s_t1[:, CK*gt:CK*(gt+1)], s_hh[:, hc:hc + CK],
                        s_ainv[:, g:g+1], s_crep[:, CK*buf:CK*(buf+1)],
                        op0=ALU.mult, op1=ALU.mult),
                        waits=[(act, a_c), (pe, ev["t1_free"][gt]),
                               (dve, ainv_ready)],
                        inc=t1_sem)
                    if g >= 14:
                        ev["crep_free"][buf] = (d_t1, 0)
                    pend_con.append((g, gt, ckk, d_t1, t1_sem))
            while pend_con:
                emit_con(P, pend_con.pop(0))
            cc_y_done = cc.c
            bar3 = barrier(P)
            with nc.Block("ph3") as blk:
                P.emit(blk)

        P = new_phase()
        phase_wait(P, bar3)
        p5ctx = ExitStack()
        with p5ctx:
            def sb(name, shape, dt=F32):
                return p5ctx.enter_context(nc.sbuf_tensor(name, list(shape), dt))

            s_yg = sb("s_yg", (128, 2 * 8 * CK), F32R)
            s_ysq = sb("s_ysq", (128, 8 * CK), F32R)
            s_h5 = sb("s_h5", (128, 2 * 8 * CK))
            s_hb5 = sb("s_hb5", (128, 8 * CK), BF16)
            s_r1 = sb("s_r1", (1, CK))
            s_mu = sb("s_mu", (1, CK))
            s_m2 = sb("s_m2", (1, CK))
            s_var = sb("s_var", (1, CK))
            s_rstd = sb("s_rstd", (1, CK), F32R)
            s_nmu = sb("s_nmu", (1, CK), F32R)
            s_sc5 = sb("s_sc5", (128, NE))
            s_sc5b = sb("s_sc5b", (128, NE))
            s_gsc = sb("s_gsc", (128, 8))
            s_gtmp = sb("s_gtmp", (128, 8))
            s_gm = sb("s_gm", (128, 1))
            s_msk = sb("s_msk", (128, NE))
            s_mskd = sb("s_mskd", (128, NE))
            s_thr = sb("s_thr", (128, 1))
            s_wsum = sb("s_wsum", (128, 1))
            s_comb = sb("s_comb", (128, NE))
            s_combT = sb("s_combT", (NE, CK), F32R)
            p_s1 = p5ctx.enter_context(nc.psum_tensor("p_s1", [1, CK], F32))
            p_s2 = p5ctx.enter_context(nc.psum_tensor("p_s2", [1, CK], F32))
            p_bc1 = p5ctx.enter_context(nc.psum_tensor("p_bc1", [128, CK], F32))
            p_bc2 = p5ctx.enter_context(nc.psum_tensor("p_bc2", [128, CK], F32))
            p_lg = p5ctx.enter_context(nc.psum_tensor("p_lg", [128, NE], F32))
            p_tr5 = p5ctx.enter_context(nc.psum_tensor("p_tr5", [NE, 128], F32))

            pe_gb = P.op("tensor", lambda e: e.matmul(
                p_bc1[:, 0:NE], s_onr32[:, :], s_gbr[:, :], start=True,
                stop=True), inc=pe)
            gbb_ready = P.op("scalar", lambda e: e.activation(
                s_gbb[:, :], p_bc1[:, 0:NE], AF.Copy),
                waits=[(pe, pe_gb)], inc=act)

            p5 = {"yg_free": [0, 0], "h5_free": [0, 0], "hb_free": 0,
                  "ps1_free": 0, "ps2_free": 0, "bc1_free": gbb_ready,
                  "bc2_free": 0, "lg_free": 0, "tr_free": 0, "combT_free": 0}
            for cko in range(CKPC):
                buf = cko % 2
                c_yg = None
                for r in range(8):
                    c_yg = P.op("gpsimd",
                                lambda e, r=r, cko=cko, buf=buf: e.indirect_dma_start(
                        out=s_yg[:, (buf*8 + r)*CK:(buf*8 + r + 1)*CK],
                        out_offset=None,
                        in_=cc_y_out.ap().rearrange(
                            "q p (k f) -> (q p k) f", f=CK),
                        in_offset=IndirectOffsetOnAxis(
                            s_ckidx[:, r*CKPC+cko:r*CKPC+cko+1], 0)),
                        waits=[(cc, cc_y_done)] +
                              ([(dve, p5["yg_free"][buf])] if r == 0 else []),
                        inc=lyg[buf], inc_n=16)
                a_sq = None
                for d in range(8):
                    a_sq = P.op("scalar", lambda e, d=d, buf=buf: e.activation(
                        s_ysq[:, d*CK:(d+1)*CK],
                        s_yg[:, (buf*8 + d)*CK:(buf*8 + d + 1)*CK].bitcast(F32),
                        AF.Square),
                        waits=[(lyg[buf], c_yg)] if d == 0 else (), inc=act)
                for d in range(8):
                    P.op("tensor", lambda e, d=d, buf=buf: e.matmul(
                        p_s1[:, :], s_onc[:, :],
                        s_yg[:, (buf*8 + d)*CK:(buf*8 + d + 1)*CK],
                        start=(d == 0), stop=(d == 7)),
                        waits=([(lyg[buf], c_yg), (dve, p5["ps1_free"])]
                               if d == 0 else ()),
                        inc=pe if d == 7 else None)
                pe_s1 = pe.c
                for d in range(8):
                    P.op("tensor", lambda e, d=d: e.matmul(
                        p_s2[:, :], s_onc[:, :], s_ysq[:, d*CK:(d+1)*CK],
                        start=(d == 0), stop=(d == 7)),
                        waits=([(act, a_sq), (dve, p5["ps2_free"])]
                               if d == 0 else ()),
                        inc=pe if d == 7 else None)
                pe_s2 = pe.c
                d_mu = P.op("vector", lambda e: e.tensor_scalar(
                    s_mu[:, :], p_s1[0:1, :], 1.0 / DM, None, op0=ALU.mult),
                    waits=[(pe, pe_s1)], inc=dve)
                p5["ps1_free"] = d_mu
                P.op("vector", lambda e: e.tensor_tensor(
                    s_m2[:, :], s_mu[:, :], s_mu[:, :], op=ALU.mult), inc=dve)
                d_var = P.op("vector", lambda e: e.scalar_tensor_tensor(
                    s_var[:, :], p_s2[0:1, :], 1.0 / DM, s_m2[:, :],
                    op0=ALU.mult, op1=ALU.subtract),
                    waits=[(pe, pe_s2)], inc=dve)
                p5["ps2_free"] = d_var
                P.op("scalar", lambda e: e.activation(
                    s_r1[:, :], s_var[:, :], AF.Ln, bias=1e-5),
                    waits=[(dve, d_var)], inc=act)
                a_rstd = P.op("scalar", lambda e: e.activation(
                    s_rstd[:, :], s_r1[:, :], AF.Exp, scale=-0.5), inc=act)
                d_nmu = P.op("vector", lambda e: e.scalar_tensor_tensor(
                    s_nmu[:, :], s_mu[:, :], -1.0, s_rstd[:, :].bitcast(F32),
                    op0=ALU.mult, op1=ALU.mult),
                    waits=[(act, a_rstd)], inc=dve)
                pe_b1 = P.op("tensor", lambda e: e.matmul(
                    p_bc1[:, :], s_onr[:, :], s_rstd[:, :], start=True, stop=True),
                    waits=[(act, a_rstd), (act, p5["bc1_free"] if cko == 0 else 0),
                           (dve, p5["bc1_free"] if cko > 0 else 0)], inc=pe)
                pe_b2 = P.op("tensor", lambda e: e.matmul(
                    p_bc2[:, :], s_onr[:, :], s_nmu[:, :], start=True, stop=True),
                    waits=[(dve, d_nmu), (dve, p5["bc2_free"])], inc=pe)
                d_h = None
                for d in range(8):
                    P.op("vector", lambda e, d=d, buf=buf: e.tensor_tensor(
                        s_h5[:, (buf*8 + d)*CK:(buf*8 + d + 1)*CK],
                        s_yg[:, (buf*8 + d)*CK:(buf*8 + d + 1)*CK].bitcast(F32),
                        p_bc1[:, :], op=ALU.mult),
                        waits=[(pe, pe_b1)] +
                              ([(ssp[buf], p5["h5_free"][buf])]
                               if d == 0 else []), inc=dve)
                    d_h = P.op("vector", lambda e, d=d, buf=buf: e.tensor_tensor(
                        s_h5[:, (buf*8 + d)*CK:(buf*8 + d + 1)*CK],
                        s_h5[:, (buf*8 + d)*CK:(buf*8 + d + 1)*CK],
                        p_bc2[:, :], op=ALU.add),
                        waits=[(pe, pe_b2)] if d == 0 else (), inc=dve)
                p5["yg_free"][buf] = d_h
                p5["bc1_free"] = d_h
                p5["bc2_free"] = d_h
                d_hb = None
                for d in range(8):
                    d_hb = P.op("vector", lambda e, d=d, buf=buf: e.tensor_copy(
                        s_hb5[:, d*CK:(d+1)*CK],
                        s_h5[:, (buf*8 + d)*CK:(buf*8 + d + 1)*CK]),
                        waits=[(shb, p5["hb_free"])] if d == 0 else (),
                        inc=dve)
                sp1 = P.op("scalar", lambda e, cko=cko, buf=buf: e.dma_start(
                    out=xres_spill.ap()[:, CK*cko:CK*(cko+1)].rearrange(
                        "(dt p) f -> p dt f", p=128),
                    in_=s_h5[:, buf*8*CK:(buf*8 + 8)*CK]),
                    waits=[(dve, d_h)], inc=ssp[buf], inc_n=16)
                P.op("scalar", lambda e, cko=cko: e.dma_start(
                    out=hb_spill.ap()[:, CK*cko:CK*(cko+1)].rearrange(
                        "(dt p) f -> p dt f", p=128),
                    in_=s_hb5[:, :]),
                    waits=[(dve, d_hb)], inc=shb, inc_n=16)
                p5["hb_free"] = shb.c
                p5["h5_free"][buf] = ssp[buf].c

                for tt in range(4):
                    for d in range(8):
                        P.op("tensor", lambda e, d=d, tt=tt, buf=buf: e.matmul(
                            p_lg[:, :],
                            s_h5[:, (buf*8 + d)*CK + tt*128:
                                 (buf*8 + d)*CK + (tt+1)*128],
                            s_gw[:, NE*d:NE*(d+1)], start=(d == 0), stop=(d == 7)),
                            waits=([(dve, d_h), (act, p5["lg_free"])]
                                   if d == 0 else ()),
                            inc=pe if d == 7 else None)
                    pe_lg = pe.c
                    a_e = P.op("scalar", lambda e: e.activation(
                        s_sc5[:, :], p_lg[:, :], AF.Exp, scale=-1.0),
                        waits=[(pe, pe_lg), (dve, 0)], inc=act)
                    p5["lg_free"] = a_e
                    P.op("vector", lambda e: e.tensor_scalar(
                        s_sc5[:, :], s_sc5[:, :], 1.0, None, op0=ALU.add),
                        waits=[(act, a_e)], inc=dve)
                    P.op("vector", lambda e: e.reciprocal(
                        s_sc5[:, :], s_sc5[:, :]), inc=dve)
                    P.op("vector", lambda e: e.tensor_tensor(
                        s_sc5b[:, :], s_sc5[:, :], s_gbb[:, :], op=ALU.add),
                        waits=[(act, gbb_ready)], inc=dve)
                    pairs = [(0, 1), (0, 2), (0, 3), (1, 2), (1, 3), (2, 3)]
                    for pi, (i1, i2) in enumerate(pairs):
                        dst = s_gsc if pi == 0 else s_gtmp
                        P.op("vector", lambda e, i1=i1, i2=i2, dst=dst:
                             e.tensor_tensor(
                            dst[:, :], s_sc5b[:, i1::4], s_sc5b[:, i2::4],
                            op=ALU.add), inc=dve)
                        if pi > 0:
                            P.op("vector", lambda e: e.tensor_tensor(
                                s_gsc[:, :], s_gsc[:, :], s_gtmp[:, :],
                                op=ALU.max), inc=dve)
                    P.op("vector", lambda e: e.tensor_copy(
                        s_gtmp[:, :], s_gsc[:, :]), inc=dve)
                    for it in range(4):
                        P.op("vector", lambda e: e.reduce_max(
                            s_gm[:, :], s_gtmp[:, :], axis=AX.X), inc=dve)
                        if it < 3:
                            P.op("vector", lambda e: e.tensor_scalar(
                                s_msk[:, 0:8], s_gtmp[:, :], s_gm[:, :1], None,
                                op0=ALU.is_ge), inc=dve)
                            P.op("vector", lambda e: e.scalar_tensor_tensor(
                                s_gtmp[:, :], s_msk[:, 0:8], -1e30, s_gtmp[:, :],
                                op0=ALU.mult, op1=ALU.add), inc=dve)
                    P.op("vector", lambda e: e.tensor_scalar(
                        s_gtmp[:, :], s_gsc[:, :], s_gm[:, :1], None,
                        op0=ALU.is_ge), inc=dve)
                    for j in range(4):
                        P.op("vector", lambda e, j=j: e.tensor_copy(
                            s_msk[:, j::4], s_gtmp[:, :]), inc=dve)
                    P.op("vector", lambda e: e.tensor_tensor(
                        s_mskd[:, :], s_sc5b[:, :], s_msk[:, :], op=ALU.mult),
                        inc=dve)
                    P.op("vector", lambda e: e.tensor_copy(
                        s_comb[:, :], s_mskd[:, :]), inc=dve)
                    for it in range(8):
                        P.op("vector", lambda e: e.reduce_max(
                            s_thr[:, :], s_comb[:, :], axis=AX.X), inc=dve)
                        if it < 7:
                            P.op("vector", lambda e: e.tensor_scalar(
                                s_msk[:, :], s_comb[:, :], s_thr[:, :1], None,
                                op0=ALU.is_ge), inc=dve)
                            P.op("vector", lambda e: e.scalar_tensor_tensor(
                                s_comb[:, :], s_msk[:, :], -1e30, s_comb[:, :],
                                op0=ALU.mult, op1=ALU.add), inc=dve)
                    P.op("vector", lambda e: e.tensor_scalar(
                        s_msk[:, :], s_mskd[:, :], s_thr[:, :1], None,
                        op0=ALU.is_ge), inc=dve)
                    P.op("vector", lambda e: e.tensor_tensor(
                        s_comb[:, :], s_msk[:, :], s_sc5[:, :], op=ALU.mult),
                        inc=dve)
                    P.op("vector", lambda e: e.reduce_sum(
                        s_wsum[:, :], s_comb[:, :], axis=AX.X), inc=dve)
                    P.op("vector", lambda e: e.tensor_scalar(
                        s_wsum[:, :], s_wsum[:, :], 1e-20, None, op0=ALU.add),
                        inc=dve)
                    P.op("vector", lambda e: e.reciprocal(
                        s_wsum[:, :], s_wsum[:, :]), inc=dve)
                    P.op("vector", lambda e: e.tensor_scalar(
                        s_wsum[:, :], s_wsum[:, :], 2.5, None, op0=ALU.mult),
                        inc=dve)
                    d_cb = P.op("vector", lambda e: e.tensor_scalar(
                        s_comb[:, :], s_comb[:, :], s_wsum[:, :1], None,
                        op0=ALU.mult), inc=dve)
                    pe_tr = P.op("tensor", lambda e: e.transpose(
                        p_tr5[:, :], s_comb[:, :], s_id[:, :]),
                        waits=[(dve, d_cb), (act, p5["tr_free"])], inc=pe)
                    p5["tr_free"] = P.op("scalar", lambda e, tt=tt: e.activation(
                        s_combT[:, tt*128:(tt+1)*128], p_tr5[:, :], AF.Copy),
                        waits=[(pe, pe_tr), (scf, p5["combT_free"])], inc=act)
                a_combT = act.c
                P.op("scalar", lambda e, cko=cko: e.dma_start(
                    out=cflat_spill.ap()[:, CK*cko:CK*(cko+1)],
                    in_=s_combT[:, :]),
                    waits=[(act, a_combT)], inc=scf, inc_n=16)
                p5["combT_free"] = scf.c
            cflat_ready = scf.c
            hb_all = shb.c
            xres_all = [ssp[0].c, ssp[1].c]
            bar5 = barrier(P)
            with nc.Block("ph5") as blk:
                P.emit(blk)

        P = new_phase()
        phase_wait(P, bar5)
        p6ctx = ExitStack()
        with p6ctx:
            def sb(name, shape, dt=F32):
                return p6ctx.enter_context(nc.sbuf_tensor(name, list(shape), dt))

            s_wg = sb("s_wg", (128, NSLOT * 8 * EI), BF16)
            s_wu = sb("s_wu", (128, NSLOT * 8 * EI), BF16)
            s_wd = sb("s_wd", (128, NSLOT * 4 * DM), BF16)
            s_hb6 = sb("s_hb6", (128, 8 * TB), BF16)
            s_routed = sb("s_routed", (128, 8 * TB))
            s_cbc = sb("s_cbc", (128, EG * TB))
            s_crow = sb("s_crow", (1, EG * TB), F32R)
            s_sg6 = sb("s_sg6", (128, 4 * SUB))
            s_x6 = sb("s_x6", (128, 2 * 4 * SUB), BF16)
            s_xr6 = sb("s_xr6", (128, 2 * TB))
            pg = [p6ctx.enter_context(nc.psum_tensor(f"pg{i}", [128, 512], F32))
                  for i in range(2)]
            pu = [p6ctx.enter_context(nc.psum_tensor(f"pu{i}", [128, 512], F32))
                  for i in range(2)]
            pr6 = [p6ctx.enter_context(nc.psum_tensor(f"pr{i}", [128, 512], F32))
                   for i in range(4)]

            def wslot(base, slot, n):
                return base[:, slot * n:(slot + 1) * n]

            p6 = {"crow_free": [0, 0],
                  "slot_free": [0] * NSLOT, "slot_ready": [0] * NSLOT,
                  "cbc_free": 0, "sg_free": 0, "x_free": [0, 0],
                  "routed_free": 0, "pr_free": 0, "pg_free": [0, 0],
                  "pu_free": [0, 0], "hb_free": 0, "xr_free": [0, 0]}

            for tb in range(2):
                c_hb = P.op("sync", lambda e, tb=tb: e.dma_start(
                    out=s_hb6[:, :],
                    in_=hb_spill.ap()[:, TB*tb:TB*(tb+1)].rearrange(
                        "(kd p) t -> p kd t", p=128)),
                    waits=[(shb, hb_all), (pe, p6["hb_free"])],
                    inc=lhb, inc_n=16)
                d_ms = None
                for ht in range(8):
                    d_ms = P.op("vector", lambda e, ht=ht: e.memset(
                        s_routed[:, ht*TB:(ht+1)*TB], 0.0),
                        waits=[(sp2, p6["routed_free"])] if ht == 0 else (),
                        inc=dve)

                for eg_i in range(NEX // EG):
                    egs = [eg_i * EG + j for j in range(EG)]
                    slots = [ei % NSLOT for ei in egs]
                    for ei, sl in zip(egs, slots):
                        P.op("gpsimd", lambda e, ei=ei, sl=sl: e.dma_start(
                            out=wslot(s_wg, sl, 8 * EI),
                            in_=wgT.ap()[ei].rearrange("(k p) i -> p k i",
                                                       p=128)),
                            waits=[(pe, p6["slot_free"][sl])],
                            inc=sw[sl], inc_n=16)
                        P.op("gpsimd", lambda e, ei=ei, sl=sl: e.dma_start(
                            out=wslot(s_wu, sl, 8 * EI),
                            in_=wuT.ap()[ei].rearrange("(k p) i -> p k i",
                                                       p=128)),
                            inc=sw[sl], inc_n=16)
                        P.op("gpsimd", lambda e, ei=ei, sl=sl: e.dma_start(
                            out=wslot(s_wd, sl, 4 * DM),
                            in_=wdT.ap()[ei].rearrange("(k p) h -> p k h",
                                                       p=128)),
                            inc=sw[sl], inc_n=16)
                        p6["slot_ready"][sl] = sw[sl].c
                    cbc_dve = 0
                    cbc_act = 0
                    for j, ei in enumerate(egs):
                        if ei < NE:
                            c_cr = P.op("scalar", lambda e, ei=ei, tb=tb, j=j:
                                        e.dma_start(
                                out=s_crow[:, j*TB:(j+1)*TB],
                                in_=cflat_spill.ap()[ei:ei+1, TB*tb:TB*(tb+1)]),
                                waits=[(scf, cflat_ready),
                                       (pe, p6["crow_free"][j])],
                                inc=scr[j], inc_n=16)
                            pe_o1 = P.op("tensor", lambda e, j=j: e.matmul(
                                pg[0][:, :], s_onr[:, :],
                                s_crow[:, j*TB:j*TB + 512],
                                start=True, stop=True),
                                waits=[(scr[j], c_cr),
                                       (act, p6["pg_free"][0]),
                                       (dve, p6["cbc_free"])], inc=pe)
                            pe_o2 = P.op("tensor", lambda e, j=j: e.matmul(
                                pg[1][:, :], s_onr[:, :],
                                s_crow[:, j*TB + 512:(j+1)*TB],
                                start=True, stop=True),
                                waits=[(act, p6["pg_free"][1])], inc=pe)
                            p6["crow_free"][j] = pe_o2
                            a1 = P.op("scalar", lambda e, j=j: e.activation(
                                s_cbc[:, j*TB:j*TB + 512], pg[0][:, :], AF.Copy),
                                waits=[(pe, pe_o1)], inc=act)
                            a2 = P.op("scalar", lambda e, j=j: e.activation(
                                s_cbc[:, j*TB + 512:(j+1)*TB], pg[1][:, :],
                                AF.Copy),
                                waits=[(pe, pe_o2)], inc=act)
                            p6["pg_free"][0] = a1
                            p6["pg_free"][1] = a2
                            cbc_act = a2
                        else:
                            cbc_dve = P.op("vector", lambda e, j=j: e.memset(
                                s_cbc[:, j*TB:(j+1)*TB], 1.0),
                                waits=[(dve, p6["cbc_free"])], inc=dve)

                    pend_dn = []

                    def emit_down(Pp, item):
                        (j_, sub_, sl_, xslot_, d_x_) = item
                        w_extra = []
                        if j_ == 0:
                            w_extra.append((dve, p6["pr_free"]))
                            if eg_i == 0 and sub_ == 0:
                                w_extra.append((dve, d_ms))
                        for ht in range(8):
                            for ki in range(4):
                                w = []
                                if ht == 0 and ki == 0:
                                    w = [(dve, d_x_)] + w_extra
                                Pp.op("tensor",
                                      lambda e, ht=ht, ki=ki, sl=sl_, j=j_,
                                      xs=xslot_: e.matmul(
                                    pr6[ht // 2][:, (ht % 2)*SUB:
                                                 (ht % 2 + 1)*SUB],
                                    wslot(s_wd, sl, 4*DM)[
                                        :, ki*DM + ht*128:
                                        ki*DM + (ht+1)*128],
                                    s_x6[:, xs*TB + ki*SUB:
                                         xs*TB + (ki+1)*SUB],
                                    start=(ki == 0 and j == 0
                                           and ht % 2 == 0),
                                    stop=(ki == 3 and j == EG - 1
                                          and ht % 2 == 1)),
                                    waits=w,
                                    inc=pe if (ki == 3 and ht == 7)
                                    else None)
                        dn_done = pe.c
                        p6["x_free"][xslot_] = dn_done
                        if j_ == EG - 1:
                            for sl2 in set(slots):
                                p6["slot_free"][sl2] = dn_done
                            for q in range(4):
                                P.op("vector", lambda e, q=q, sub=sub_:
                                     e.tensor_tensor(
                                    s_routed[:, (2*q)*TB + sub*SUB:
                                             (2*q)*TB + sub*SUB + SUB],
                                    pr6[q][:, 0:SUB],
                                    s_routed[:, (2*q)*TB + sub*SUB:
                                             (2*q)*TB + sub*SUB + SUB],
                                    op=ALU.add),
                                    waits=[(pe, dn_done)], inc=dve)
                                P.op("vector", lambda e, q=q, sub=sub_:
                                     e.tensor_tensor(
                                    s_routed[:, (2*q+1)*TB + sub*SUB:
                                             (2*q+1)*TB + sub*SUB + SUB],
                                    pr6[q][:, SUB:2*SUB],
                                    s_routed[:, (2*q+1)*TB + sub*SUB:
                                             (2*q+1)*TB + sub*SUB + SUB],
                                    op=ALU.add), inc=dve)
                            p6["pr_free"] = dve.c

                    for sub in range(NSUB):
                        for j, ei in enumerate(egs):
                            sl = slots[j]
                            xslot = (sub * EG + j) % 2
                            for it in range(4):
                                for kd in range(8):
                                    w = []
                                    if it == 0 and kd == 0:
                                        w = [(sw[sl], p6["slot_ready"][sl]),
                                             (lhb, c_hb),
                                             (act, p6["pg_free"][0]),
                                             (act, p6["pg_free"][1])]
                                        if cbc_act:
                                            w.append((act, cbc_act))
                                        if cbc_dve:
                                            w.append((dve, cbc_dve))
                                    P.op("tensor",
                                         lambda e, it=it, kd=kd, sl=sl, sub=sub:
                                         e.matmul(
                                        pg[it // 2][:, (it % 2)*SUB:
                                                    (it % 2 + 1)*SUB],
                                        wslot(s_wg, sl, 8*EI)[
                                            :, kd*EI + it*128: kd*EI + (it+1)*128],
                                        s_hb6[:, kd*TB + sub*SUB:
                                              kd*TB + (sub+1)*SUB],
                                        start=(kd == 0 and it % 2 == 0),
                                        stop=(kd == 7 and it % 2 == 1)),
                                        waits=w,
                                        inc=pe if (kd == 7 and it % 2 == 1)
                                        else None)
                            pe_g = pe.c
                            for it in range(4):
                                for kd in range(8):
                                    w = []
                                    if it == 0 and kd == 0:
                                        w = [(dve, p6["pu_free"][0]),
                                             (dve, p6["pu_free"][1])]
                                    P.op("tensor",
                                         lambda e, it=it, kd=kd, sl=sl, sub=sub:
                                         e.matmul(
                                        pu[it // 2][:, (it % 2)*SUB:
                                                    (it % 2 + 1)*SUB],
                                        wslot(s_wu, sl, 8*EI)[
                                            :, kd*EI + it*128: kd*EI + (it+1)*128],
                                        s_hb6[:, kd*TB + sub*SUB:
                                              kd*TB + (sub+1)*SUB],
                                        start=(kd == 0 and it % 2 == 0),
                                        stop=(kd == 7 and it % 2 == 1)),
                                        waits=w,
                                        inc=pe if (kd == 7 and it % 2 == 1)
                                        else None)
                            pe_u = pe.c
                            if pend_dn:
                                emit_down(P, pend_dn.pop(0))
                            a_sg1 = P.op("scalar", lambda e: e.activation(
                                s_sg6[:, 0:512], pg[0][:, :], AF.Silu),
                                waits=[(pe, pe_g), (dve, p6["sg_free"])],
                                inc=act)
                            a_sg2 = P.op("scalar", lambda e: e.activation(
                                s_sg6[:, 512:1024], pg[1][:, :], AF.Silu),
                                inc=act)
                            p6["pg_free"][0] = a_sg1
                            p6["pg_free"][1] = a_sg2
                            d_t1 = P.op("vector", lambda e: e.tensor_tensor(
                                s_sg6[:, 0:512], pu[0][:, :], s_sg6[:, 0:512],
                                op=ALU.mult),
                                waits=[(pe, pe_u), (act, a_sg1)], inc=dve)
                            d_t2 = P.op("vector", lambda e: e.tensor_tensor(
                                s_sg6[:, 512:1024], pu[1][:, :],
                                s_sg6[:, 512:1024], op=ALU.mult),
                                waits=[(act, a_sg2)], inc=dve)
                            p6["pu_free"][0] = d_t1
                            p6["pu_free"][1] = d_t2
                            d_x = None
                            for it in range(4):
                                d_x = P.op("vector",
                                           lambda e, it=it, j=j, sub=sub,
                                           xs=xslot: e.tensor_tensor(
                                    s_x6[:, xs*TB + it*SUB:
                                         xs*TB + (it+1)*SUB],
                                    s_sg6[:, it*SUB:(it+1)*SUB],
                                    s_cbc[:, j*TB + sub*SUB:
                                          j*TB + sub*SUB + SUB],
                                    op=ALU.mult),
                                    waits=([(dve, d_t2),
                                            (pe, p6["x_free"][xslot])]
                                           if it == 0 else ()), inc=dve)
                            p6["sg_free"] = d_x
                            pend_dn.append((j, sub, sl, xslot, d_x))
                    while pend_dn:
                        emit_down(P, pend_dn.pop(0))
                    p6["cbc_free"] = dve.c

                for ht in range(8):
                    hbuf = ht % 2
                    c_xr = P.op("sync", lambda e, ht=ht, tb=tb, hbuf=hbuf:
                                e.dma_start(
                        out=s_xr6[:, hbuf*TB:(hbuf+1)*TB],
                        in_=xres_spill.ap()[128*ht:128*(ht+1), TB*tb:TB*(tb+1)]),
                        waits=[(dve, p6["xr_free"][hbuf]),
                               (ssp[0], xres_all[0]), (ssp[1], xres_all[1])],
                        inc=lxr[hbuf], inc_n=16)
                    d_p2 = P.op("vector", lambda e, ht=ht, hbuf=hbuf:
                                e.tensor_tensor(
                        s_routed[:, ht*TB:(ht+1)*TB],
                        s_routed[:, ht*TB:(ht+1)*TB],
                        s_xr6[:, hbuf*TB:(hbuf+1)*TB], op=ALU.add),
                        waits=[(lxr[hbuf], c_xr)], inc=dve)
                    p6["xr_free"][hbuf] = d_p2
                    P.op("scalar", lambda e, ht=ht, tb=tb: e.dma_start(
                        out=pre2_spill.ap()[128*ht:128*(ht+1), TB*tb:TB*(tb+1)],
                        in_=s_routed[:, ht*TB:(ht+1)*TB]),
                        waits=[(dve, d_p2)], inc=sp2, inc_n=16)
                p6["routed_free"] = sp2.c
                p6["hb_free"] = pe.c

            pre2_done = sp2.c
            bar6 = barrier(P)
            with nc.Block("ph6") as blk:
                P.emit(blk)

        P = new_phase()
        phase_wait(P, bar6)
        p7ctx = ExitStack()
        with p7ctx:
            def sb(name, shape, dt=F32):
                return p7ctx.enter_context(nc.sbuf_tensor(name, list(shape), dt))

            s_p7 = sb("s_p7", (128, 2 * 8 * CK))
            s_q7 = sb("s_q7", (128, 8 * CK))
            s_out7 = sb("s_out7", (128, 2 * DM))
            s_rs7 = sb("s_rs7", (1, CK))
            s_nm7 = sb("s_nm7", (1, CK))
            s_sclr = sb("s_sclr", (128, 1))
            s_sclm = sb("s_sclm", (128, 1))
            s_r17 = sb("s_r17", (1, CK))
            s_mu7 = sb("s_mu7", (1, CK))
            s_m27 = sb("s_m27", (1, CK))
            s_v7 = sb("s_v7", (1, CK))
            p_t7 = [p7ctx.enter_context(nc.psum_tensor(f"p_t7{i}",
                                                       [128, 128], F32))
                    for i in range(2)]
            p_s17 = p7ctx.enter_context(nc.psum_tensor("p_s17", [1, CK], F32))
            p_s27 = p7ctx.enter_context(nc.psum_tensor("p_s27", [1, CK], F32))
            p_sc7 = p7ctx.enter_context(nc.psum_tensor("p_sc7", [128, 2], F32))

            p7 = {"p7_free": [0, 0], "out_free": [0, 0], "q_free": 0,
                  "tr_free": [0, 0], "scl_free": 0, "ps1_free": 0,
                  "ps2_free": 0}
            for cko in range(CKPC):
                buf = cko % 2
                c_p2 = P.op("sync", lambda e, cko=cko, buf=buf: e.dma_start(
                    out=s_p7[:, buf*8*CK:(buf + 1)*8*CK],
                    in_=pre2_spill.ap()[:, CK*cko:CK*(cko+1)].rearrange(
                        "(dt p) t -> p dt t", p=128)),
                    waits=[(sp2, pre2_done), (pe, p7["p7_free"][buf])],
                    inc=lp2[buf], inc_n=16)
                a_sq = None
                for d in range(8):
                    a_sq = P.op("scalar", lambda e, d=d, buf=buf: e.activation(
                        s_q7[:, d*CK:(d+1)*CK],
                        s_p7[:, (buf*8 + d)*CK:(buf*8 + d+1)*CK], AF.Square),
                        waits=([(lp2[buf], c_p2), (pe, p7["q_free"])]
                               if d == 0 else ()), inc=act)
                for d in range(8):
                    P.op("tensor", lambda e, d=d, buf=buf: e.matmul(
                        p_s17[:, :], s_onc32[:, :],
                        s_p7[:, (buf*8 + d)*CK:(buf*8 + d+1)*CK],
                        start=(d == 0), stop=(d == 7)),
                        waits=([(lp2[buf], c_p2), (dve, p7["ps1_free"])]
                               if d == 0 else ()),
                        inc=pe if d == 7 else None)
                pe_s1 = pe.c
                for d in range(8):
                    P.op("tensor", lambda e, d=d: e.matmul(
                        p_s27[:, :], s_onc32[:, :], s_q7[:, d*CK:(d+1)*CK],
                        start=(d == 0), stop=(d == 7)),
                        waits=([(act, a_sq), (dve, p7["ps2_free"])]
                               if d == 0 else ()),
                        inc=pe if d == 7 else None)
                pe_s2 = pe.c
                p7["q_free"] = pe_s2
                d_mu = P.op("vector", lambda e: e.tensor_scalar(
                    s_mu7[:, :], p_s17[0:1, :], 1.0 / DM, None, op0=ALU.mult),
                    waits=[(pe, pe_s1)], inc=dve)
                p7["ps1_free"] = d_mu
                P.op("vector", lambda e: e.tensor_tensor(
                    s_m27[:, :], s_mu7[:, :], s_mu7[:, :], op=ALU.mult), inc=dve)
                d_v = P.op("vector", lambda e: e.scalar_tensor_tensor(
                    s_v7[:, :], p_s27[0:1, :], 1.0 / DM, s_m27[:, :],
                    op0=ALU.mult, op1=ALU.subtract),
                    waits=[(pe, pe_s2)], inc=dve)
                p7["ps2_free"] = d_v
                P.op("scalar", lambda e: e.activation(
                    s_r17[:, :], s_v7[:, :], AF.Ln, bias=1e-5),
                    waits=[(dve, d_v)], inc=act)
                a_rs = P.op("scalar", lambda e: e.activation(
                    s_rs7[:, :], s_r17[:, :], AF.Exp, scale=-0.5), inc=act)
                d_nm = P.op("vector", lambda e: e.scalar_tensor_tensor(
                    s_nm7[:, :], s_mu7[:, :], -1.0, s_rs7[:, :],
                    op0=ALU.mult, op1=ALU.mult),
                    waits=[(act, a_rs)], inc=dve)
                for tt in range(4):
                    obuf = tt % 2
                    pe_sc1 = P.op("tensor", lambda e, tt=tt: e.transpose(
                        p_sc7[:, 0:1], s_rs7[:, tt*128:(tt+1)*128], s_id[0:1, 0:1]),
                        waits=[(act, a_rs), (dve, d_nm),
                               (dve, p7["scl_free"])], inc=pe)
                    pe_sc2 = P.op("tensor", lambda e, tt=tt: e.transpose(
                        p_sc7[:, 1:2], s_nm7[:, tt*128:(tt+1)*128], s_id[0:1, 0:1]),
                        inc=pe)
                    a_sc1 = P.op("scalar", lambda e: e.activation(
                        s_sclr[:, :], p_sc7[:, 0:1], AF.Copy),
                        waits=[(pe, pe_sc2)], inc=act)
                    a_scl = P.op("scalar", lambda e: e.activation(
                        s_sclm[:, :], p_sc7[:, 1:2], AF.Copy), inc=act)
                    p7["scl_free"] = a_scl
                    for d in range(8):
                        dbuf = d % 2
                        pe_tr = P.op("tensor",
                                     lambda e, d=d, tt=tt, buf=buf, dbuf=dbuf:
                                     e.transpose(
                            p_t7[dbuf][:, :],
                            s_p7[:, (buf*8 + d)*CK + tt*128:
                                 (buf*8 + d)*CK + (tt+1)*128],
                            s_id[:, :]),
                            waits=[(dve, p7["tr_free"][dbuf])], inc=pe)
                        a_o = P.op("vector",
                                   lambda e, d=d, dbuf=dbuf, obuf=obuf:
                                   e.tensor_scalar(
                            s_out7[:, obuf*DM + d*128: obuf*DM + (d+1)*128],
                            p_t7[dbuf][:, :], s_sclr[:, 0:1], s_sclm[:, 0:1],
                            op0=ALU.mult, op1=ALU.add),
                            waits=[(pe, pe_tr), (act, a_scl)] +
                                  ([(so[obuf], p7["out_free"][obuf])]
                                   if d == 0 else []),
                            inc=dve)
                        p7["tr_free"][dbuf] = a_o
                    p7["scl_free"] = dve.c
                    if tt == 3:
                        p7["p7_free"][buf] = pe.c
                    a_last = dve.c
                    P.op("sync", lambda e, cko=cko, tt=tt, obuf=obuf:
                         e.dma_start(
                        out=out_h.ap()[cko*CK + tt*128: cko*CK + (tt+1)*128, :],
                        in_=s_out7[:, obuf*DM:(obuf+1)*DM]),
                        waits=[(dve, a_last)], inc=so[obuf], inc_n=16)
                    p7["out_free"][obuf] = so[obuf].c

            P.op("sync", lambda e: None,
                 waits=[(so[0], so[0].c), (so[1], so[1].c)])
            with nc.Block("ph7") as blk:
                P.emit(blk)

    return nc


def kernel(**inputs):
    f = lambda k: np.asarray(inputs[k], np.float32)
    u = np.ascontiguousarray(f("L0_attn_u").reshape(T, DM))
    conv_w = f("L0_attn_conv_w")[:, 0, :]
    conv_b = f("L0_attn_conv_b").reshape(DM, 1)
    x_proj_w = f("L0_attn_x_proj_w")
    dt_proj_w = f("L0_attn_dt_proj_w")
    dt_proj_b = f("L0_attn_dt_proj_b")
    A_log = f("L0_attn_A_log")
    D_skip = f("L0_attn_D_skip")
    gate_w = f("L0_ffn_gate_w")
    gate_bias = f("L0_ffn_gate_bias")
    w_gate = f("L0_ffn_w_gate")
    w_up = f("L0_ffn_w_up")
    w_down = f("L0_ffn_w_down")
    sh_gate = f("L0_ffn_sh_gate")
    sh_up = f("L0_ffn_sh_up")
    sh_down = f("L0_ffn_sh_down")

    wgT = np.empty((NEX, DM, EI), np.float32)
    wuT = np.empty((NEX, DM, EI), np.float32)
    wdT = np.empty((NEX, EI, DM), np.float32)
    wgT[:NE] = w_gate.transpose(0, 2, 1)
    wuT[:NE] = w_up.transpose(0, 2, 1)
    wdT[:NE] = w_down.transpose(0, 2, 1)
    shgT = sh_gate.T
    shuT = sh_up.T
    shdT = sh_down.T
    wgT[NE] = shgT[:, :EI]; wgT[NE+1] = shgT[:, EI:]
    wuT[NE] = shuT[:, :EI]; wuT[NE+1] = shuT[:, EI:]
    wdT[NE] = shdT[:EI, :]; wdT[NE+1] = shdT[EI:, :]
    wgT = np.ascontiguousarray(wgT)
    wuT = np.ascontiguousarray(wuT)
    wdT = np.ascontiguousarray(wdT)

    selrep = np.zeros((16, 128, 128), np.float32)
    selcon = np.zeros((16, 128, 128), np.float32)
    for g in range(16):
        for n in range(16):
            for j in range(8):
                selrep[g, 8 * g + j, n * 8 + j] = 1.0
                selcon[g, n * 8 + j, 8 * g + j] = 1.0
    selbc = np.zeros((NST, 128), np.float32)
    for n in range(16):
        selbc[n, n * 8:(n + 1) * 8] = 1.0

    shared = {
        "xpwT": np.ascontiguousarray(x_proj_w.T),
        "cw": np.ascontiguousarray(conv_w),
        "cb": conv_b,
        "gatewT": np.ascontiguousarray(gate_w.T),
        "gateb": gate_bias.reshape(1, NE),
        "selrep": selrep, "selcon": selcon, "selbc": selbc,
        "ones_row": np.ones((1, 128), np.float32),
        "ones_col": np.ones((128, 1), np.float32),
        "ident": np.eye(128, dtype=np.float32),
        "wgT": wgT, "wuT": wuT, "wdT": wdT,
    }

    in_maps = []
    for c in range(NCORE):
        t0 = TPC * c
        halo = (np.zeros((3, DM), np.float32) if (t0 % LQ) == 0
                else u[t0-3:t0])
        u_tokT_a = np.ascontiguousarray(
            np.concatenate([halo, u[t0:t0+TPC]], axis=0).T)
        ch0 = CPC * c
        u_chanT_a = np.ascontiguousarray(u[:, ch0:ch0+CPC].T)
        pvec = np.arange(128)
        gvec = np.arange(16)
        alog_a = A_log[ch0 + 8 * gvec[None, :] + (pvec[:, None] % 8),
                       pvec[:, None] // 8].astype(np.float32)
        rr = np.arange(8)
        jj = np.arange(CKPC)
        kk = CKPC * c + jj
        qq = kk // 8
        kin = kk % 8
        idx = ((qq[None, None, :] * 1024 + 128 * rr[None, :, None]
                + pvec[:, None, None]) * 8 + kin[None, None, :]
               ).reshape(128, 8 * CKPC).astype(np.int32)
        m = dict(shared)
        m.update({
            "u_tokT": u_tokT_a,
            "u_chanT": u_chanT_a,
            "dtwT": np.ascontiguousarray(dt_proj_w[ch0:ch0+CPC].T),
            "dtb": dt_proj_b[ch0:ch0+CPC].reshape(CPC, 1),
            "alog": np.ascontiguousarray(alog_a),
            "dcol": D_skip[ch0:ch0+CPC].reshape(CPC, 1),
            "ckidx": np.ascontiguousarray(idx),
        })
        in_maps.append(m)

    nc = bass.Bass("TRN2", target_bir_lowering=False, debug=False)
    nc = _build(nc)
    res = run_bass_kernel_spmd(nc, in_maps, core_ids=list(range(NCORE)))
    out = np.concatenate([np.asarray(res.results[c]["out"], np.float32)
                          for c in range(NCORE)], axis=0)
    return out.reshape(B, LQ, DM)


# version 10
# speedup vs baseline: 1.2530x; 1.0180x over previous
import numpy as np
from contextlib import ExitStack

import concourse.bass as bass
import concourse.mybir as mybir
from concourse.bass_utils import run_bass_kernel_spmd
from concourse.bass_core_rust import IndirectOffsetOnAxis

F32 = mybir.dt.float32
F32R = mybir.dt.float32r
BF16 = mybir.dt.bfloat16
I32 = mybir.dt.int32
AF = mybir.ActivationFunctionType
ALU = mybir.AluOpType
AX = mybir.AxisListType

NCORE = 8
DM = 1024
NST = 16
DTR = 64
B = 2
LQ = 8192
T = B * LQ
TPC = T // NCORE
CPC = DM // NCORE
CK = 512
NCK = T // CK
CKPC = NCK // NCORE
NE = 32
EI = 512
NEX = NE + 2
TB = 1024
SUB = 256
NSUB = TB // SUB
EG = 2
NSLOT = 4

ENGINES = ("sync", "tensor", "vector", "scalar", "gpsimd")


class Sem:
    def __init__(self, ctx, nc, name):
        self.h = ctx.enter_context(nc.semaphore(name))
        self.c = 0


class Plan:
    def __init__(self, self_sems=None):
        self.ops = {e: [] for e in ENGINES}
        self.self_sems = self_sems or {}

    def op(self, eng, emit, waits=(), inc=None, inc_n=1):
        w = [(s.h, v) for (s, v) in waits if v > 0]
        ss = self.self_sems.get(eng)
        if ss is not None and ss.c > 0:
            w.append((ss.h, ss.c))
        if inc is not None:
            inc.c += inc_n
            self.ops[eng].append((w, emit, inc.h, inc_n))
            return inc.c
        self.ops[eng].append((w, emit, None, 0))
        return None

    WMARKS = {e: {} for e in ENGINES}

    def emit(self, block):
        for eng_name in ENGINES:
            lst = self.ops[eng_name]
            if not lst:
                continue

            def body(eng, lst=lst, eng_name=eng_name):
                wm = Plan.WMARKS[eng_name]
                for waits, emit, sem_h, n in lst:
                    for (h, v) in waits:
                        if wm.get(id(h), -1) >= v:
                            continue
                        eng.wait_ge(h, v)
                        wm[id(h)] = v
                    ins = emit(eng)
                    if sem_h is not None:
                        if ins is None:
                            eng.sem_inc(sem_h, n)
                        else:
                            ins.then_inc(sem_h, n)
            getattr(block, eng_name)(body)


def _build(nc):
    dram = {}

    def din(name, shape, dt=F32):
        dram[name] = nc.dram_tensor(name, list(shape), dt, kind="ExternalInput")
        return dram[name]

    u_tokT = din("u_tokT", (DM, TPC + 3), F32R)
    u_chanT = din("u_chanT", (CPC, T), F32R)
    dtwT = din("dtwT", (DTR, CPC), F32R)
    dtb = din("dtb", (CPC, 1), F32)
    alog = din("alog", (128, 16), F32)
    dcol = din("dcol", (CPC, 1), F32)
    xpwT = din("xpwT", (DM, 96), F32)
    cw = din("cw", (DM, 4), F32)
    cb = din("cb", (DM, 1), F32)
    gatewT = din("gatewT", (DM, NE), F32)
    gateb = din("gateb", (1, NE), F32)
    selrep = din("selrep", (16, 128, 128), F32R)
    selcon = din("selcon", (16, 128, 128), F32R)
    selbc = din("selbc", (NST, 128), F32R)
    ones_row = din("ones_row", (1, 128), F32R)
    ones_col = din("ones_col", (128, 1), F32R)
    ident = din("ident", (128, 128), F32)
    ckidx = din("ckidx", (128, 8 * CKPC), I32)
    wgT = din("wgT", (NEX, DM, EI), F32)
    wuT = din("wuT", (NEX, DM, EI), F32)
    wdT = din("wdT", (NEX, EI, DM), F32)
    out_h = nc.dram_tensor("out", [TPC, DM], F32, kind="ExternalOutput")

    cc_prep_in = nc.dram_tensor("cc_prep_in", [96, TPC], F32R, kind="Internal")
    cc_prep_out = nc.dram_tensor("cc_prep_out", [96 * NCORE, TPC], F32R,
                                 kind="Internal")
    cc_y_in = nc.dram_tensor("cc_y_in", [4, CPC, 4096], F32R, kind="Internal")
    cc_y_out = nc.dram_tensor("cc_y_out", [4, DM, 4096], F32R, kind="Internal")
    xres_spill = nc.dram_tensor("xres_spill", [DM, TPC], F32, kind="Internal")
    hb_spill = nc.dram_tensor("hb_spill", [DM, TPC], BF16, kind="Internal")
    pre2_spill = nc.dram_tensor("pre2_spill", [DM, TPC], F32, kind="Internal")
    cflat_spill = nc.dram_tensor("cflat_spill", [NE, TPC], F32R, kind="Internal")

    octx = ExitStack()
    with octx:
        def osb(name, shape, dt=F32):
            return octx.enter_context(nc.sbuf_tensor(name, list(shape), dt))

        def sem(name):
            return Sem(octx, nc, name)

        CONSTS = [0.0, 1.0, -1.0, -0.5, 1e-5, 2.5]
        s_const = osb("s_const", (128, len(CONSTS)))
        for i, v in enumerate(CONSTS):
            nc.const_aps.register(F32, v, s_const[:, i:i + 1])
        s_acol = osb("s_acol", (128, 16))
        s_ainv = osb("s_ainv", (128, 16))
        s_dp1 = osb("s_dp1", (CPC, 1))
        s_dtb = osb("s_dtb", (CPC, 1))
        s_carry = osb("s_carry", (128, 16))
        s_zero = osb("s_zero", (128, 1))
        s_onr = osb("s_onr", (1, 128), F32R)
        s_onc = osb("s_onc", (128, 1), F32R)
        s_onc32 = osb("s_onc32", (128, 1))
        s_onr32 = osb("s_onr32", (1, 128))
        s_id = osb("s_id", (128, 128))
        s_gw = osb("s_gw", (128, 8 * NE))
        s_gbb = osb("s_gbb", (128, NE))
        s_gbr = osb("s_gbr", (1, NE), F32)
        s_ckidx = osb("s_ckidx", (128, 8 * CKPC), I32)
        s_tmp16 = osb("s_tmp16", (128, 16))

        ld0 = sem("ld0")
        ld1 = sem("ld1")
        lu = [sem("lu0"), sem("lu1")]
        ldt = [sem("ldt0"), sem("ldt1")]
        lbc = [sem("lbc0"), sem("lbc1")]
        ys = [sem("ys0"), sem("ys1")]
        lyg = [sem("lyg0"), sem("lyg1")]
        ssp = [sem("ssp0"), sem("ssp1")]
        shb = sem("shb")
        scf = sem("scf")
        stp = [sem("stp0"), sem("stp1")]
        sw = [sem(f"sw{i}") for i in range(NSLOT)]
        scr = [sem("scr0"), sem("scr1")]
        lhb = sem("lhb")
        lxr = [sem("lxr0"), sem("lxr1")]
        sp2 = sem("sp2")
        lp2 = [sem("lp20"), sem("lp21")]
        so = [sem("so0"), sem("so1")]
        cc = sem("cc")
        pe = sem("pe")
        act = sem("act")
        dve = sem("dve")
        pool = sem("pool")
        bar = sem("bar")

        phases = []

        def new_phase():
            P = Plan(self_sems={"vector": dve, "scalar": act, "gpsimd": pool})
            phases.append(P)
            return P

        ALL_DMA = ([ld0, ld1, lu[0], lu[1], ldt[0], ldt[1], lbc[0], lbc[1],
                    ys[0], ys[1], lyg[0], lyg[1], ssp[0], ssp[1], shb, scf,
                    stp[0], stp[1]] + sw + scr + [lhb, lxr[0], lxr[1], sp2,
                    lp2[0], lp2[1], so[0], so[1]])

        def barrier(P):
            for e in ENGINES:
                P.op(e, lambda eng: None,
                     waits=[(s, s.c) for s in ALL_DMA], inc=bar)
            tgt = bar.c
            return tgt

        def phase_wait(P, tgt):
            for e in ENGINES:
                P.op(e, lambda eng: None, waits=[(bar, tgt)])

        def ld16(P, q, s, dst_fn, src_fn, waits=()):
            return P.op(q, lambda e: e.dma_start(out=dst_fn(), in_=src_fn()),
                        waits=waits, inc=s, inc_n=16)

        P = new_phase()
        p1ctx = ExitStack()
        with p1ctx:
            def sb(name, shape, dt=F32):
                return p1ctx.enter_context(nc.sbuf_tensor(name, list(shape), dt))

            for i, v in enumerate(CONSTS):
                P.op("vector", lambda e, i=i, v=v: e.memset(s_const[:, i:i+1], v),
                     inc=dve)
            P.op("vector", lambda e: e.memset(s_zero[:, :], 0.0), inc=dve)
            P.op("vector", lambda e: e.memset(s_carry[:, :], 0.0), inc=dve)
            P.op("vector", lambda e: e.memset(s_onc32[:, :], 1.0), inc=dve)
            P.op("vector", lambda e: e.memset(s_onr32[:, :], 1.0), inc=dve)
            const_ready = dve.c

            ld16(P, "sync", ld0, lambda: s_onr[:, :], lambda: ones_row.ap())
            ld16(P, "sync", ld0, lambda: s_onc[:, :], lambda: ones_col.ap())
            ld16(P, "sync", ld0, lambda: s_id[:, :], lambda: ident.ap())
            ld16(P, "sync", ld0, lambda: s_ckidx[:, :], lambda: ckidx.ap())
            ld16(P, "sync", ld0, lambda: s_gbr[:, :], lambda: gateb.ap())
            for d in range(8):
                ld16(P, "sync", ld0, lambda d=d: s_gw[:, NE*d:NE*(d+1)],
                     lambda d=d: gatewT.ap()[128*d:128*(d+1), :])
            ld16(P, "sync", ld0, lambda: s_tmp16[:, :], lambda: alog.ap())
            s_dcol_t = sb("s_dcol_t", (CPC, 1))
            ld16(P, "sync", ld0, lambda: s_dcol_t[:, :], lambda: dcol.ap())
            ld16(P, "sync", ld0, lambda: s_dtb[:, :], lambda: dtb.ap())
            c_small = ld0.c

            a1 = P.op("scalar", lambda e: e.activation(
                s_acol[:, :], s_tmp16[:, :], AF.Exp),
                waits=[(ld0, c_small), (dve, const_ready)], inc=act)
            a2 = P.op("scalar", lambda e: e.activation(
                s_ainv[:, :], s_tmp16[:, :], AF.Exp, scale=-1.0), inc=act)
            P.op("vector", lambda e: e.tensor_scalar(
                s_acol[:, :], s_acol[:, :], -1.0, None, op0=ALU.mult),
                waits=[(act, a2)], inc=dve)
            ainv_ready = P.op("vector", lambda e: e.tensor_scalar(
                s_ainv[:, :], s_ainv[:, :], -1.0, None, op0=ALU.mult), inc=dve)
            acol_ready = ainv_ready
            dp1_ready = P.op("vector", lambda e: e.tensor_scalar(
                s_dp1[:, :], s_dcol_t[:, :], 1.0, None, op0=ALU.add),
                waits=[(ld0, c_small)], inc=dve)

            UW = TPC + 3
            s_utok = sb("s_utok", (128, 8 * UW), F32R)
            s_xpw = sb("s_xpw", (128, 8 * 96))
            s_cw = sb("s_cw", (128, 8 * 4))
            s_cb = sb("s_cb", (128, 8))
            s_wk = sb("s_wk", (128, 8 * 4 * 96), F32R)
            s_c96 = sb("s_c96", (96, 1))
            s_proj = sb("s_proj", (96, 2 * CK), F32R)
            pp = p1ctx.enter_context(nc.psum_tensor("pp", [96, CK], F32))
            pc96 = p1ctx.enter_context(nc.psum_tensor("pc96", [96, 1], F32))

            for d in range(8):
                ld16(P, "sync", ld1, lambda d=d: s_utok[:, UW*d:UW*(d+1)],
                     lambda d=d: u_tokT.ap()[128*d:128*(d+1), :])
                ld16(P, "sync", ld1, lambda d=d: s_xpw[:, 96*d:96*(d+1)],
                     lambda d=d: xpwT.ap()[128*d:128*(d+1), :])
                ld16(P, "sync", ld1, lambda d=d: s_cw[:, 4*d:4*(d+1)],
                     lambda d=d: cw.ap()[128*d:128*(d+1), :])
                ld16(P, "sync", ld1, lambda d=d: s_cb[:, d:d+1],
                     lambda d=d: cb.ap()[128*d:128*(d+1), :])
            c_p1 = ld1.c

            for d in range(8):
                for k in range(4):
                    P.op("vector", lambda e, d=d, k=k: e.tensor_scalar(
                        s_wk[:, d*384 + k*96: d*384 + (k+1)*96],
                        s_xpw[:, 96*d:96*(d+1)], s_cw[:, 4*d+k:4*d+k+1], None,
                        op0=ALU.mult),
                        waits=[(ld1, c_p1)] if (d == 0 and k == 0) else (),
                        inc=dve)
            wk_ready = dve.c

            for d in range(8):
                P.op("tensor", lambda e, d=d: e.matmul(
                    pc96[:, :], s_xpw[:, 96*d:96*(d+1)], s_cb[:, d:d+1],
                    start=(d == 0), stop=(d == 7)),
                    waits=[(dve, wk_ready)] if d == 0 else (),
                    inc=pe if d == 7 else None)
            c96_mm = pe.c
            c96_ready = P.op("scalar", lambda e: e.copy(s_c96[:, :], pc96[:, :]),
                             waits=[(pe, c96_mm)], inc=act)

            prev_evac = 0
            for ck_i in range(CKPC):
                buf = ck_i % 2
                for d in range(8):
                    for k in range(4):
                        col0 = UW*d + ck_i*CK + k
                        w = [(dve, prev_evac)] if (d == 0 and k == 0) else ()
                        P.op("tensor", lambda e, d=d, k=k, col0=col0: e.matmul(
                            pp[:, :], s_wk[:, d*384 + k*96: d*384 + (k+1)*96],
                            s_utok[:, col0:col0+CK],
                            start=(d == 0 and k == 0), stop=(d == 7 and k == 3)),
                            waits=w, inc=pe if (d == 7 and k == 3) else None)
                mm_done = pe.c
                prev_evac = P.op("vector", lambda e, buf=buf: e.tensor_scalar(
                    s_proj[:, CK*buf:CK*(buf+1)], pp[:, :], s_c96[:, :1], None,
                    op0=ALU.add),
                    waits=[(pe, mm_done), (act, c96_ready),
                           (stp[buf], stp[buf].c)], inc=dve)
                P.op("sync", lambda e, ck_i=ck_i, buf=buf: e.dma_start(
                    out=cc_prep_in.ap()[:, CK*ck_i:CK*(ck_i+1)],
                    in_=s_proj[:, CK*buf:CK*(buf+1)]),
                    waits=[(dve, prev_evac)], inc=stp[buf], inc_n=16)

            cc_prep_done = P.op("gpsimd", lambda e: e.collective_compute(
                "AllGather", ALU.bypass, replica_groups=[list(range(NCORE))],
                ins=[cc_prep_in.ap().opt()], outs=[cc_prep_out.ap().opt()]),
                waits=[(stp[0], stp[0].c), (stp[1], stp[1].c)], inc=cc)
            bar1 = barrier(P)
            with nc.Block("ph1") as blk:
                P.emit(blk)

        P = new_phase()
        phase_wait(P, bar1)
        p3ctx = ExitStack()
        with p3ctx:
            def sb(name, shape, dt=F32):
                return p3ctx.enter_context(nc.sbuf_tensor(name, list(shape), dt))

            s_u3 = sb("s_u3", (128, 2 * CK), F32R)
            s_dt3 = sb("s_dt3", (64, 2 * CK), F32R)
            s_bt3 = sb("s_bt3", (NST, 2 * CK), F32R)
            s_ct3 = sb("s_ct3", (NST, 2 * CK), F32R)
            s_selr = sb("s_selr", (128, 16 * 128), F32R)
            s_selc = sb("s_selc", (128, 16 * 128), F32R)
            s_selb = sb("s_selb", (NST, 128), F32R)
            s_dtw = sb("s_dtw", (DTR, CPC), F32R)
            s_sp = sb("s_sp", (128, CK))
            s_delta = sb("s_delta", (128, 2 * CK), F32R)
            RDA, RW, RT = 6, 4, 8
            s_dA = sb("s_dA", (128, RDA * CK))
            s_w1 = sb("s_w1", (128, RW * CK))
            s_w2 = sb("s_w2", (128, RW * CK))
            s_hh = sb("s_hh", (128, 32 * CK))
            s_t1 = sb("s_t1", (128, RT * CK), F32R)
            s_brep = sb("s_brep", (128, 2 * CK))
            s_crep = sb("s_crep", (128, 2 * CK))
            s_y3 = sb("s_y3", (128, 2 * CK), F32R)
            s_tp = sb("s_tp", (128, 2 * CK))
            psh = p3ctx.enter_context(nc.psum_tensor("psh", [128, CK], F32))
            pdr = [p3ctx.enter_context(nc.psum_tensor(f"pdr{i}", [128, CK], F32))
                   for i in range(2)]
            pur = [p3ctx.enter_context(nc.psum_tensor(f"pur{i}", [128, CK], F32))
                   for i in range(3)]
            py3 = [p3ctx.enter_context(nc.psum_tensor(f"py3{i}", [128, CK], F32))
                   for i in range(2)]

            for g in range(16):
                ld16(P, "sync", ld1, lambda g=g: s_selr[:, 128*g:128*(g+1)],
                     lambda g=g: selrep.ap()[g, :, :])
                ld16(P, "sync", ld1, lambda g=g: s_selc[:, 128*g:128*(g+1)],
                     lambda g=g: selcon.ap()[g, :, :])
            ld16(P, "sync", ld1, lambda: s_selb[:, :], lambda: selbc.ap())
            ld16(P, "sync", ld1, lambda: s_dtw[:, :], lambda: dtwT.ap())
            sel_ready = ld1.c

            ev = {
                "w2_free": [0] * RW,
                "t1_free": [0] * RT,
                "u_free": [0, 0],
                "dt_free": [0, 0],
                "bc_free": [0, 0],
                "delta_free": [0, 0],
                "brep_free": [0, 0],
                "crep_free": [(0, 0), (0, 0)],
                "psh_free_act": 0,
                "pdr_free": [0] * 2,
                "pur_free": [0] * 3,
                "py_free": [0, 0],
                "y_free": [0, 0],
                "w1s_free": [0] * RW,
                "loads": {},
                "spq": [],
                "a": {},
            }
            dA_free = [0] * RDA
            CLAG = 4
            pend_con = []

            def hh_col(ck, g):
                return ((ck % 2) * 16 + g) * CK

            def emit_loads(Pp, ck):
                buf = ck % 2
                rk, cko = ck // CKPC, ck % CKPC
                c_u = Pp.op("sync", lambda e, ck=ck, buf=buf: e.dma_start(
                    out=s_u3[:, CK*buf:CK*(buf+1)],
                    in_=u_chanT.ap()[:, CK*ck:CK*(ck+1)]),
                    waits=[(dve, ev["u_free"][buf])], inc=lu[buf], inc_n=16)
                c_dt = Pp.op("sync", lambda e, rk=rk, cko=cko, buf=buf:
                             e.dma_start(
                    out=s_dt3[:, CK*buf:CK*(buf+1)],
                    in_=cc_prep_out.ap()[96*rk:96*rk+64, CK*cko:CK*(cko+1)]),
                    waits=[(cc, cc_prep_done), (pe, ev["dt_free"][buf])],
                    inc=ldt[buf], inc_n=16)
                Pp.op("sync", lambda e, rk=rk, cko=cko, buf=buf: e.dma_start(
                    out=s_bt3[:, CK*buf:CK*(buf+1)],
                    in_=cc_prep_out.ap()[96*rk+64:96*rk+80, CK*cko:CK*(cko+1)]),
                    waits=[(pe, ev["bc_free"][buf])], inc=lbc[buf], inc_n=16)
                c_c = Pp.op("sync", lambda e, rk=rk, cko=cko, buf=buf:
                            e.dma_start(
                    out=s_ct3[:, CK*buf:CK*(buf+1)],
                    in_=cc_prep_out.ap()[96*rk+80:96*rk+96, CK*cko:CK*(cko+1)]),
                    inc=lbc[buf], inc_n=16)
                ev["loads"][ck] = (c_u, c_dt, c_c)

            def emit_prep1(Pp, ck):
                buf = ck % 2
                (c_u, c_dt, c_c) = ev["loads"][ck]
                pe_d = Pp.op("tensor", lambda e, buf=buf: e.matmul(
                    psh[:, :], s_dtw[:, :], s_dt3[:, CK*buf:CK*(buf+1)],
                    start=True, stop=True),
                    waits=[(ldt[buf], c_dt), (ld1, sel_ready),
                           (act, ev["psh_free_act"])], inc=pe)
                ev["dt_free"][buf] = pe.c
                a_sp = Pp.op("scalar", lambda e: e.activation(
                    s_sp[:, :], psh[:, :], AF.Exp, bias=s_dtb[:, :1]),
                    waits=[(pe, pe_d)], inc=act)
                ev["a"].setdefault(ck, {})["a_sp"] = a_sp

            def emit_prep2(Pp, ck):
                buf = ck % 2
                (c_u, c_dt, c_c) = ev["loads"][ck]
                a_sp = ev["a"][ck]["a_sp"]
                a_dl = Pp.op("scalar", lambda e, buf=buf: e.activation(
                    s_delta[:, CK*buf:CK*(buf+1)], s_sp[:, :], AF.Ln, bias=1.0),
                    waits=[(pe, ev["delta_free"][buf])], inc=act)
                pe_b = Pp.op("tensor", lambda e, buf=buf: e.matmul(
                    psh[:, :], s_selb[:, :], s_bt3[:, CK*buf:CK*(buf+1)],
                    start=True, stop=True),
                    waits=[(lbc[buf], c_c), (act, a_sp)], inc=pe)
                a_b = Pp.op("scalar", lambda e, buf=buf: e.activation(
                    s_brep[:, CK*buf:CK*(buf+1)], psh[:, :], AF.Copy),
                    waits=[(pe, pe_b), (pool, ev["brep_free"][buf])], inc=act)
                ev["a"][ck].update(a_dl=a_dl, a_b=a_b)

            def emit_prep3(Pp, ck):
                buf = ck % 2
                a_b = ev["a"][ck]["a_b"]
                pe_c = Pp.op("tensor", lambda e, buf=buf: e.matmul(
                    psh[:, :], s_selb[:, :], s_ct3[:, CK*buf:CK*(buf+1)],
                    start=True, stop=True),
                    waits=[(act, a_b)], inc=pe)
                ev["bc_free"][buf] = pe.c
                a_c = Pp.op("scalar", lambda e, buf=buf: e.activation(
                    s_crep[:, CK*buf:CK*(buf+1)], psh[:, :], AF.Copy),
                    waits=[(pe, pe_c), (dve, ev["crep_free"][buf][0]),
                           (pool, ev["crep_free"][buf][1])], inc=act)
                ev["psh_free_act"] = a_c
                ev["a"][ck]["a_c"] = a_c

            def emit_con(Pp, item):
                g, gt, ck, d_t1, t1_sem = item
                pyb = ck % 2
                pe_con = Pp.op("tensor", lambda e, g=g, gt=gt, pyb=pyb: e.matmul(
                    py3[pyb][:, :], s_selc[:, 128*g:128*(g+1)],
                    s_t1[:, CK*gt:CK*(gt+1)],
                    start=(g == 0), stop=(g == 15)),
                    waits=[(t1_sem, d_t1)] +
                          ([(dve, ev["py_free"][pyb])] if g == 0 else []),
                    inc=pe)
                ev["t1_free"][gt] = pe_con
                if g == 15:
                    buf = ck % 2
                    d_y = Pp.op("vector", lambda e, buf=buf, pyb=pyb:
                                e.scalar_tensor_tensor(
                        s_y3[:, CK*buf:CK*(buf+1)],
                        s_u3[:, CK*buf:CK*(buf+1)].bitcast(F32),
                        s_dp1[:, :1], py3[pyb][:, :],
                        op0=ALU.mult, op1=ALU.add),
                        waits=[(pe, pe_con), (dve, dp1_ready),
                               (ys[buf], ev["y_free"][buf])], inc=dve)
                    ev["u_free"][buf] = d_y
                    ev["py_free"][pyb] = d_y
                    Pp.op("gpsimd", lambda e, ck=ck, buf=buf: e.dma_start(
                        out=cc_y_in.ap()[ck // 8][:, (ck % 8)*CK:
                                                  (ck % 8 + 1)*CK],
                        in_=s_y3[:, CK*buf:CK*(buf+1)]),
                        waits=[(dve, d_y)], inc=ys[buf], inc_n=16)
                    ev["y_free"][buf] = ys[buf].c
                    if ck % 8 == 7:
                        q = ck // 8
                        Pp.op("gpsimd", lambda e, q=q: e.collective_compute(
                            "AllGather", ALU.bypass,
                            replica_groups=[list(range(NCORE))],
                            ins=[cc_y_in.ap()[q].opt()],
                            outs=[cc_y_out.ap()[q].opt()]),
                            waits=[(ys[0], ys[0].c), (ys[1], ys[1].c)], inc=cc)

            emit_loads(P, 0)
            emit_prep1(P, 0)
            emit_prep2(P, 0)
            emit_prep3(P, 0)
            for ckk in range(NCK):
                buf = ckk % 2
                a_dl = ev["a"][ckk]["a_dl"]
                a_c = ev["a"][ckk]["a_c"]
                (c_u, c_dt, c_c) = ev["loads"][ckk]
                for g in range(16):
                    gw = g % RW
                    gda = g % RDA
                    gt = (ckk * 16 + g) % RT
                    gp2 = g % 2
                    gp3 = g % 3
                    if len(pend_con) > CLAG:
                        emit_con(P, pend_con.pop(0))
                    if ckk + 1 < NCK:
                        if g == 6:
                            emit_loads(P, ckk + 1)
                        elif g == 8:
                            emit_prep1(P, ckk + 1)
                        elif g == 10:
                            emit_prep2(P, ckk + 1)
                        elif g == 12:
                            emit_prep3(P, ckk + 1)
                    pe_dr = P.op("tensor", lambda e, g=g, buf=buf, gp2=gp2:
                                 e.matmul(
                        pdr[gp2][:, :], s_selr[:, 128*g:128*(g+1)],
                        s_delta[:, CK*buf:CK*(buf+1)], start=True, stop=True),
                        waits=[(act, a_dl), (act, ev["pdr_free"][gp2])], inc=pe)
                    if g == 15:
                        ev["delta_free"][buf] = pe.c
                    pe_ur = P.op("tensor", lambda e, g=g, buf=buf, gp3=gp3:
                                 e.matmul(
                        pur[gp3][:, :], s_selr[:, 128*g:128*(g+1)],
                        s_u3[:, CK*buf:CK*(buf+1)], start=True, stop=True),
                        waits=[(lu[buf], c_u), (dve, ev["pur_free"][gp3])],
                        inc=pe)
                    a_dA = P.op("scalar", lambda e, g=g, gp2=gp2, gda=gda:
                                e.activation(
                        s_dA[:, CK*gda:CK*(gda+1)], pdr[gp2][:, :], AF.Exp,
                        scale=s_acol[:, g:g+1]),
                        waits=[(pe, pe_dr), (dve, dA_free[gda]),
                               (dve, acol_ready)], inc=act)
                    ev["pdr_free"][gp2] = a_dA
                    d_w1 = P.op("vector", lambda e, gw=gw, gda=gda, gp3=gp3:
                                e.scalar_tensor_tensor(
                        s_w1[:, CK*gw:CK*(gw+1)], s_dA[:, CK*gda:CK*(gda+1)],
                        -1.0, pur[gp3][:, :], op0=ALU.add, op1=ALU.mult),
                        waits=[(act, a_dA), (pe, pe_ur),
                               (pool, ev["w1s_free"][gw])], inc=dve)
                    ev["pur_free"][gp3] = d_w1
                    p_w2 = P.op("gpsimd", lambda e, gw=gw, buf=buf:
                                e.tensor_tensor(
                        s_w2[:, CK*gw:CK*(gw+1)], s_w1[:, CK*gw:CK*(gw+1)],
                        s_brep[:, CK*buf:CK*(buf+1)], op=ALU.mult),
                        waits=[(dve, d_w1), (act, ev["a"][ckk]["a_b"]),
                               (dve, ev["w2_free"][gw])], inc=pool)
                    ev["w1s_free"][gw] = p_w2
                    if g >= 14:
                        ev["brep_free"][buf] = p_w2
                    if ckk in (0, NCK // 2):
                        init_ap = s_zero[:, 0:1]
                    else:
                        pc = hh_col(ckk - 1, g)
                        init_ap = s_hh[:, pc + CK - 1: pc + CK]
                    hc = hh_col(ckk, g)
                    d_sc = P.op("vector", lambda e, hc=hc, gda=gda, gw=gw,
                                init_ap=init_ap: e.tensor_tensor_scan(
                        s_hh[:, hc:hc + CK], s_dA[:, CK*gda:CK*(gda+1)],
                        s_w2[:, CK*gw:CK*gw+CK],
                        init_ap, op0=ALU.mult, op1=ALU.add),
                        waits=[(pool, p_w2)], inc=dve)
                    ev["w2_free"][gw] = d_sc
                    dA_free[gda] = d_sc
                    t1_sem = dve
                    d_t1 = P.op("vector", lambda e, hc=hc, gt=gt, g=g,
                                buf=buf: e.scalar_tensor_tensor(
                        s_t1[:, CK*gt:CK*(gt+1)], s_hh[:, hc:hc + CK],
                        s_ainv[:, g:g+1], s_crep[:, CK*buf:CK*(buf+1)],
                        op0=ALU.mult, op1=ALU.mult),
                        waits=[(act, a_c), (pe, ev["t1_free"][gt]),
                               (dve, ainv_ready)],
                        inc=t1_sem)
                    if g >= 14:
                        ev["crep_free"][buf] = (d_t1, 0)
                    pend_con.append((g, gt, ckk, d_t1, t1_sem))
            while pend_con:
                emit_con(P, pend_con.pop(0))
            cc_y_done = cc.c
            bar3 = barrier(P)
            with nc.Block("ph3") as blk:
                P.emit(blk)

        P = new_phase()
        phase_wait(P, bar3)
        p5ctx = ExitStack()
        with p5ctx:
            def sb(name, shape, dt=F32):
                return p5ctx.enter_context(nc.sbuf_tensor(name, list(shape), dt))

            s_yg = sb("s_yg", (128, 2 * 8 * CK), F32R)
            s_ysq = sb("s_ysq", (128, 8 * CK), F32R)
            s_h5 = sb("s_h5", (128, 2 * 8 * CK))
            s_hb5 = sb("s_hb5", (128, 8 * CK), BF16)
            s_r1 = sb("s_r1", (1, CK))
            s_mu = sb("s_mu", (1, CK))
            s_m2 = sb("s_m2", (1, CK))
            s_var = sb("s_var", (1, CK))
            s_rstd = sb("s_rstd", (1, CK), F32R)
            s_nmu = sb("s_nmu", (1, CK), F32R)
            s_sc5 = sb("s_sc5", (128, NE))
            s_sc5b = sb("s_sc5b", (128, NE))
            s_gsc = sb("s_gsc", (128, 8))
            s_gtmp = sb("s_gtmp", (128, 8))
            s_gm = sb("s_gm", (128, 1))
            s_msk = sb("s_msk", (128, NE))
            s_mskd = sb("s_mskd", (128, NE))
            s_thr = sb("s_thr", (128, 1))
            s_wsum = sb("s_wsum", (128, 1))
            s_comb = sb("s_comb", (128, NE))
            s_combT = sb("s_combT", (NE, CK), F32R)
            p_s1 = p5ctx.enter_context(nc.psum_tensor("p_s1", [1, CK], F32))
            p_s2 = p5ctx.enter_context(nc.psum_tensor("p_s2", [1, CK], F32))
            p_bc1 = p5ctx.enter_context(nc.psum_tensor("p_bc1", [128, CK], F32))
            p_bc2 = p5ctx.enter_context(nc.psum_tensor("p_bc2", [128, CK], F32))
            p_lg = p5ctx.enter_context(nc.psum_tensor("p_lg", [128, NE], F32))
            p_tr5 = p5ctx.enter_context(nc.psum_tensor("p_tr5", [NE, 128], F32))

            pe_gb = P.op("tensor", lambda e: e.matmul(
                p_bc1[:, 0:NE], s_onr32[:, :], s_gbr[:, :], start=True,
                stop=True), inc=pe)
            gbb_ready = P.op("scalar", lambda e: e.activation(
                s_gbb[:, :], p_bc1[:, 0:NE], AF.Copy),
                waits=[(pe, pe_gb)], inc=act)

            p5 = {"yg_free": [0, 0], "h5_free": [0, 0], "hb_free": 0,
                  "h5_rt": [0, 0],
                  "ps1_free": 0, "ps2_free": 0, "bc1_free": gbb_ready,
                  "bc2_free": 0, "lg_free": 0, "tr_free": 0, "combT_free": 0}
            def emit_routing(cko, buf, d_h):
                    for tt in range(4):
                        for d in range(8):
                            P.op("tensor", lambda e, d=d, tt=tt, buf=buf: e.matmul(
                                p_lg[:, :],
                                s_h5[:, (buf*8 + d)*CK + tt*128:
                                     (buf*8 + d)*CK + (tt+1)*128],
                                s_gw[:, NE*d:NE*(d+1)], start=(d == 0), stop=(d == 7)),
                                waits=([(dve, d_h), (act, p5["lg_free"])]
                                       if d == 0 else ()),
                                inc=pe if d == 7 else None)
                        pe_lg = pe.c
                        a_e = P.op("scalar", lambda e: e.activation(
                            s_sc5[:, :], p_lg[:, :], AF.Exp, scale=-1.0),
                            waits=[(pe, pe_lg), (dve, 0)], inc=act)
                        p5["lg_free"] = a_e
                        P.op("vector", lambda e: e.tensor_scalar(
                            s_sc5[:, :], s_sc5[:, :], 1.0, None, op0=ALU.add),
                            waits=[(act, a_e)], inc=dve)
                        P.op("vector", lambda e: e.reciprocal(
                            s_sc5[:, :], s_sc5[:, :]), inc=dve)
                        P.op("vector", lambda e: e.tensor_tensor(
                            s_sc5b[:, :], s_sc5[:, :], s_gbb[:, :], op=ALU.add),
                            waits=[(act, gbb_ready)], inc=dve)
                        pairs = [(0, 1), (0, 2), (0, 3), (1, 2), (1, 3), (2, 3)]
                        for pi, (i1, i2) in enumerate(pairs):
                            dst = s_gsc if pi == 0 else s_gtmp
                            P.op("vector", lambda e, i1=i1, i2=i2, dst=dst:
                                 e.tensor_tensor(
                                dst[:, :], s_sc5b[:, i1::4], s_sc5b[:, i2::4],
                                op=ALU.add), inc=dve)
                            if pi > 0:
                                P.op("vector", lambda e: e.tensor_tensor(
                                    s_gsc[:, :], s_gsc[:, :], s_gtmp[:, :],
                                    op=ALU.max), inc=dve)
                        P.op("vector", lambda e: e.tensor_copy(
                            s_gtmp[:, :], s_gsc[:, :]), inc=dve)
                        for it in range(4):
                            P.op("vector", lambda e: e.reduce_max(
                                s_gm[:, :], s_gtmp[:, :], axis=AX.X), inc=dve)
                            if it < 3:
                                P.op("vector", lambda e: e.tensor_scalar(
                                    s_msk[:, 0:8], s_gtmp[:, :], s_gm[:, :1], None,
                                    op0=ALU.is_ge), inc=dve)
                                P.op("vector", lambda e: e.scalar_tensor_tensor(
                                    s_gtmp[:, :], s_msk[:, 0:8], -1e30, s_gtmp[:, :],
                                    op0=ALU.mult, op1=ALU.add), inc=dve)
                        P.op("vector", lambda e: e.tensor_scalar(
                            s_gtmp[:, :], s_gsc[:, :], s_gm[:, :1], None,
                            op0=ALU.is_ge), inc=dve)
                        for j in range(4):
                            P.op("vector", lambda e, j=j: e.tensor_copy(
                                s_msk[:, j::4], s_gtmp[:, :]), inc=dve)
                        P.op("vector", lambda e: e.tensor_tensor(
                            s_mskd[:, :], s_sc5b[:, :], s_msk[:, :], op=ALU.mult),
                            inc=dve)
                        P.op("vector", lambda e: e.tensor_copy(
                            s_comb[:, :], s_mskd[:, :]), inc=dve)
                        for it in range(8):
                            P.op("vector", lambda e: e.reduce_max(
                                s_thr[:, :], s_comb[:, :], axis=AX.X), inc=dve)
                            if it < 7:
                                P.op("vector", lambda e: e.tensor_scalar(
                                    s_msk[:, :], s_comb[:, :], s_thr[:, :1], None,
                                    op0=ALU.is_ge), inc=dve)
                                P.op("vector", lambda e: e.scalar_tensor_tensor(
                                    s_comb[:, :], s_msk[:, :], -1e30, s_comb[:, :],
                                    op0=ALU.mult, op1=ALU.add), inc=dve)
                        P.op("vector", lambda e: e.tensor_scalar(
                            s_msk[:, :], s_mskd[:, :], s_thr[:, :1], None,
                            op0=ALU.is_ge), inc=dve)
                        P.op("vector", lambda e: e.tensor_tensor(
                            s_comb[:, :], s_msk[:, :], s_sc5[:, :], op=ALU.mult),
                            inc=dve)
                        P.op("vector", lambda e: e.reduce_sum(
                            s_wsum[:, :], s_comb[:, :], axis=AX.X), inc=dve)
                        P.op("vector", lambda e: e.tensor_scalar(
                            s_wsum[:, :], s_wsum[:, :], 1e-20, None, op0=ALU.add),
                            inc=dve)
                        P.op("vector", lambda e: e.reciprocal(
                            s_wsum[:, :], s_wsum[:, :]), inc=dve)
                        P.op("vector", lambda e: e.tensor_scalar(
                            s_wsum[:, :], s_wsum[:, :], 2.5, None, op0=ALU.mult),
                            inc=dve)
                        d_cb = P.op("vector", lambda e: e.tensor_scalar(
                            s_comb[:, :], s_comb[:, :], s_wsum[:, :1], None,
                            op0=ALU.mult), inc=dve)
                        pe_tr = P.op("tensor", lambda e: e.transpose(
                            p_tr5[:, :], s_comb[:, :], s_id[:, :]),
                            waits=[(dve, d_cb), (act, p5["tr_free"])], inc=pe)
                        p5["tr_free"] = P.op("scalar", lambda e, tt=tt: e.activation(
                            s_combT[:, tt*128:(tt+1)*128], p_tr5[:, :], AF.Copy),
                            waits=[(pe, pe_tr), (scf, p5["combT_free"])], inc=act)

                    a_combT = act.c
                    P.op("scalar", lambda e, cko=cko: e.dma_start(
                        out=cflat_spill.ap()[:, CK*cko:CK*(cko+1)],
                        in_=s_combT[:, :]),
                        waits=[(act, a_combT)], inc=scf, inc_n=16)
                    p5["combT_free"] = scf.c
                    p5["h5_rt"][buf] = dve.c

            rt_pend = []
            for cko in range(CKPC):
                buf = cko % 2
                c_yg = None
                for r in range(8):
                    c_yg = P.op("gpsimd",
                                lambda e, r=r, cko=cko, buf=buf: e.indirect_dma_start(
                        out=s_yg[:, (buf*8 + r)*CK:(buf*8 + r + 1)*CK],
                        out_offset=None,
                        in_=cc_y_out.ap().rearrange(
                            "q p (k f) -> (q p k) f", f=CK),
                        in_offset=IndirectOffsetOnAxis(
                            s_ckidx[:, r*CKPC+cko:r*CKPC+cko+1], 0)),
                        waits=[(cc, cc_y_done)] +
                              ([(dve, p5["yg_free"][buf])] if r == 0 else []),
                        inc=lyg[buf], inc_n=16)
                a_sq = None
                for d in range(8):
                    a_sq = P.op("scalar", lambda e, d=d, buf=buf: e.activation(
                        s_ysq[:, d*CK:(d+1)*CK],
                        s_yg[:, (buf*8 + d)*CK:(buf*8 + d + 1)*CK].bitcast(F32),
                        AF.Square),
                        waits=[(lyg[buf], c_yg)] if d == 0 else (), inc=act)
                for d in range(8):
                    P.op("tensor", lambda e, d=d, buf=buf: e.matmul(
                        p_s1[:, :], s_onc[:, :],
                        s_yg[:, (buf*8 + d)*CK:(buf*8 + d + 1)*CK],
                        start=(d == 0), stop=(d == 7)),
                        waits=([(lyg[buf], c_yg), (dve, p5["ps1_free"])]
                               if d == 0 else ()),
                        inc=pe if d == 7 else None)
                pe_s1 = pe.c
                for d in range(8):
                    P.op("tensor", lambda e, d=d: e.matmul(
                        p_s2[:, :], s_onc[:, :], s_ysq[:, d*CK:(d+1)*CK],
                        start=(d == 0), stop=(d == 7)),
                        waits=([(act, a_sq), (dve, p5["ps2_free"])]
                               if d == 0 else ()),
                        inc=pe if d == 7 else None)
                pe_s2 = pe.c
                d_mu = P.op("vector", lambda e: e.tensor_scalar(
                    s_mu[:, :], p_s1[0:1, :], 1.0 / DM, None, op0=ALU.mult),
                    waits=[(pe, pe_s1)], inc=dve)
                p5["ps1_free"] = d_mu
                P.op("vector", lambda e: e.tensor_tensor(
                    s_m2[:, :], s_mu[:, :], s_mu[:, :], op=ALU.mult), inc=dve)
                d_var = P.op("vector", lambda e: e.scalar_tensor_tensor(
                    s_var[:, :], p_s2[0:1, :], 1.0 / DM, s_m2[:, :],
                    op0=ALU.mult, op1=ALU.subtract),
                    waits=[(pe, pe_s2)], inc=dve)
                p5["ps2_free"] = d_var
                P.op("scalar", lambda e: e.activation(
                    s_r1[:, :], s_var[:, :], AF.Ln, bias=1e-5),
                    waits=[(dve, d_var)], inc=act)
                a_rstd = P.op("scalar", lambda e: e.activation(
                    s_rstd[:, :], s_r1[:, :], AF.Exp, scale=-0.5), inc=act)
                d_nmu = P.op("vector", lambda e: e.scalar_tensor_tensor(
                    s_nmu[:, :], s_mu[:, :], -1.0, s_rstd[:, :].bitcast(F32),
                    op0=ALU.mult, op1=ALU.mult),
                    waits=[(act, a_rstd)], inc=dve)
                pe_b1 = P.op("tensor", lambda e: e.matmul(
                    p_bc1[:, :], s_onr[:, :], s_rstd[:, :], start=True, stop=True),
                    waits=[(act, a_rstd), (act, p5["bc1_free"] if cko == 0 else 0),
                           (dve, p5["bc1_free"] if cko > 0 else 0)], inc=pe)
                pe_b2 = P.op("tensor", lambda e: e.matmul(
                    p_bc2[:, :], s_onr[:, :], s_nmu[:, :], start=True, stop=True),
                    waits=[(dve, d_nmu), (dve, p5["bc2_free"])], inc=pe)
                d_h = None
                for d in range(8):
                    P.op("vector", lambda e, d=d, buf=buf: e.tensor_tensor(
                        s_h5[:, (buf*8 + d)*CK:(buf*8 + d + 1)*CK],
                        s_yg[:, (buf*8 + d)*CK:(buf*8 + d + 1)*CK].bitcast(F32),
                        p_bc1[:, :], op=ALU.mult),
                        waits=[(pe, pe_b1)] +
                              ([(ssp[buf], p5["h5_free"][buf]),
                                (dve, p5["h5_rt"][buf])]
                               if d == 0 else []), inc=dve)
                    d_h = P.op("vector", lambda e, d=d, buf=buf: e.tensor_tensor(
                        s_h5[:, (buf*8 + d)*CK:(buf*8 + d + 1)*CK],
                        s_h5[:, (buf*8 + d)*CK:(buf*8 + d + 1)*CK],
                        p_bc2[:, :], op=ALU.add),
                        waits=[(pe, pe_b2)] if d == 0 else (), inc=dve)
                p5["yg_free"][buf] = d_h
                p5["bc1_free"] = d_h
                p5["bc2_free"] = d_h
                d_hb = None
                for d in range(8):
                    d_hb = P.op("vector", lambda e, d=d, buf=buf: e.tensor_copy(
                        s_hb5[:, d*CK:(d+1)*CK],
                        s_h5[:, (buf*8 + d)*CK:(buf*8 + d + 1)*CK]),
                        waits=[(shb, p5["hb_free"])] if d == 0 else (),
                        inc=dve)
                sp1 = P.op("scalar", lambda e, cko=cko, buf=buf: e.dma_start(
                    out=xres_spill.ap()[:, CK*cko:CK*(cko+1)].rearrange(
                        "(dt p) f -> p dt f", p=128),
                    in_=s_h5[:, buf*8*CK:(buf*8 + 8)*CK]),
                    waits=[(dve, d_h)], inc=ssp[buf], inc_n=16)
                P.op("scalar", lambda e, cko=cko: e.dma_start(
                    out=hb_spill.ap()[:, CK*cko:CK*(cko+1)].rearrange(
                        "(dt p) f -> p dt f", p=128),
                    in_=s_hb5[:, :]),
                    waits=[(dve, d_hb)], inc=shb, inc_n=16)
                p5["hb_free"] = shb.c
                p5["h5_free"][buf] = ssp[buf].c

                rt_pend.append((cko, buf, d_h))
                if rt_pend and len(rt_pend) > 1:
                    emit_routing(*rt_pend.pop(0))

            while rt_pend:
                emit_routing(*rt_pend.pop(0))
            cflat_ready = scf.c
            hb_all = shb.c
            xres_all = [ssp[0].c, ssp[1].c]
            bar5 = barrier(P)
            with nc.Block("ph5") as blk:
                P.emit(blk)

        P = new_phase()
        phase_wait(P, bar5)
        p6ctx = ExitStack()
        with p6ctx:
            def sb(name, shape, dt=F32):
                return p6ctx.enter_context(nc.sbuf_tensor(name, list(shape), dt))

            s_wg = sb("s_wg", (128, NSLOT * 8 * EI), BF16)
            s_wu = sb("s_wu", (128, NSLOT * 8 * EI), BF16)
            s_wd = sb("s_wd", (128, NSLOT * 4 * DM), BF16)
            s_hb6 = sb("s_hb6", (128, 8 * TB), BF16)
            s_routed = sb("s_routed", (128, 8 * TB))
            s_cbc = sb("s_cbc", (128, EG * TB))
            s_crow = sb("s_crow", (1, EG * TB), F32R)
            s_sg6 = sb("s_sg6", (128, 4 * SUB))
            s_x6 = sb("s_x6", (128, 2 * 4 * SUB), BF16)
            s_xr6 = sb("s_xr6", (128, 2 * TB))
            pg = [p6ctx.enter_context(nc.psum_tensor(f"pg{i}", [128, 512], F32))
                  for i in range(2)]
            pu = [p6ctx.enter_context(nc.psum_tensor(f"pu{i}", [128, 512], F32))
                  for i in range(2)]
            pr6 = [p6ctx.enter_context(nc.psum_tensor(f"pr{i}", [128, 512], F32))
                   for i in range(4)]

            def wslot(base, slot, n):
                return base[:, slot * n:(slot + 1) * n]

            p6 = {"crow_free": [0, 0],
                  "slot_free": [0] * NSLOT, "slot_ready": [0] * NSLOT,
                  "cbc_free": 0, "sg_free": 0, "x_free": [0, 0],
                  "routed_free": 0, "pr_free": 0, "pg_free": [0, 0],
                  "pu_free": [0, 0], "hb_free": 0, "xr_free": [0, 0]}

            for tb in range(2):
                c_hb = P.op("sync", lambda e, tb=tb: e.dma_start(
                    out=s_hb6[:, :],
                    in_=hb_spill.ap()[:, TB*tb:TB*(tb+1)].rearrange(
                        "(kd p) t -> p kd t", p=128)),
                    waits=[(shb, hb_all), (pe, p6["hb_free"])],
                    inc=lhb, inc_n=16)
                d_ms = None
                for ht in range(8):
                    d_ms = P.op("vector", lambda e, ht=ht: e.memset(
                        s_routed[:, ht*TB:(ht+1)*TB], 0.0),
                        waits=[(sp2, p6["routed_free"])] if ht == 0 else (),
                        inc=dve)

                pend_dn = []

                def emit_down(Pp, item):
                    (j_, sub_, sl_, xslot_, d_x_, need_ms) = item
                    w_extra = []
                    if j_ == 0:
                        w_extra.append((dve, p6["pr_free"]))
                        if need_ms:
                            w_extra.append((dve, d_ms))
                    for ht in range(8):
                        for ki in range(4):
                            w = []
                            if ht == 0 and ki == 0:
                                w = [(dve, d_x_)] + w_extra
                            Pp.op("tensor",
                                  lambda e, ht=ht, ki=ki, sl=sl_, j=j_,
                                  xs=xslot_: e.matmul(
                                pr6[ht // 2][:, (ht % 2)*SUB:
                                             (ht % 2 + 1)*SUB],
                                wslot(s_wd, sl, 4*DM)[
                                    :, ki*DM + ht*128:
                                    ki*DM + (ht+1)*128],
                                s_x6[:, xs*TB + ki*SUB:
                                     xs*TB + (ki+1)*SUB],
                                start=(ki == 0 and j == 0
                                       and ht % 2 == 0),
                                stop=(ki == 3 and j == EG - 1
                                      and ht % 2 == 1)),
                                waits=w,
                                inc=pe if (ki == 3 and ht == 7)
                                else None)
                    dn_done = pe.c
                    p6["x_free"][xslot_] = dn_done
                    p6["slot_free"][sl_] = dn_done
                    if j_ == EG - 1:
                        for q in range(4):
                            P.op("vector", lambda e, q=q, sub=sub_:
                                 e.tensor_tensor(
                                s_routed[:, (2*q)*TB + sub*SUB:
                                         (2*q)*TB + sub*SUB + SUB],
                                pr6[q][:, 0:SUB],
                                s_routed[:, (2*q)*TB + sub*SUB:
                                         (2*q)*TB + sub*SUB + SUB],
                                op=ALU.add),
                                waits=[(pe, dn_done)], inc=dve)
                            P.op("vector", lambda e, q=q, sub=sub_:
                                 e.tensor_tensor(
                                s_routed[:, (2*q+1)*TB + sub*SUB:
                                         (2*q+1)*TB + sub*SUB + SUB],
                                pr6[q][:, SUB:2*SUB],
                                s_routed[:, (2*q+1)*TB + sub*SUB:
                                         (2*q+1)*TB + sub*SUB + SUB],
                                op=ALU.add), inc=dve)
                        p6["pr_free"] = dve.c


                for eg_i in range(NEX // EG):
                    egs = [eg_i * EG + j for j in range(EG)]
                    slots = [ei % NSLOT for ei in egs]
                    for ei, sl in zip(egs, slots):
                        P.op("gpsimd", lambda e, ei=ei, sl=sl: e.dma_start(
                            out=wslot(s_wg, sl, 8 * EI),
                            in_=wgT.ap()[ei].rearrange("(k p) i -> p k i",
                                                       p=128)),
                            waits=[(pe, p6["slot_free"][sl])],
                            inc=sw[sl], inc_n=16)
                        P.op("gpsimd", lambda e, ei=ei, sl=sl: e.dma_start(
                            out=wslot(s_wu, sl, 8 * EI),
                            in_=wuT.ap()[ei].rearrange("(k p) i -> p k i",
                                                       p=128)),
                            inc=sw[sl], inc_n=16)
                        P.op("gpsimd", lambda e, ei=ei, sl=sl: e.dma_start(
                            out=wslot(s_wd, sl, 4 * DM),
                            in_=wdT.ap()[ei].rearrange("(k p) h -> p k h",
                                                       p=128)),
                            inc=sw[sl], inc_n=16)
                        p6["slot_ready"][sl] = sw[sl].c
                    cbc_dve = 0
                    cbc_act = 0
                    for j, ei in enumerate(egs):
                        if ei < NE:
                            c_cr = P.op("scalar", lambda e, ei=ei, tb=tb, j=j:
                                        e.dma_start(
                                out=s_crow[:, j*TB:(j+1)*TB],
                                in_=cflat_spill.ap()[ei:ei+1, TB*tb:TB*(tb+1)]),
                                waits=[(scf, cflat_ready),
                                       (pe, p6["crow_free"][j])],
                                inc=scr[j], inc_n=16)
                            pe_o1 = P.op("tensor", lambda e, j=j: e.matmul(
                                pg[0][:, :], s_onr[:, :],
                                s_crow[:, j*TB:j*TB + 512],
                                start=True, stop=True),
                                waits=[(scr[j], c_cr),
                                       (act, p6["pg_free"][0]),
                                       (dve, p6["cbc_free"])], inc=pe)
                            pe_o2 = P.op("tensor", lambda e, j=j: e.matmul(
                                pg[1][:, :], s_onr[:, :],
                                s_crow[:, j*TB + 512:(j+1)*TB],
                                start=True, stop=True),
                                waits=[(act, p6["pg_free"][1])], inc=pe)
                            p6["crow_free"][j] = pe_o2
                            a1 = P.op("scalar", lambda e, j=j: e.activation(
                                s_cbc[:, j*TB:j*TB + 512], pg[0][:, :], AF.Copy),
                                waits=[(pe, pe_o1)], inc=act)
                            a2 = P.op("scalar", lambda e, j=j: e.activation(
                                s_cbc[:, j*TB + 512:(j+1)*TB], pg[1][:, :],
                                AF.Copy),
                                waits=[(pe, pe_o2)], inc=act)
                            p6["pg_free"][0] = a1
                            p6["pg_free"][1] = a2
                            cbc_act = a2
                        else:
                            cbc_dve = P.op("vector", lambda e, j=j: e.memset(
                                s_cbc[:, j*TB:(j+1)*TB], 1.0),
                                waits=[(dve, p6["cbc_free"])], inc=dve)

                    for sub in range(NSUB):
                        for j, ei in enumerate(egs):
                            sl = slots[j]
                            xslot = (sub * EG + j) % 2
                            for it in range(4):
                                for kd in range(8):
                                    w = []
                                    if it == 0 and kd == 0:
                                        w = [(sw[sl], p6["slot_ready"][sl]),
                                             (lhb, c_hb),
                                             (act, p6["pg_free"][0]),
                                             (act, p6["pg_free"][1])]
                                        if cbc_act:
                                            w.append((act, cbc_act))
                                        if cbc_dve:
                                            w.append((dve, cbc_dve))
                                    P.op("tensor",
                                         lambda e, it=it, kd=kd, sl=sl, sub=sub:
                                         e.matmul(
                                        pg[it // 2][:, (it % 2)*SUB:
                                                    (it % 2 + 1)*SUB],
                                        wslot(s_wg, sl, 8*EI)[
                                            :, kd*EI + it*128: kd*EI + (it+1)*128],
                                        s_hb6[:, kd*TB + sub*SUB:
                                              kd*TB + (sub+1)*SUB],
                                        start=(kd == 0 and it % 2 == 0),
                                        stop=(kd == 7 and it % 2 == 1)),
                                        waits=w,
                                        inc=pe if (kd == 7 and it % 2 == 1)
                                        else None)
                            pe_g = pe.c
                            for it in range(4):
                                for kd in range(8):
                                    w = []
                                    if it == 0 and kd == 0:
                                        w = [(dve, p6["pu_free"][0]),
                                             (dve, p6["pu_free"][1])]
                                    P.op("tensor",
                                         lambda e, it=it, kd=kd, sl=sl, sub=sub:
                                         e.matmul(
                                        pu[it // 2][:, (it % 2)*SUB:
                                                    (it % 2 + 1)*SUB],
                                        wslot(s_wu, sl, 8*EI)[
                                            :, kd*EI + it*128: kd*EI + (it+1)*128],
                                        s_hb6[:, kd*TB + sub*SUB:
                                              kd*TB + (sub+1)*SUB],
                                        start=(kd == 0 and it % 2 == 0),
                                        stop=(kd == 7 and it % 2 == 1)),
                                        waits=w,
                                        inc=pe if (kd == 7 and it % 2 == 1)
                                        else None)
                            pe_u = pe.c
                            if pend_dn:
                                emit_down(P, pend_dn.pop(0))
                            a_sg1 = P.op("scalar", lambda e: e.activation(
                                s_sg6[:, 0:512], pg[0][:, :], AF.Silu),
                                waits=[(pe, pe_g), (dve, p6["sg_free"])],
                                inc=act)
                            a_sg2 = P.op("scalar", lambda e: e.activation(
                                s_sg6[:, 512:1024], pg[1][:, :], AF.Silu),
                                inc=act)
                            p6["pg_free"][0] = a_sg1
                            p6["pg_free"][1] = a_sg2
                            d_t1 = P.op("vector", lambda e: e.tensor_tensor(
                                s_sg6[:, 0:512], pu[0][:, :], s_sg6[:, 0:512],
                                op=ALU.mult),
                                waits=[(pe, pe_u), (act, a_sg1)], inc=dve)
                            d_t2 = P.op("vector", lambda e: e.tensor_tensor(
                                s_sg6[:, 512:1024], pu[1][:, :],
                                s_sg6[:, 512:1024], op=ALU.mult),
                                waits=[(act, a_sg2)], inc=dve)
                            p6["pu_free"][0] = d_t1
                            p6["pu_free"][1] = d_t2
                            d_x = None
                            for it in range(4):
                                d_x = P.op("vector",
                                           lambda e, it=it, j=j, sub=sub,
                                           xs=xslot: e.tensor_tensor(
                                    s_x6[:, xs*TB + it*SUB:
                                         xs*TB + (it+1)*SUB],
                                    s_sg6[:, it*SUB:(it+1)*SUB],
                                    s_cbc[:, j*TB + sub*SUB:
                                          j*TB + sub*SUB + SUB],
                                    op=ALU.mult),
                                    waits=([(dve, d_t2),
                                            (pe, p6["x_free"][xslot])]
                                           if it == 0 else ()), inc=dve)
                            p6["sg_free"] = d_x
                            pend_dn.append((j, sub, sl, xslot, d_x,
                                            eg_i == 0 and sub == 0))
                    p6["cbc_free"] = dve.c
                if True:
                    while pend_dn:
                        emit_down(P, pend_dn.pop(0))

                for ht in range(8):
                    hbuf = ht % 2
                    c_xr = P.op("sync", lambda e, ht=ht, tb=tb, hbuf=hbuf:
                                e.dma_start(
                        out=s_xr6[:, hbuf*TB:(hbuf+1)*TB],
                        in_=xres_spill.ap()[128*ht:128*(ht+1), TB*tb:TB*(tb+1)]),
                        waits=[(dve, p6["xr_free"][hbuf]),
                               (ssp[0], xres_all[0]), (ssp[1], xres_all[1])],
                        inc=lxr[hbuf], inc_n=16)
                    d_p2 = P.op("vector", lambda e, ht=ht, hbuf=hbuf:
                                e.tensor_tensor(
                        s_routed[:, ht*TB:(ht+1)*TB],
                        s_routed[:, ht*TB:(ht+1)*TB],
                        s_xr6[:, hbuf*TB:(hbuf+1)*TB], op=ALU.add),
                        waits=[(lxr[hbuf], c_xr)], inc=dve)
                    p6["xr_free"][hbuf] = d_p2
                    P.op("scalar", lambda e, ht=ht, tb=tb: e.dma_start(
                        out=pre2_spill.ap()[128*ht:128*(ht+1), TB*tb:TB*(tb+1)],
                        in_=s_routed[:, ht*TB:(ht+1)*TB]),
                        waits=[(dve, d_p2)], inc=sp2, inc_n=16)
                p6["routed_free"] = sp2.c
                p6["hb_free"] = pe.c

            pre2_done = sp2.c
            bar6 = barrier(P)
            with nc.Block("ph6") as blk:
                P.emit(blk)

        P = new_phase()
        phase_wait(P, bar6)
        p7ctx = ExitStack()
        with p7ctx:
            def sb(name, shape, dt=F32):
                return p7ctx.enter_context(nc.sbuf_tensor(name, list(shape), dt))

            s_p7 = sb("s_p7", (128, 2 * 8 * CK))
            s_q7 = sb("s_q7", (128, 8 * CK))
            s_out7 = sb("s_out7", (128, 2 * DM))
            s_rs7 = sb("s_rs7", (1, CK))
            s_nm7 = sb("s_nm7", (1, CK))
            s_sclr = sb("s_sclr", (128, 1))
            s_sclm = sb("s_sclm", (128, 1))
            s_r17 = sb("s_r17", (1, CK))
            s_mu7 = sb("s_mu7", (1, CK))
            s_m27 = sb("s_m27", (1, CK))
            s_v7 = sb("s_v7", (1, CK))
            p_t7 = [p7ctx.enter_context(nc.psum_tensor(f"p_t7{i}",
                                                       [128, 128], F32))
                    for i in range(2)]
            p_s17 = p7ctx.enter_context(nc.psum_tensor("p_s17", [1, CK], F32))
            p_s27 = p7ctx.enter_context(nc.psum_tensor("p_s27", [1, CK], F32))
            p_sc7 = p7ctx.enter_context(nc.psum_tensor("p_sc7", [128, 2], F32))

            p7 = {"p7_free": [0, 0], "out_free": [0, 0], "q_free": 0,
                  "tr_free": [0, 0], "scl_free": 0, "ps1_free": 0,
                  "ps2_free": 0}
            for cko in range(CKPC):
                buf = cko % 2
                c_p2 = P.op("sync", lambda e, cko=cko, buf=buf: e.dma_start(
                    out=s_p7[:, buf*8*CK:(buf + 1)*8*CK],
                    in_=pre2_spill.ap()[:, CK*cko:CK*(cko+1)].rearrange(
                        "(dt p) t -> p dt t", p=128)),
                    waits=[(sp2, pre2_done), (pe, p7["p7_free"][buf])],
                    inc=lp2[buf], inc_n=16)
                a_sq = None
                for d in range(8):
                    a_sq = P.op("scalar", lambda e, d=d, buf=buf: e.activation(
                        s_q7[:, d*CK:(d+1)*CK],
                        s_p7[:, (buf*8 + d)*CK:(buf*8 + d+1)*CK], AF.Square),
                        waits=([(lp2[buf], c_p2), (pe, p7["q_free"])]
                               if d == 0 else ()), inc=act)
                for d in range(8):
                    P.op("tensor", lambda e, d=d, buf=buf: e.matmul(
                        p_s17[:, :], s_onc32[:, :],
                        s_p7[:, (buf*8 + d)*CK:(buf*8 + d+1)*CK],
                        start=(d == 0), stop=(d == 7)),
                        waits=([(lp2[buf], c_p2), (dve, p7["ps1_free"])]
                               if d == 0 else ()),
                        inc=pe if d == 7 else None)
                pe_s1 = pe.c
                for d in range(8):
                    P.op("tensor", lambda e, d=d: e.matmul(
                        p_s27[:, :], s_onc32[:, :], s_q7[:, d*CK:(d+1)*CK],
                        start=(d == 0), stop=(d == 7)),
                        waits=([(act, a_sq), (dve, p7["ps2_free"])]
                               if d == 0 else ()),
                        inc=pe if d == 7 else None)
                pe_s2 = pe.c
                p7["q_free"] = pe_s2
                d_mu = P.op("vector", lambda e: e.tensor_scalar(
                    s_mu7[:, :], p_s17[0:1, :], 1.0 / DM, None, op0=ALU.mult),
                    waits=[(pe, pe_s1)], inc=dve)
                p7["ps1_free"] = d_mu
                P.op("vector", lambda e: e.tensor_tensor(
                    s_m27[:, :], s_mu7[:, :], s_mu7[:, :], op=ALU.mult), inc=dve)
                d_v = P.op("vector", lambda e: e.scalar_tensor_tensor(
                    s_v7[:, :], p_s27[0:1, :], 1.0 / DM, s_m27[:, :],
                    op0=ALU.mult, op1=ALU.subtract),
                    waits=[(pe, pe_s2)], inc=dve)
                p7["ps2_free"] = d_v
                P.op("scalar", lambda e: e.activation(
                    s_r17[:, :], s_v7[:, :], AF.Ln, bias=1e-5),
                    waits=[(dve, d_v)], inc=act)
                a_rs = P.op("scalar", lambda e: e.activation(
                    s_rs7[:, :], s_r17[:, :], AF.Exp, scale=-0.5), inc=act)
                d_nm = P.op("vector", lambda e: e.scalar_tensor_tensor(
                    s_nm7[:, :], s_mu7[:, :], -1.0, s_rs7[:, :],
                    op0=ALU.mult, op1=ALU.mult),
                    waits=[(act, a_rs)], inc=dve)
                for tt in range(4):
                    obuf = tt % 2
                    pe_sc1 = P.op("tensor", lambda e, tt=tt: e.transpose(
                        p_sc7[:, 0:1], s_rs7[:, tt*128:(tt+1)*128], s_id[0:1, 0:1]),
                        waits=[(act, a_rs), (dve, d_nm),
                               (dve, p7["scl_free"])], inc=pe)
                    pe_sc2 = P.op("tensor", lambda e, tt=tt: e.transpose(
                        p_sc7[:, 1:2], s_nm7[:, tt*128:(tt+1)*128], s_id[0:1, 0:1]),
                        inc=pe)
                    a_sc1 = P.op("scalar", lambda e: e.activation(
                        s_sclr[:, :], p_sc7[:, 0:1], AF.Copy),
                        waits=[(pe, pe_sc2)], inc=act)
                    a_scl = P.op("scalar", lambda e: e.activation(
                        s_sclm[:, :], p_sc7[:, 1:2], AF.Copy), inc=act)
                    p7["scl_free"] = a_scl
                    for d in range(8):
                        dbuf = d % 2
                        pe_tr = P.op("tensor",
                                     lambda e, d=d, tt=tt, buf=buf, dbuf=dbuf:
                                     e.transpose(
                            p_t7[dbuf][:, :],
                            s_p7[:, (buf*8 + d)*CK + tt*128:
                                 (buf*8 + d)*CK + (tt+1)*128],
                            s_id[:, :]),
                            waits=[(dve, p7["tr_free"][dbuf])], inc=pe)
                        a_o = P.op("vector",
                                   lambda e, d=d, dbuf=dbuf, obuf=obuf:
                                   e.tensor_scalar(
                            s_out7[:, obuf*DM + d*128: obuf*DM + (d+1)*128],
                            p_t7[dbuf][:, :], s_sclr[:, 0:1], s_sclm[:, 0:1],
                            op0=ALU.mult, op1=ALU.add),
                            waits=[(pe, pe_tr), (act, a_scl)] +
                                  ([(so[obuf], p7["out_free"][obuf])]
                                   if d == 0 else []),
                            inc=dve)
                        p7["tr_free"][dbuf] = a_o
                    p7["scl_free"] = dve.c
                    if tt == 3:
                        p7["p7_free"][buf] = pe.c
                    a_last = dve.c
                    P.op("sync", lambda e, cko=cko, tt=tt, obuf=obuf:
                         e.dma_start(
                        out=out_h.ap()[cko*CK + tt*128: cko*CK + (tt+1)*128, :],
                        in_=s_out7[:, obuf*DM:(obuf+1)*DM]),
                        waits=[(dve, a_last)], inc=so[obuf], inc_n=16)
                    p7["out_free"][obuf] = so[obuf].c

            P.op("sync", lambda e: None,
                 waits=[(so[0], so[0].c), (so[1], so[1].c)])
            with nc.Block("ph7") as blk:
                P.emit(blk)

    return nc


def kernel(**inputs):
    f = lambda k: np.asarray(inputs[k], np.float32)
    u = np.ascontiguousarray(f("L0_attn_u").reshape(T, DM))
    conv_w = f("L0_attn_conv_w")[:, 0, :]
    conv_b = f("L0_attn_conv_b").reshape(DM, 1)
    x_proj_w = f("L0_attn_x_proj_w")
    dt_proj_w = f("L0_attn_dt_proj_w")
    dt_proj_b = f("L0_attn_dt_proj_b")
    A_log = f("L0_attn_A_log")
    D_skip = f("L0_attn_D_skip")
    gate_w = f("L0_ffn_gate_w")
    gate_bias = f("L0_ffn_gate_bias")
    w_gate = f("L0_ffn_w_gate")
    w_up = f("L0_ffn_w_up")
    w_down = f("L0_ffn_w_down")
    sh_gate = f("L0_ffn_sh_gate")
    sh_up = f("L0_ffn_sh_up")
    sh_down = f("L0_ffn_sh_down")

    wgT = np.empty((NEX, DM, EI), np.float32)
    wuT = np.empty((NEX, DM, EI), np.float32)
    wdT = np.empty((NEX, EI, DM), np.float32)
    wgT[:NE] = w_gate.transpose(0, 2, 1)
    wuT[:NE] = w_up.transpose(0, 2, 1)
    wdT[:NE] = w_down.transpose(0, 2, 1)
    shgT = sh_gate.T
    shuT = sh_up.T
    shdT = sh_down.T
    wgT[NE] = shgT[:, :EI]; wgT[NE+1] = shgT[:, EI:]
    wuT[NE] = shuT[:, :EI]; wuT[NE+1] = shuT[:, EI:]
    wdT[NE] = shdT[:EI, :]; wdT[NE+1] = shdT[EI:, :]
    wgT = np.ascontiguousarray(wgT)
    wuT = np.ascontiguousarray(wuT)
    wdT = np.ascontiguousarray(wdT)

    selrep = np.zeros((16, 128, 128), np.float32)
    selcon = np.zeros((16, 128, 128), np.float32)
    for g in range(16):
        for n in range(16):
            for j in range(8):
                selrep[g, 8 * g + j, n * 8 + j] = 1.0
                selcon[g, n * 8 + j, 8 * g + j] = 1.0
    selbc = np.zeros((NST, 128), np.float32)
    for n in range(16):
        selbc[n, n * 8:(n + 1) * 8] = 1.0

    shared = {
        "xpwT": np.ascontiguousarray(x_proj_w.T),
        "cw": np.ascontiguousarray(conv_w),
        "cb": conv_b,
        "gatewT": np.ascontiguousarray(gate_w.T),
        "gateb": gate_bias.reshape(1, NE),
        "selrep": selrep, "selcon": selcon, "selbc": selbc,
        "ones_row": np.ones((1, 128), np.float32),
        "ones_col": np.ones((128, 1), np.float32),
        "ident": np.eye(128, dtype=np.float32),
        "wgT": wgT, "wuT": wuT, "wdT": wdT,
    }

    in_maps = []
    for c in range(NCORE):
        t0 = TPC * c
        halo = (np.zeros((3, DM), np.float32) if (t0 % LQ) == 0
                else u[t0-3:t0])
        u_tokT_a = np.ascontiguousarray(
            np.concatenate([halo, u[t0:t0+TPC]], axis=0).T)
        ch0 = CPC * c
        u_chanT_a = np.ascontiguousarray(u[:, ch0:ch0+CPC].T)
        pvec = np.arange(128)
        gvec = np.arange(16)
        alog_a = A_log[ch0 + 8 * gvec[None, :] + (pvec[:, None] % 8),
                       pvec[:, None] // 8].astype(np.float32)
        rr = np.arange(8)
        jj = np.arange(CKPC)
        kk = CKPC * c + jj
        qq = kk // 8
        kin = kk % 8
        idx = ((qq[None, None, :] * 1024 + 128 * rr[None, :, None]
                + pvec[:, None, None]) * 8 + kin[None, None, :]
               ).reshape(128, 8 * CKPC).astype(np.int32)
        m = dict(shared)
        m.update({
            "u_tokT": u_tokT_a,
            "u_chanT": u_chanT_a,
            "dtwT": np.ascontiguousarray(dt_proj_w[ch0:ch0+CPC].T),
            "dtb": dt_proj_b[ch0:ch0+CPC].reshape(CPC, 1),
            "alog": np.ascontiguousarray(alog_a),
            "dcol": D_skip[ch0:ch0+CPC].reshape(CPC, 1),
            "ckidx": np.ascontiguousarray(idx),
        })
        in_maps.append(m)

    nc = bass.Bass("TRN2", target_bir_lowering=False, debug=False)
    nc = _build(nc)
    res = run_bass_kernel_spmd(nc, in_maps, core_ids=list(range(NCORE)))
    out = np.concatenate([np.asarray(res.results[c]["out"], np.float32)
                          for c in range(NCORE)], axis=0)
    return out.reshape(B, LQ, DM)


# version 11
# speedup vs baseline: 1.3128x; 1.0477x over previous
import numpy as np
from contextlib import ExitStack

import concourse.bass as bass
import concourse.mybir as mybir
from concourse.bass_utils import run_bass_kernel_spmd
from concourse.bass_core_rust import IndirectOffsetOnAxis

F32 = mybir.dt.float32
F32R = mybir.dt.float32r
BF16 = mybir.dt.bfloat16
I32 = mybir.dt.int32
AF = mybir.ActivationFunctionType
ALU = mybir.AluOpType
AX = mybir.AxisListType

NCORE = 8
DM = 1024
NST = 16
DTR = 64
B = 2
LQ = 8192
T = B * LQ
TPC = T // NCORE
CPC = DM // NCORE
CK = 512
NCK = T // CK
CKPC = NCK // NCORE
NE = 32
EI = 512
NEX = NE + 2
TB = 1024
SUB = 256
NSUB = TB // SUB
EG = 2
NSLOT = 4

ENGINES = ("sync", "tensor", "vector", "scalar", "gpsimd")


class Sem:
    def __init__(self, ctx, nc, name):
        self.h = ctx.enter_context(nc.semaphore(name))
        self.c = 0


class Plan:
    def __init__(self, self_sems=None):
        self.ops = {e: [] for e in ENGINES}
        self.self_sems = self_sems or {}

    def op(self, eng, emit, waits=(), inc=None, inc_n=1):
        w = [(s.h, v) for (s, v) in waits if v > 0]
        ss = self.self_sems.get(eng)
        if ss is not None and ss.c > 0:
            w.append((ss.h, ss.c))
        if inc is not None:
            inc.c += inc_n
            self.ops[eng].append((w, emit, inc.h, inc_n))
            return inc.c
        self.ops[eng].append((w, emit, None, 0))
        return None

    WMARKS = {e: {} for e in ENGINES}

    def emit(self, block):
        for eng_name in ENGINES:
            lst = self.ops[eng_name]
            if not lst:
                continue

            def body(eng, lst=lst, eng_name=eng_name):
                wm = Plan.WMARKS[eng_name]
                for waits, emit, sem_h, n in lst:
                    for (h, v) in waits:
                        if wm.get(id(h), -1) >= v:
                            continue
                        eng.wait_ge(h, v)
                        wm[id(h)] = v
                    ins = emit(eng)
                    if sem_h is not None:
                        if ins is None:
                            eng.sem_inc(sem_h, n)
                        else:
                            ins.then_inc(sem_h, n)
            getattr(block, eng_name)(body)


def _build(nc):
    dram = {}

    def din(name, shape, dt=F32):
        dram[name] = nc.dram_tensor(name, list(shape), dt, kind="ExternalInput")
        return dram[name]

    u_tokT = din("u_tokT", (DM, TPC + 3), F32R)
    u_chanT = din("u_chanT", (CPC, T), F32R)
    dtwT = din("dtwT", (DTR, CPC), F32R)
    dtb = din("dtb", (CPC, 1), F32)
    alog = din("alog", (128, 16), F32)
    dcol = din("dcol", (CPC, 1), F32)
    xpwT = din("xpwT", (DM, 96), F32)
    cw = din("cw", (DM, 4), F32)
    cb = din("cb", (DM, 1), F32)
    gatewT = din("gatewT", (DM, NE), F32)
    gateb = din("gateb", (1, NE), F32)
    selrep = din("selrep", (16, 128, 128), F32R)
    selcon = din("selcon", (16, 128, 128), F32R)
    selbc = din("selbc", (NST, 128), F32R)
    ones_row = din("ones_row", (1, 128), F32R)
    ones_col = din("ones_col", (128, 1), F32R)
    ident = din("ident", (128, 128), F32)
    ckidx = din("ckidx", (128, 8 * CKPC), I32)
    wgT = din("wgT", (NEX, DM, EI), F32)
    wuT = din("wuT", (NEX, DM, EI), F32)
    wdT = din("wdT", (NEX, EI, DM), F32)
    out_h = nc.dram_tensor("out", [TPC, DM], F32, kind="ExternalOutput")

    cc_prep_in = nc.dram_tensor("cc_prep_in", [96, TPC], F32R, kind="Internal")
    cc_prep_out = nc.dram_tensor("cc_prep_out", [96 * NCORE, TPC], F32R,
                                 kind="Internal")
    cc_y_in = nc.dram_tensor("cc_y_in", [4, CPC, 4096], F32R, kind="Internal")
    cc_y_out = nc.dram_tensor("cc_y_out", [4, DM, 4096], F32R, kind="Internal")
    xres_spill = nc.dram_tensor("xres_spill", [DM, TPC], F32, kind="Internal")
    hb_spill = nc.dram_tensor("hb_spill", [DM, TPC], BF16, kind="Internal")
    pre2_spill = nc.dram_tensor("pre2_spill", [DM, TPC], F32, kind="Internal")
    cflat_spill = nc.dram_tensor("cflat_spill", [NE, TPC], F32R, kind="Internal")

    octx = ExitStack()
    with octx:
        def osb(name, shape, dt=F32):
            return octx.enter_context(nc.sbuf_tensor(name, list(shape), dt))

        def sem(name):
            return Sem(octx, nc, name)

        CONSTS = [0.0, 1.0, -1.0, -0.5, 1e-5, 2.5]
        s_const = osb("s_const", (128, len(CONSTS)))
        for i, v in enumerate(CONSTS):
            nc.const_aps.register(F32, v, s_const[:, i:i + 1])
        s_acol = osb("s_acol", (128, 16))
        s_ainv = osb("s_ainv", (128, 16))
        s_dp1 = osb("s_dp1", (CPC, 1))
        s_dtb = osb("s_dtb", (CPC, 1))
        s_carry = osb("s_carry", (128, 16))
        s_zero = osb("s_zero", (128, 1))
        s_onr = osb("s_onr", (1, 128), F32R)
        s_onc = osb("s_onc", (128, 1), F32R)
        s_onc32 = osb("s_onc32", (128, 1))
        s_onr32 = osb("s_onr32", (1, 128))
        s_id = osb("s_id", (128, 128))
        s_gw = osb("s_gw", (128, 8 * NE))
        s_gbb = osb("s_gbb", (128, NE))
        s_gbr = osb("s_gbr", (1, NE), F32)
        s_ckidx = osb("s_ckidx", (128, 8 * CKPC), I32)
        s_tmp16 = osb("s_tmp16", (128, 16))

        ld0 = sem("ld0")
        ld1 = sem("ld1")
        lu = [sem("lu0"), sem("lu1")]
        ldt = [sem("ldt0"), sem("ldt1")]
        lbc = [sem("lbc0"), sem("lbc1")]
        ys = [sem("ys0"), sem("ys1")]
        lyg = [sem("lyg0"), sem("lyg1")]
        ssp = [sem("ssp0"), sem("ssp1")]
        shb = sem("shb")
        scf = sem("scf")
        stp = [sem("stp0"), sem("stp1")]
        sw = [sem(f"sw{i}") for i in range(NSLOT)]
        scr = [sem("scr0"), sem("scr1")]
        lhb = sem("lhb")
        lxr = [sem("lxr0"), sem("lxr1")]
        sp2 = sem("sp2")
        lp2 = [sem("lp20"), sem("lp21")]
        so = [sem("so0"), sem("so1")]
        cc = sem("cc")
        pe = sem("pe")
        act = sem("act")
        dve = sem("dve")
        pool = sem("pool")
        bar = sem("bar")

        phases = []

        def new_phase():
            P = Plan(self_sems={"vector": dve, "scalar": act, "gpsimd": pool})
            phases.append(P)
            return P

        ALL_DMA = ([ld0, ld1, lu[0], lu[1], ldt[0], ldt[1], lbc[0], lbc[1],
                    ys[0], ys[1], lyg[0], lyg[1], ssp[0], ssp[1], shb, scf,
                    stp[0], stp[1]] + sw + scr + [lhb, lxr[0], lxr[1], sp2,
                    lp2[0], lp2[1], so[0], so[1]])

        def barrier(P):
            for e in ENGINES:
                P.op(e, lambda eng: None,
                     waits=[(s, s.c) for s in ALL_DMA], inc=bar)
            tgt = bar.c
            return tgt

        def phase_wait(P, tgt):
            for e in ENGINES:
                P.op(e, lambda eng: None, waits=[(bar, tgt)])

        def ld16(P, q, s, dst_fn, src_fn, waits=()):
            return P.op(q, lambda e: e.dma_start(out=dst_fn(), in_=src_fn()),
                        waits=waits, inc=s, inc_n=16)

        P = new_phase()
        p1ctx = ExitStack()
        with p1ctx:
            def sb(name, shape, dt=F32):
                return p1ctx.enter_context(nc.sbuf_tensor(name, list(shape), dt))

            for i, v in enumerate(CONSTS):
                P.op("vector", lambda e, i=i, v=v: e.memset(s_const[:, i:i+1], v),
                     inc=dve)
            P.op("vector", lambda e: e.memset(s_zero[:, :], 0.0), inc=dve)
            P.op("vector", lambda e: e.memset(s_carry[:, :], 0.0), inc=dve)
            P.op("vector", lambda e: e.memset(s_onc32[:, :], 1.0), inc=dve)
            P.op("vector", lambda e: e.memset(s_onr32[:, :], 1.0), inc=dve)
            const_ready = dve.c

            ld16(P, "sync", ld0, lambda: s_onr[:, :], lambda: ones_row.ap())
            ld16(P, "sync", ld0, lambda: s_onc[:, :], lambda: ones_col.ap())
            ld16(P, "sync", ld0, lambda: s_id[:, :], lambda: ident.ap())
            ld16(P, "sync", ld0, lambda: s_ckidx[:, :], lambda: ckidx.ap())
            ld16(P, "sync", ld0, lambda: s_gbr[:, :], lambda: gateb.ap())
            for d in range(8):
                ld16(P, "sync", ld0, lambda d=d: s_gw[:, NE*d:NE*(d+1)],
                     lambda d=d: gatewT.ap()[128*d:128*(d+1), :])
            ld16(P, "sync", ld0, lambda: s_tmp16[:, :], lambda: alog.ap())
            s_dcol_t = sb("s_dcol_t", (CPC, 1))
            ld16(P, "sync", ld0, lambda: s_dcol_t[:, :], lambda: dcol.ap())
            ld16(P, "sync", ld0, lambda: s_dtb[:, :], lambda: dtb.ap())
            c_small = ld0.c

            a1 = P.op("scalar", lambda e: e.activation(
                s_acol[:, :], s_tmp16[:, :], AF.Exp),
                waits=[(ld0, c_small), (dve, const_ready)], inc=act)
            a2 = P.op("scalar", lambda e: e.activation(
                s_ainv[:, :], s_tmp16[:, :], AF.Exp, scale=-1.0), inc=act)
            P.op("vector", lambda e: e.tensor_scalar(
                s_acol[:, :], s_acol[:, :], -1.0, None, op0=ALU.mult),
                waits=[(act, a2)], inc=dve)
            ainv_ready = P.op("vector", lambda e: e.tensor_scalar(
                s_ainv[:, :], s_ainv[:, :], -1.0, None, op0=ALU.mult), inc=dve)
            acol_ready = ainv_ready
            dp1_ready = P.op("vector", lambda e: e.tensor_scalar(
                s_dp1[:, :], s_dcol_t[:, :], 1.0, None, op0=ALU.add),
                waits=[(ld0, c_small)], inc=dve)

            UW = TPC + 3
            s_utok = sb("s_utok", (128, 8 * UW), F32R)
            s_xpw = sb("s_xpw", (128, 8 * 96))
            s_cw = sb("s_cw", (128, 8 * 4))
            s_cb = sb("s_cb", (128, 8))
            s_wk = sb("s_wk", (128, 8 * 4 * 96), F32R)
            s_c96 = sb("s_c96", (96, 1))
            s_proj = sb("s_proj", (96, 2 * CK), F32R)
            pp = p1ctx.enter_context(nc.psum_tensor("pp", [96, CK], F32))
            pc96 = p1ctx.enter_context(nc.psum_tensor("pc96", [96, 1], F32))

            for d in range(8):
                ld16(P, "sync", ld1, lambda d=d: s_utok[:, UW*d:UW*(d+1)],
                     lambda d=d: u_tokT.ap()[128*d:128*(d+1), :])
                ld16(P, "sync", ld1, lambda d=d: s_xpw[:, 96*d:96*(d+1)],
                     lambda d=d: xpwT.ap()[128*d:128*(d+1), :])
                ld16(P, "sync", ld1, lambda d=d: s_cw[:, 4*d:4*(d+1)],
                     lambda d=d: cw.ap()[128*d:128*(d+1), :])
                ld16(P, "sync", ld1, lambda d=d: s_cb[:, d:d+1],
                     lambda d=d: cb.ap()[128*d:128*(d+1), :])
            c_p1 = ld1.c

            for d in range(8):
                for k in range(4):
                    P.op("vector", lambda e, d=d, k=k: e.tensor_scalar(
                        s_wk[:, d*384 + k*96: d*384 + (k+1)*96],
                        s_xpw[:, 96*d:96*(d+1)], s_cw[:, 4*d+k:4*d+k+1], None,
                        op0=ALU.mult),
                        waits=[(ld1, c_p1)] if (d == 0 and k == 0) else (),
                        inc=dve)
            wk_ready = dve.c

            for d in range(8):
                P.op("tensor", lambda e, d=d: e.matmul(
                    pc96[:, :], s_xpw[:, 96*d:96*(d+1)], s_cb[:, d:d+1],
                    start=(d == 0), stop=(d == 7)),
                    waits=[(dve, wk_ready)] if d == 0 else (),
                    inc=pe if d == 7 else None)
            c96_mm = pe.c
            c96_ready = P.op("scalar", lambda e: e.copy(s_c96[:, :], pc96[:, :]),
                             waits=[(pe, c96_mm)], inc=act)

            prev_evac = 0
            for ck_i in range(CKPC):
                buf = ck_i % 2
                for d in range(8):
                    for k in range(4):
                        col0 = UW*d + ck_i*CK + k
                        w = [(dve, prev_evac)] if (d == 0 and k == 0) else ()
                        P.op("tensor", lambda e, d=d, k=k, col0=col0: e.matmul(
                            pp[:, :], s_wk[:, d*384 + k*96: d*384 + (k+1)*96],
                            s_utok[:, col0:col0+CK],
                            start=(d == 0 and k == 0), stop=(d == 7 and k == 3)),
                            waits=w, inc=pe if (d == 7 and k == 3) else None)
                mm_done = pe.c
                prev_evac = P.op("vector", lambda e, buf=buf: e.tensor_scalar(
                    s_proj[:, CK*buf:CK*(buf+1)], pp[:, :], s_c96[:, :1], None,
                    op0=ALU.add),
                    waits=[(pe, mm_done), (act, c96_ready),
                           (stp[buf], stp[buf].c)], inc=dve)
                P.op("sync", lambda e, ck_i=ck_i, buf=buf: e.dma_start(
                    out=cc_prep_in.ap()[:, CK*ck_i:CK*(ck_i+1)],
                    in_=s_proj[:, CK*buf:CK*(buf+1)]),
                    waits=[(dve, prev_evac)], inc=stp[buf], inc_n=16)

            cc_prep_done = P.op("gpsimd", lambda e: e.collective_compute(
                "AllGather", ALU.bypass, replica_groups=[list(range(NCORE))],
                ins=[cc_prep_in.ap().opt()], outs=[cc_prep_out.ap().opt()]),
                waits=[(stp[0], stp[0].c), (stp[1], stp[1].c)], inc=cc)
            bar1 = barrier(P)
            with nc.Block("ph1") as blk:
                P.emit(blk)

        P = new_phase()
        phase_wait(P, bar1)
        p3ctx = ExitStack()
        with p3ctx:
            def sb(name, shape, dt=F32):
                return p3ctx.enter_context(nc.sbuf_tensor(name, list(shape), dt))

            s_u3 = sb("s_u3", (128, 2 * CK), F32R)
            s_dt3 = sb("s_dt3", (64, 2 * CK), F32R)
            s_bt3 = sb("s_bt3", (NST, 2 * CK), F32R)
            s_ct3 = sb("s_ct3", (NST, 2 * CK), F32R)
            s_selr = sb("s_selr", (128, 16 * 128), F32R)
            s_selc = sb("s_selc", (128, 16 * 128), F32R)
            s_selb = sb("s_selb", (NST, 128), F32R)
            s_dtw = sb("s_dtw", (DTR, CPC), F32R)
            s_sp = sb("s_sp", (128, CK))
            s_delta = sb("s_delta", (128, 2 * CK), F32R)
            RDA, RW, RT = 6, 4, 8
            s_dA = sb("s_dA", (128, RDA * CK))
            s_w1 = sb("s_w1", (128, RW * CK))
            s_w2 = sb("s_w2", (128, RW * CK))
            s_hh = sb("s_hh", (128, 32 * CK))
            s_t1 = sb("s_t1", (128, RT * CK), F32R)
            s_brep = sb("s_brep", (128, 2 * CK))
            s_crep = sb("s_crep", (128, 2 * CK))
            s_y3 = sb("s_y3", (128, 2 * CK), F32R)
            s_tp = sb("s_tp", (128, 2 * CK))
            psh = p3ctx.enter_context(nc.psum_tensor("psh", [128, CK], F32))
            pdr = [p3ctx.enter_context(nc.psum_tensor(f"pdr{i}", [128, CK], F32))
                   for i in range(2)]
            pur = [p3ctx.enter_context(nc.psum_tensor(f"pur{i}", [128, CK], F32))
                   for i in range(3)]
            py3 = [p3ctx.enter_context(nc.psum_tensor(f"py3{i}", [128, CK], F32))
                   for i in range(2)]

            for g in range(16):
                ld16(P, "sync", ld1, lambda g=g: s_selr[:, 128*g:128*(g+1)],
                     lambda g=g: selrep.ap()[g, :, :])
                ld16(P, "sync", ld1, lambda g=g: s_selc[:, 128*g:128*(g+1)],
                     lambda g=g: selcon.ap()[g, :, :])
            ld16(P, "sync", ld1, lambda: s_selb[:, :], lambda: selbc.ap())
            ld16(P, "sync", ld1, lambda: s_dtw[:, :], lambda: dtwT.ap())
            sel_ready = ld1.c

            ev = {
                "w2_free": [0] * RW,
                "t1_free": [0] * RT,
                "u_free": [0, 0],
                "dt_free": [0, 0],
                "bc_free": [0, 0],
                "delta_free": [0, 0],
                "brep_free": [0, 0],
                "crep_free": [(0, 0), (0, 0)],
                "psh_free_act": 0,
                "pdr_free": [0] * 2,
                "pur_free": [0] * 3,
                "py_free": [0, 0],
                "y_free": [0, 0],
                "w1s_free": [0] * RW,
                "loads": {},
                "spq": [],
                "a": {},
            }
            dA_free = [0] * RDA
            CLAG = 4
            pend_con = []

            def hh_col(ck, g):
                return ((ck % 2) * 16 + g) * CK

            def emit_loads(Pp, ck):
                buf = ck % 2
                rk, cko = ck // CKPC, ck % CKPC
                c_u = Pp.op("sync", lambda e, ck=ck, buf=buf: e.dma_start(
                    out=s_u3[:, CK*buf:CK*(buf+1)],
                    in_=u_chanT.ap()[:, CK*ck:CK*(ck+1)]),
                    waits=[(dve, ev["u_free"][buf])], inc=lu[buf], inc_n=16)
                c_dt = Pp.op("sync", lambda e, rk=rk, cko=cko, buf=buf:
                             e.dma_start(
                    out=s_dt3[:, CK*buf:CK*(buf+1)],
                    in_=cc_prep_out.ap()[96*rk:96*rk+64, CK*cko:CK*(cko+1)]),
                    waits=[(cc, cc_prep_done), (pe, ev["dt_free"][buf])],
                    inc=ldt[buf], inc_n=16)
                Pp.op("sync", lambda e, rk=rk, cko=cko, buf=buf: e.dma_start(
                    out=s_bt3[:, CK*buf:CK*(buf+1)],
                    in_=cc_prep_out.ap()[96*rk+64:96*rk+80, CK*cko:CK*(cko+1)]),
                    waits=[(pe, ev["bc_free"][buf])], inc=lbc[buf], inc_n=16)
                c_c = Pp.op("sync", lambda e, rk=rk, cko=cko, buf=buf:
                            e.dma_start(
                    out=s_ct3[:, CK*buf:CK*(buf+1)],
                    in_=cc_prep_out.ap()[96*rk+80:96*rk+96, CK*cko:CK*(cko+1)]),
                    inc=lbc[buf], inc_n=16)
                ev["loads"][ck] = (c_u, c_dt, c_c)

            def emit_prep1(Pp, ck):
                buf = ck % 2
                (c_u, c_dt, c_c) = ev["loads"][ck]
                pe_d = Pp.op("tensor", lambda e, buf=buf: e.matmul(
                    psh[:, :], s_dtw[:, :], s_dt3[:, CK*buf:CK*(buf+1)],
                    start=True, stop=True),
                    waits=[(ldt[buf], c_dt), (ld1, sel_ready),
                           (act, ev["psh_free_act"])], inc=pe)
                ev["dt_free"][buf] = pe.c
                a_sp = Pp.op("scalar", lambda e: e.activation(
                    s_sp[:, :], psh[:, :], AF.Exp, bias=s_dtb[:, :1]),
                    waits=[(pe, pe_d)], inc=act)
                ev["a"].setdefault(ck, {})["a_sp"] = a_sp

            def emit_prep2(Pp, ck):
                buf = ck % 2
                (c_u, c_dt, c_c) = ev["loads"][ck]
                a_sp = ev["a"][ck]["a_sp"]
                a_dl = Pp.op("scalar", lambda e, buf=buf: e.activation(
                    s_delta[:, CK*buf:CK*(buf+1)], s_sp[:, :], AF.Ln, bias=1.0),
                    waits=[(pe, ev["delta_free"][buf])], inc=act)
                pe_b = Pp.op("tensor", lambda e, buf=buf: e.matmul(
                    psh[:, :], s_selb[:, :], s_bt3[:, CK*buf:CK*(buf+1)],
                    start=True, stop=True),
                    waits=[(lbc[buf], c_c), (act, a_sp)], inc=pe)
                a_b = Pp.op("scalar", lambda e, buf=buf: e.activation(
                    s_brep[:, CK*buf:CK*(buf+1)], psh[:, :], AF.Copy),
                    waits=[(pe, pe_b), (pool, ev["brep_free"][buf])], inc=act)
                ev["a"][ck].update(a_dl=a_dl, a_b=a_b)

            def emit_prep3(Pp, ck):
                buf = ck % 2
                a_b = ev["a"][ck]["a_b"]
                pe_c = Pp.op("tensor", lambda e, buf=buf: e.matmul(
                    psh[:, :], s_selb[:, :], s_ct3[:, CK*buf:CK*(buf+1)],
                    start=True, stop=True),
                    waits=[(act, a_b)], inc=pe)
                ev["bc_free"][buf] = pe.c
                a_c = Pp.op("scalar", lambda e, buf=buf: e.activation(
                    s_crep[:, CK*buf:CK*(buf+1)], psh[:, :], AF.Copy),
                    waits=[(pe, pe_c), (dve, ev["crep_free"][buf][0]),
                           (pool, ev["crep_free"][buf][1])], inc=act)
                ev["psh_free_act"] = a_c
                ev["a"][ck]["a_c"] = a_c

            def emit_con(Pp, item):
                g, gt, ck, d_t1, t1_sem = item
                pyb = ck % 2
                pe_con = Pp.op("tensor", lambda e, g=g, gt=gt, pyb=pyb: e.matmul(
                    py3[pyb][:, :], s_selc[:, 128*g:128*(g+1)],
                    s_t1[:, CK*gt:CK*(gt+1)],
                    start=(g == 0), stop=(g == 15)),
                    waits=[(t1_sem, d_t1)] +
                          ([(dve, ev["py_free"][pyb])] if g == 0 else []),
                    inc=pe)
                ev["t1_free"][gt] = pe_con
                if g == 15:
                    buf = ck % 2
                    d_y = Pp.op("vector", lambda e, buf=buf, pyb=pyb:
                                e.scalar_tensor_tensor(
                        s_y3[:, CK*buf:CK*(buf+1)],
                        s_u3[:, CK*buf:CK*(buf+1)].bitcast(F32),
                        s_dp1[:, :1], py3[pyb][:, :],
                        op0=ALU.mult, op1=ALU.add),
                        waits=[(pe, pe_con), (dve, dp1_ready),
                               (ys[buf], ev["y_free"][buf])], inc=dve)
                    ev["u_free"][buf] = d_y
                    ev["py_free"][pyb] = d_y
                    Pp.op("gpsimd", lambda e, ck=ck, buf=buf: e.dma_start(
                        out=cc_y_in.ap()[ck // 8][:, (ck % 8)*CK:
                                                  (ck % 8 + 1)*CK],
                        in_=s_y3[:, CK*buf:CK*(buf+1)]),
                        waits=[(dve, d_y)], inc=ys[buf], inc_n=16)
                    ev["y_free"][buf] = ys[buf].c
                    if ck % 8 == 7:
                        q = ck // 8
                        Pp.op("gpsimd", lambda e, q=q: e.collective_compute(
                            "AllGather", ALU.bypass,
                            replica_groups=[list(range(NCORE))],
                            ins=[cc_y_in.ap()[q].opt()],
                            outs=[cc_y_out.ap()[q].opt()]),
                            waits=[(ys[0], ys[0].c), (ys[1], ys[1].c)], inc=cc)

            pend_scan = []

            def emit_scan_t1(Pp, it):
                g, gw, gda, gt = it["g"], it["gw"], it["gda"], it["gt"]
                buf, ckk = it["buf"], it["ckk"]
                if ckk in (0, NCK // 2):
                    init_ap = s_zero[:, 0:1]
                else:
                    pc = hh_col(ckk - 1, g)
                    init_ap = s_hh[:, pc + CK - 1: pc + CK]
                hc = hh_col(ckk, g)
                d_sc = Pp.op("vector", lambda e, hc=hc, gda=gda, gw=gw,
                             init_ap=init_ap: e.tensor_tensor_scan(
                    s_hh[:, hc:hc + CK], s_dA[:, CK*gda:CK*(gda+1)],
                    s_w2[:, CK*gw:CK*gw+CK],
                    init_ap, op0=ALU.mult, op1=ALU.add),
                    waits=[(pool, it["p_w2"])], inc=dve)
                ev["w2_free"][gw] = d_sc
                dA_free[gda] = d_sc
                d_t1 = Pp.op("vector", lambda e, hc=hc, gt=gt, g=g,
                             buf=buf: e.scalar_tensor_tensor(
                    s_t1[:, CK*gt:CK*(gt+1)], s_hh[:, hc:hc + CK],
                    s_ainv[:, g:g+1], s_crep[:, CK*buf:CK*(buf+1)],
                    op0=ALU.mult, op1=ALU.mult),
                    waits=[(act, it["a_c"]), (pe, ev["t1_free"][gt]),
                           (dve, ainv_ready)],
                    inc=dve)
                if g >= 14:
                    ev["crep_free"][buf] = (d_t1, 0)
                pend_con.append((g, gt, ckk, d_t1, dve))

            emit_loads(P, 0)
            emit_prep1(P, 0)
            emit_prep2(P, 0)
            emit_prep3(P, 0)
            for ckk in range(NCK):
                buf = ckk % 2
                a_dl = ev["a"][ckk]["a_dl"]
                a_c = ev["a"][ckk]["a_c"]
                (c_u, c_dt, c_c) = ev["loads"][ckk]
                for g in range(16):
                    gw = g % RW
                    gda = g % RDA
                    gt = (ckk * 16 + g) % RT
                    gp2 = g % 2
                    gp3 = g % 3
                    if len(pend_con) > CLAG:
                        emit_con(P, pend_con.pop(0))
                    if ckk + 1 < NCK:
                        if g == 6:
                            emit_loads(P, ckk + 1)
                        elif g == 8:
                            emit_prep1(P, ckk + 1)
                        elif g == 10:
                            emit_prep2(P, ckk + 1)
                        elif g == 12:
                            emit_prep3(P, ckk + 1)
                    pe_dr = P.op("tensor", lambda e, g=g, buf=buf, gp2=gp2:
                                 e.matmul(
                        pdr[gp2][:, :], s_selr[:, 128*g:128*(g+1)],
                        s_delta[:, CK*buf:CK*(buf+1)], start=True, stop=True),
                        waits=[(act, a_dl), (act, ev["pdr_free"][gp2])], inc=pe)
                    if g == 15:
                        ev["delta_free"][buf] = pe.c
                    pe_ur = P.op("tensor", lambda e, g=g, buf=buf, gp3=gp3:
                                 e.matmul(
                        pur[gp3][:, :], s_selr[:, 128*g:128*(g+1)],
                        s_u3[:, CK*buf:CK*(buf+1)], start=True, stop=True),
                        waits=[(lu[buf], c_u), (dve, ev["pur_free"][gp3])],
                        inc=pe)
                    a_dA = P.op("scalar", lambda e, g=g, gp2=gp2, gda=gda:
                                e.activation(
                        s_dA[:, CK*gda:CK*(gda+1)], pdr[gp2][:, :], AF.Exp,
                        scale=s_acol[:, g:g+1]),
                        waits=[(pe, pe_dr), (dve, dA_free[gda]),
                               (dve, acol_ready)], inc=act)
                    ev["pdr_free"][gp2] = a_dA
                    d_w1 = P.op("vector", lambda e, gw=gw, gda=gda, gp3=gp3:
                                e.scalar_tensor_tensor(
                        s_w1[:, CK*gw:CK*(gw+1)], s_dA[:, CK*gda:CK*(gda+1)],
                        -1.0, pur[gp3][:, :], op0=ALU.add, op1=ALU.mult),
                        waits=[(act, a_dA), (pe, pe_ur),
                               (pool, ev["w1s_free"][gw])], inc=dve)
                    ev["pur_free"][gp3] = d_w1
                    p_w2 = P.op("gpsimd", lambda e, gw=gw, buf=buf:
                                e.tensor_tensor(
                        s_w2[:, CK*gw:CK*(gw+1)], s_w1[:, CK*gw:CK*(gw+1)],
                        s_brep[:, CK*buf:CK*(buf+1)], op=ALU.mult),
                        waits=[(dve, d_w1), (act, ev["a"][ckk]["a_b"]),
                               (dve, ev["w2_free"][gw])], inc=pool)
                    ev["w1s_free"][gw] = p_w2
                    if g >= 14:
                        ev["brep_free"][buf] = p_w2
                    pend_scan.append(dict(
                        g=g, gw=gw, gda=gda, gt=gt, buf=buf, ckk=ckk,
                        p_w2=p_w2, a_c=a_c))
                    if len(pend_scan) > 1:
                        emit_scan_t1(P, pend_scan.pop(0))
            while pend_scan:
                emit_scan_t1(P, pend_scan.pop(0))
            while pend_con:
                emit_con(P, pend_con.pop(0))
            cc_y_done = cc.c
            bar3 = barrier(P)
            with nc.Block("ph3") as blk:
                P.emit(blk)

        P = new_phase()
        phase_wait(P, bar3)
        p5ctx = ExitStack()
        with p5ctx:
            def sb(name, shape, dt=F32):
                return p5ctx.enter_context(nc.sbuf_tensor(name, list(shape), dt))

            s_yg = sb("s_yg", (128, 2 * 8 * CK), F32R)
            s_ysq = sb("s_ysq", (128, 8 * CK), F32R)
            s_h5 = sb("s_h5", (128, 2 * 8 * CK))
            s_hb5 = sb("s_hb5", (128, 8 * CK), BF16)
            s_r1 = sb("s_r1", (1, CK))
            s_mu = sb("s_mu", (1, CK))
            s_m2 = sb("s_m2", (1, CK))
            s_var = sb("s_var", (1, CK))
            s_rstd = sb("s_rstd", (1, CK), F32R)
            s_nmu = sb("s_nmu", (1, CK), F32R)
            s_sc5 = sb("s_sc5", (128, NE))
            s_sc5b = sb("s_sc5b", (128, NE))
            s_gsc = sb("s_gsc", (128, 8))
            s_gtmp = sb("s_gtmp", (128, 8))
            s_gm = sb("s_gm", (128, 1))
            s_msk = sb("s_msk", (128, NE))
            s_mskd = sb("s_mskd", (128, NE))
            s_thr = sb("s_thr", (128, 1))
            s_wsum = sb("s_wsum", (128, 1))
            s_comb = sb("s_comb", (128, NE))
            s_combT = sb("s_combT", (NE, CK), F32R)
            p_s1 = p5ctx.enter_context(nc.psum_tensor("p_s1", [1, CK], F32))
            p_s2 = p5ctx.enter_context(nc.psum_tensor("p_s2", [1, CK], F32))
            p_bc1 = p5ctx.enter_context(nc.psum_tensor("p_bc1", [128, CK], F32))
            p_bc2 = p5ctx.enter_context(nc.psum_tensor("p_bc2", [128, CK], F32))
            p_lg = p5ctx.enter_context(nc.psum_tensor("p_lg", [128, NE], F32))
            p_tr5 = p5ctx.enter_context(nc.psum_tensor("p_tr5", [NE, 128], F32))

            pe_gb = P.op("tensor", lambda e: e.matmul(
                p_bc1[:, 0:NE], s_onr32[:, :], s_gbr[:, :], start=True,
                stop=True), inc=pe)
            gbb_ready = P.op("scalar", lambda e: e.activation(
                s_gbb[:, :], p_bc1[:, 0:NE], AF.Copy),
                waits=[(pe, pe_gb)], inc=act)

            p5 = {"yg_free": [0, 0], "h5_free": [0, 0], "hb_free": 0,
                  "h5_rt": [0, 0],
                  "ps1_free": 0, "ps2_free": 0, "bc1_free": gbb_ready,
                  "bc2_free": 0, "lg_free": 0, "tr_free": 0, "combT_free": 0}
            def emit_routing(cko, buf, d_h):
                    for tt in range(4):
                        for d in range(8):
                            P.op("tensor", lambda e, d=d, tt=tt, buf=buf: e.matmul(
                                p_lg[:, :],
                                s_h5[:, (buf*8 + d)*CK + tt*128:
                                     (buf*8 + d)*CK + (tt+1)*128],
                                s_gw[:, NE*d:NE*(d+1)], start=(d == 0), stop=(d == 7)),
                                waits=([(dve, d_h), (act, p5["lg_free"])]
                                       if d == 0 else ()),
                                inc=pe if d == 7 else None)
                        pe_lg = pe.c
                        a_e = P.op("scalar", lambda e: e.activation(
                            s_sc5[:, :], p_lg[:, :], AF.Exp, scale=-1.0),
                            waits=[(pe, pe_lg), (dve, 0)], inc=act)
                        p5["lg_free"] = a_e
                        P.op("vector", lambda e: e.tensor_scalar(
                            s_sc5[:, :], s_sc5[:, :], 1.0, None, op0=ALU.add),
                            waits=[(act, a_e)], inc=dve)
                        P.op("vector", lambda e: e.reciprocal(
                            s_sc5[:, :], s_sc5[:, :]), inc=dve)
                        P.op("vector", lambda e: e.tensor_tensor(
                            s_sc5b[:, :], s_sc5[:, :], s_gbb[:, :], op=ALU.add),
                            waits=[(act, gbb_ready)], inc=dve)
                        pairs = [(0, 1), (0, 2), (0, 3), (1, 2), (1, 3), (2, 3)]
                        for pi, (i1, i2) in enumerate(pairs):
                            dst = s_gsc if pi == 0 else s_gtmp
                            P.op("vector", lambda e, i1=i1, i2=i2, dst=dst:
                                 e.tensor_tensor(
                                dst[:, :], s_sc5b[:, i1::4], s_sc5b[:, i2::4],
                                op=ALU.add), inc=dve)
                            if pi > 0:
                                P.op("vector", lambda e: e.tensor_tensor(
                                    s_gsc[:, :], s_gsc[:, :], s_gtmp[:, :],
                                    op=ALU.max), inc=dve)
                        P.op("vector", lambda e: e.tensor_copy(
                            s_gtmp[:, :], s_gsc[:, :]), inc=dve)
                        for it in range(4):
                            P.op("vector", lambda e: e.reduce_max(
                                s_gm[:, :], s_gtmp[:, :], axis=AX.X), inc=dve)
                            if it < 3:
                                P.op("vector", lambda e: e.tensor_scalar(
                                    s_msk[:, 0:8], s_gtmp[:, :], s_gm[:, :1], None,
                                    op0=ALU.is_ge), inc=dve)
                                P.op("vector", lambda e: e.scalar_tensor_tensor(
                                    s_gtmp[:, :], s_msk[:, 0:8], -1e30, s_gtmp[:, :],
                                    op0=ALU.mult, op1=ALU.add), inc=dve)
                        P.op("vector", lambda e: e.tensor_scalar(
                            s_gtmp[:, :], s_gsc[:, :], s_gm[:, :1], None,
                            op0=ALU.is_ge), inc=dve)
                        for j in range(4):
                            P.op("vector", lambda e, j=j: e.tensor_copy(
                                s_msk[:, j::4], s_gtmp[:, :]), inc=dve)
                        P.op("vector", lambda e: e.tensor_tensor(
                            s_mskd[:, :], s_sc5b[:, :], s_msk[:, :], op=ALU.mult),
                            inc=dve)
                        P.op("vector", lambda e: e.tensor_copy(
                            s_comb[:, :], s_mskd[:, :]), inc=dve)
                        for it in range(8):
                            P.op("vector", lambda e: e.reduce_max(
                                s_thr[:, :], s_comb[:, :], axis=AX.X), inc=dve)
                            if it < 7:
                                P.op("vector", lambda e: e.tensor_scalar(
                                    s_msk[:, :], s_comb[:, :], s_thr[:, :1], None,
                                    op0=ALU.is_ge), inc=dve)
                                P.op("vector", lambda e: e.scalar_tensor_tensor(
                                    s_comb[:, :], s_msk[:, :], -1e30, s_comb[:, :],
                                    op0=ALU.mult, op1=ALU.add), inc=dve)
                        P.op("vector", lambda e: e.tensor_scalar(
                            s_msk[:, :], s_mskd[:, :], s_thr[:, :1], None,
                            op0=ALU.is_ge), inc=dve)
                        P.op("vector", lambda e: e.tensor_tensor(
                            s_comb[:, :], s_msk[:, :], s_sc5[:, :], op=ALU.mult),
                            inc=dve)
                        P.op("vector", lambda e: e.reduce_sum(
                            s_wsum[:, :], s_comb[:, :], axis=AX.X), inc=dve)
                        P.op("vector", lambda e: e.tensor_scalar(
                            s_wsum[:, :], s_wsum[:, :], 1e-20, None, op0=ALU.add),
                            inc=dve)
                        P.op("vector", lambda e: e.reciprocal(
                            s_wsum[:, :], s_wsum[:, :]), inc=dve)
                        P.op("vector", lambda e: e.tensor_scalar(
                            s_wsum[:, :], s_wsum[:, :], 2.5, None, op0=ALU.mult),
                            inc=dve)
                        d_cb = P.op("vector", lambda e: e.tensor_scalar(
                            s_comb[:, :], s_comb[:, :], s_wsum[:, :1], None,
                            op0=ALU.mult), inc=dve)
                        pe_tr = P.op("tensor", lambda e: e.transpose(
                            p_tr5[:, :], s_comb[:, :], s_id[:, :]),
                            waits=[(dve, d_cb), (act, p5["tr_free"])], inc=pe)
                        p5["tr_free"] = P.op("scalar", lambda e, tt=tt: e.activation(
                            s_combT[:, tt*128:(tt+1)*128], p_tr5[:, :], AF.Copy),
                            waits=[(pe, pe_tr), (scf, p5["combT_free"])], inc=act)

                    a_combT = act.c
                    P.op("scalar", lambda e, cko=cko: e.dma_start(
                        out=cflat_spill.ap()[:, CK*cko:CK*(cko+1)],
                        in_=s_combT[:, :]),
                        waits=[(act, a_combT)], inc=scf, inc_n=16)
                    p5["combT_free"] = scf.c
                    p5["h5_rt"][buf] = dve.c

            rt_pend = []
            for cko in range(CKPC):
                buf = cko % 2
                c_yg = None
                for r in range(8):
                    c_yg = P.op("gpsimd",
                                lambda e, r=r, cko=cko, buf=buf: e.indirect_dma_start(
                        out=s_yg[:, (buf*8 + r)*CK:(buf*8 + r + 1)*CK],
                        out_offset=None,
                        in_=cc_y_out.ap().rearrange(
                            "q p (k f) -> (q p k) f", f=CK),
                        in_offset=IndirectOffsetOnAxis(
                            s_ckidx[:, r*CKPC+cko:r*CKPC+cko+1], 0)),
                        waits=[(cc, cc_y_done)] +
                              ([(dve, p5["yg_free"][buf])] if r == 0 else []),
                        inc=lyg[buf], inc_n=16)
                a_sq = None
                for d in range(8):
                    a_sq = P.op("scalar", lambda e, d=d, buf=buf: e.activation(
                        s_ysq[:, d*CK:(d+1)*CK],
                        s_yg[:, (buf*8 + d)*CK:(buf*8 + d + 1)*CK].bitcast(F32),
                        AF.Square),
                        waits=[(lyg[buf], c_yg)] if d == 0 else (), inc=act)
                for d in range(8):
                    P.op("tensor", lambda e, d=d, buf=buf: e.matmul(
                        p_s1[:, :], s_onc[:, :],
                        s_yg[:, (buf*8 + d)*CK:(buf*8 + d + 1)*CK],
                        start=(d == 0), stop=(d == 7)),
                        waits=([(lyg[buf], c_yg), (dve, p5["ps1_free"])]
                               if d == 0 else ()),
                        inc=pe if d == 7 else None)
                pe_s1 = pe.c
                for d in range(8):
                    P.op("tensor", lambda e, d=d: e.matmul(
                        p_s2[:, :], s_onc[:, :], s_ysq[:, d*CK:(d+1)*CK],
                        start=(d == 0), stop=(d == 7)),
                        waits=([(act, a_sq), (dve, p5["ps2_free"])]
                               if d == 0 else ()),
                        inc=pe if d == 7 else None)
                pe_s2 = pe.c
                d_mu = P.op("vector", lambda e: e.tensor_scalar(
                    s_mu[:, :], p_s1[0:1, :], 1.0 / DM, None, op0=ALU.mult),
                    waits=[(pe, pe_s1)], inc=dve)
                p5["ps1_free"] = d_mu
                P.op("vector", lambda e: e.tensor_tensor(
                    s_m2[:, :], s_mu[:, :], s_mu[:, :], op=ALU.mult), inc=dve)
                d_var = P.op("vector", lambda e: e.scalar_tensor_tensor(
                    s_var[:, :], p_s2[0:1, :], 1.0 / DM, s_m2[:, :],
                    op0=ALU.mult, op1=ALU.subtract),
                    waits=[(pe, pe_s2)], inc=dve)
                p5["ps2_free"] = d_var
                P.op("scalar", lambda e: e.activation(
                    s_r1[:, :], s_var[:, :], AF.Ln, bias=1e-5),
                    waits=[(dve, d_var)], inc=act)
                a_rstd = P.op("scalar", lambda e: e.activation(
                    s_rstd[:, :], s_r1[:, :], AF.Exp, scale=-0.5), inc=act)
                d_nmu = P.op("vector", lambda e: e.scalar_tensor_tensor(
                    s_nmu[:, :], s_mu[:, :], -1.0, s_rstd[:, :].bitcast(F32),
                    op0=ALU.mult, op1=ALU.mult),
                    waits=[(act, a_rstd)], inc=dve)
                pe_b1 = P.op("tensor", lambda e: e.matmul(
                    p_bc1[:, :], s_onr[:, :], s_rstd[:, :], start=True, stop=True),
                    waits=[(act, a_rstd), (act, p5["bc1_free"] if cko == 0 else 0),
                           (dve, p5["bc1_free"] if cko > 0 else 0)], inc=pe)
                pe_b2 = P.op("tensor", lambda e: e.matmul(
                    p_bc2[:, :], s_onr[:, :], s_nmu[:, :], start=True, stop=True),
                    waits=[(dve, d_nmu), (dve, p5["bc2_free"])], inc=pe)
                d_h = None
                for d in range(8):
                    P.op("vector", lambda e, d=d, buf=buf: e.tensor_tensor(
                        s_h5[:, (buf*8 + d)*CK:(buf*8 + d + 1)*CK],
                        s_yg[:, (buf*8 + d)*CK:(buf*8 + d + 1)*CK].bitcast(F32),
                        p_bc1[:, :], op=ALU.mult),
                        waits=[(pe, pe_b1)] +
                              ([(ssp[buf], p5["h5_free"][buf]),
                                (dve, p5["h5_rt"][buf])]
                               if d == 0 else []), inc=dve)
                    d_h = P.op("vector", lambda e, d=d, buf=buf: e.tensor_tensor(
                        s_h5[:, (buf*8 + d)*CK:(buf*8 + d + 1)*CK],
                        s_h5[:, (buf*8 + d)*CK:(buf*8 + d + 1)*CK],
                        p_bc2[:, :], op=ALU.add),
                        waits=[(pe, pe_b2)] if d == 0 else (), inc=dve)
                p5["yg_free"][buf] = d_h
                p5["bc1_free"] = d_h
                p5["bc2_free"] = d_h
                d_hb = None
                for d in range(8):
                    d_hb = P.op("vector", lambda e, d=d, buf=buf: e.tensor_copy(
                        s_hb5[:, d*CK:(d+1)*CK],
                        s_h5[:, (buf*8 + d)*CK:(buf*8 + d + 1)*CK]),
                        waits=[(shb, p5["hb_free"])] if d == 0 else (),
                        inc=dve)
                sp1 = P.op("scalar", lambda e, cko=cko, buf=buf: e.dma_start(
                    out=xres_spill.ap()[:, CK*cko:CK*(cko+1)].rearrange(
                        "(dt p) f -> p dt f", p=128),
                    in_=s_h5[:, buf*8*CK:(buf*8 + 8)*CK]),
                    waits=[(dve, d_h)], inc=ssp[buf], inc_n=16)
                P.op("scalar", lambda e, cko=cko: e.dma_start(
                    out=hb_spill.ap()[:, CK*cko:CK*(cko+1)].rearrange(
                        "(dt p) f -> p dt f", p=128),
                    in_=s_hb5[:, :]),
                    waits=[(dve, d_hb)], inc=shb, inc_n=16)
                p5["hb_free"] = shb.c
                p5["h5_free"][buf] = ssp[buf].c

                rt_pend.append((cko, buf, d_h))
                if rt_pend and len(rt_pend) > 1:
                    emit_routing(*rt_pend.pop(0))

            while rt_pend:
                emit_routing(*rt_pend.pop(0))
            cflat_ready = scf.c
            hb_all = shb.c
            xres_all = [ssp[0].c, ssp[1].c]
            bar5 = barrier(P)
            with nc.Block("ph5") as blk:
                P.emit(blk)

        P = new_phase()
        phase_wait(P, bar5)
        p6ctx = ExitStack()
        with p6ctx:
            def sb(name, shape, dt=F32):
                return p6ctx.enter_context(nc.sbuf_tensor(name, list(shape), dt))

            s_wg = sb("s_wg", (128, NSLOT * 8 * EI), BF16)
            s_wu = sb("s_wu", (128, NSLOT * 8 * EI), BF16)
            s_wd = sb("s_wd", (128, NSLOT * 4 * DM), BF16)
            s_hb6 = sb("s_hb6", (128, 8 * TB), BF16)
            s_routed = sb("s_routed", (128, 8 * TB))
            s_cbc = sb("s_cbc", (128, EG * TB))
            s_crow = sb("s_crow", (1, EG * TB), F32R)
            s_sg6 = sb("s_sg6", (128, 4 * SUB))
            s_x6 = sb("s_x6", (128, 2 * 4 * SUB), BF16)
            s_xr6 = sb("s_xr6", (128, 2 * TB))
            pg = [p6ctx.enter_context(nc.psum_tensor(f"pg{i}", [128, 512], F32))
                  for i in range(2)]
            pu = [p6ctx.enter_context(nc.psum_tensor(f"pu{i}", [128, 512], F32))
                  for i in range(2)]
            pr6 = [p6ctx.enter_context(nc.psum_tensor(f"pr{i}", [128, 512], F32))
                   for i in range(4)]

            def wslot(base, slot, n):
                return base[:, slot * n:(slot + 1) * n]

            p6 = {"crow_free": [0, 0],
                  "slot_free": [0] * NSLOT, "slot_ready": [0] * NSLOT,
                  "cbc_free": 0, "sg_free": 0, "x_free": [0, 0],
                  "routed_free": 0, "pr_free": 0, "pg_free": [0, 0],
                  "pu_free": [0, 0], "hb_free": 0, "xr_free": [0, 0]}

            for tb in range(2):
                c_hb = P.op("sync", lambda e, tb=tb: e.dma_start(
                    out=s_hb6[:, :],
                    in_=hb_spill.ap()[:, TB*tb:TB*(tb+1)].rearrange(
                        "(kd p) t -> p kd t", p=128)),
                    waits=[(shb, hb_all), (pe, p6["hb_free"])],
                    inc=lhb, inc_n=16)
                d_ms = None
                for ht in range(8):
                    d_ms = P.op("vector", lambda e, ht=ht: e.memset(
                        s_routed[:, ht*TB:(ht+1)*TB], 0.0),
                        waits=[(sp2, p6["routed_free"])] if ht == 0 else (),
                        inc=dve)

                pend_dn = []

                def emit_down(Pp, item):
                    (j_, sub_, sl_, xslot_, d_x_, need_ms) = item
                    w_extra = []
                    if j_ == 0:
                        w_extra.append((dve, p6["pr_free"]))
                        if need_ms:
                            w_extra.append((dve, d_ms))
                    for ht in range(8):
                        for ki in range(4):
                            w = []
                            if ht == 0 and ki == 0:
                                w = [(dve, d_x_)] + w_extra
                            Pp.op("tensor",
                                  lambda e, ht=ht, ki=ki, sl=sl_, j=j_,
                                  xs=xslot_: e.matmul(
                                pr6[ht // 2][:, (ht % 2)*SUB:
                                             (ht % 2 + 1)*SUB],
                                wslot(s_wd, sl, 4*DM)[
                                    :, ki*DM + ht*128:
                                    ki*DM + (ht+1)*128],
                                s_x6[:, xs*TB + ki*SUB:
                                     xs*TB + (ki+1)*SUB],
                                start=(ki == 0 and j == 0
                                       and ht % 2 == 0),
                                stop=(ki == 3 and j == EG - 1
                                      and ht % 2 == 1)),
                                waits=w,
                                inc=pe if (ki == 3 and ht == 7)
                                else None)
                    dn_done = pe.c
                    p6["x_free"][xslot_] = dn_done
                    p6["slot_free"][sl_] = dn_done
                    if j_ == EG - 1:
                        for q in range(4):
                            P.op("vector", lambda e, q=q, sub=sub_:
                                 e.tensor_tensor(
                                s_routed[:, (2*q)*TB + sub*SUB:
                                         (2*q)*TB + sub*SUB + SUB],
                                pr6[q][:, 0:SUB],
                                s_routed[:, (2*q)*TB + sub*SUB:
                                         (2*q)*TB + sub*SUB + SUB],
                                op=ALU.add),
                                waits=[(pe, dn_done)], inc=dve)
                            P.op("vector", lambda e, q=q, sub=sub_:
                                 e.tensor_tensor(
                                s_routed[:, (2*q+1)*TB + sub*SUB:
                                         (2*q+1)*TB + sub*SUB + SUB],
                                pr6[q][:, SUB:2*SUB],
                                s_routed[:, (2*q+1)*TB + sub*SUB:
                                         (2*q+1)*TB + sub*SUB + SUB],
                                op=ALU.add), inc=dve)
                        p6["pr_free"] = dve.c


                for eg_i in range(NEX // EG):
                    egs = [eg_i * EG + j for j in range(EG)]
                    slots = [ei % NSLOT for ei in egs]
                    for ei, sl in zip(egs, slots):
                        P.op("gpsimd", lambda e, ei=ei, sl=sl: e.dma_start(
                            out=wslot(s_wg, sl, 8 * EI),
                            in_=wgT.ap()[ei].rearrange("(k p) i -> p k i",
                                                       p=128)),
                            waits=[(pe, p6["slot_free"][sl])],
                            inc=sw[sl], inc_n=16)
                        P.op("gpsimd", lambda e, ei=ei, sl=sl: e.dma_start(
                            out=wslot(s_wu, sl, 8 * EI),
                            in_=wuT.ap()[ei].rearrange("(k p) i -> p k i",
                                                       p=128)),
                            inc=sw[sl], inc_n=16)
                        P.op("gpsimd", lambda e, ei=ei, sl=sl: e.dma_start(
                            out=wslot(s_wd, sl, 4 * DM),
                            in_=wdT.ap()[ei].rearrange("(k p) h -> p k h",
                                                       p=128)),
                            inc=sw[sl], inc_n=16)
                        p6["slot_ready"][sl] = sw[sl].c
                    cbc_dve = 0
                    cbc_act = 0
                    for j, ei in enumerate(egs):
                        if ei < NE:
                            c_cr = P.op("scalar", lambda e, ei=ei, tb=tb, j=j:
                                        e.dma_start(
                                out=s_crow[:, j*TB:(j+1)*TB],
                                in_=cflat_spill.ap()[ei:ei+1, TB*tb:TB*(tb+1)]),
                                waits=[(scf, cflat_ready),
                                       (pe, p6["crow_free"][j])],
                                inc=scr[j], inc_n=16)
                            pe_o1 = P.op("tensor", lambda e, j=j: e.matmul(
                                pg[0][:, :], s_onr[:, :],
                                s_crow[:, j*TB:j*TB + 512],
                                start=True, stop=True),
                                waits=[(scr[j], c_cr),
                                       (act, p6["pg_free"][0]),
                                       (dve, p6["cbc_free"])], inc=pe)
                            pe_o2 = P.op("tensor", lambda e, j=j: e.matmul(
                                pg[1][:, :], s_onr[:, :],
                                s_crow[:, j*TB + 512:(j+1)*TB],
                                start=True, stop=True),
                                waits=[(act, p6["pg_free"][1])], inc=pe)
                            p6["crow_free"][j] = pe_o2
                            a1 = P.op("scalar", lambda e, j=j: e.activation(
                                s_cbc[:, j*TB:j*TB + 512], pg[0][:, :], AF.Copy),
                                waits=[(pe, pe_o1)], inc=act)
                            a2 = P.op("scalar", lambda e, j=j: e.activation(
                                s_cbc[:, j*TB + 512:(j+1)*TB], pg[1][:, :],
                                AF.Copy),
                                waits=[(pe, pe_o2)], inc=act)
                            p6["pg_free"][0] = a1
                            p6["pg_free"][1] = a2
                            cbc_act = a2
                        else:
                            cbc_dve = P.op("vector", lambda e, j=j: e.memset(
                                s_cbc[:, j*TB:(j+1)*TB], 1.0),
                                waits=[(dve, p6["cbc_free"])], inc=dve)

                    for sub in range(NSUB):
                        for j, ei in enumerate(egs):
                            sl = slots[j]
                            xslot = (sub * EG + j) % 2
                            for it in range(4):
                                for kd in range(8):
                                    w = []
                                    if it == 0 and kd == 0:
                                        w = [(sw[sl], p6["slot_ready"][sl]),
                                             (lhb, c_hb),
                                             (act, p6["pg_free"][0]),
                                             (act, p6["pg_free"][1])]
                                        if cbc_act:
                                            w.append((act, cbc_act))
                                        if cbc_dve:
                                            w.append((dve, cbc_dve))
                                    P.op("tensor",
                                         lambda e, it=it, kd=kd, sl=sl, sub=sub:
                                         e.matmul(
                                        pg[it // 2][:, (it % 2)*SUB:
                                                    (it % 2 + 1)*SUB],
                                        wslot(s_wg, sl, 8*EI)[
                                            :, kd*EI + it*128: kd*EI + (it+1)*128],
                                        s_hb6[:, kd*TB + sub*SUB:
                                              kd*TB + (sub+1)*SUB],
                                        start=(kd == 0 and it % 2 == 0),
                                        stop=(kd == 7 and it % 2 == 1)),
                                        waits=w,
                                        inc=pe if (kd == 7 and it % 2 == 1)
                                        else None)
                            pe_g = pe.c
                            for it in range(4):
                                for kd in range(8):
                                    w = []
                                    if it == 0 and kd == 0:
                                        w = [(dve, p6["pu_free"][0]),
                                             (dve, p6["pu_free"][1])]
                                    P.op("tensor",
                                         lambda e, it=it, kd=kd, sl=sl, sub=sub:
                                         e.matmul(
                                        pu[it // 2][:, (it % 2)*SUB:
                                                    (it % 2 + 1)*SUB],
                                        wslot(s_wu, sl, 8*EI)[
                                            :, kd*EI + it*128: kd*EI + (it+1)*128],
                                        s_hb6[:, kd*TB + sub*SUB:
                                              kd*TB + (sub+1)*SUB],
                                        start=(kd == 0 and it % 2 == 0),
                                        stop=(kd == 7 and it % 2 == 1)),
                                        waits=w,
                                        inc=pe if (kd == 7 and it % 2 == 1)
                                        else None)
                            pe_u = pe.c
                            if pend_dn:
                                emit_down(P, pend_dn.pop(0))
                            a_sg1 = P.op("scalar", lambda e: e.activation(
                                s_sg6[:, 0:512], pg[0][:, :], AF.Silu),
                                waits=[(pe, pe_g), (dve, p6["sg_free"])],
                                inc=act)
                            a_sg2 = P.op("scalar", lambda e: e.activation(
                                s_sg6[:, 512:1024], pg[1][:, :], AF.Silu),
                                inc=act)
                            p6["pg_free"][0] = a_sg1
                            p6["pg_free"][1] = a_sg2
                            d_t1 = P.op("vector", lambda e: e.tensor_tensor(
                                s_sg6[:, 0:512], pu[0][:, :], s_sg6[:, 0:512],
                                op=ALU.mult),
                                waits=[(pe, pe_u), (act, a_sg1)], inc=dve)
                            d_t2 = P.op("vector", lambda e: e.tensor_tensor(
                                s_sg6[:, 512:1024], pu[1][:, :],
                                s_sg6[:, 512:1024], op=ALU.mult),
                                waits=[(act, a_sg2)], inc=dve)
                            p6["pu_free"][0] = d_t1
                            p6["pu_free"][1] = d_t2
                            d_x = None
                            for it in range(4):
                                d_x = P.op("vector",
                                           lambda e, it=it, j=j, sub=sub,
                                           xs=xslot: e.tensor_tensor(
                                    s_x6[:, xs*TB + it*SUB:
                                         xs*TB + (it+1)*SUB],
                                    s_sg6[:, it*SUB:(it+1)*SUB],
                                    s_cbc[:, j*TB + sub*SUB:
                                          j*TB + sub*SUB + SUB],
                                    op=ALU.mult),
                                    waits=([(dve, d_t2),
                                            (pe, p6["x_free"][xslot])]
                                           if it == 0 else ()), inc=dve)
                            p6["sg_free"] = d_x
                            pend_dn.append((j, sub, sl, xslot, d_x,
                                            eg_i == 0 and sub == 0))
                    p6["cbc_free"] = dve.c
                if True:
                    while pend_dn:
                        emit_down(P, pend_dn.pop(0))

                for ht in range(8):
                    hbuf = ht % 2
                    c_xr = P.op("sync", lambda e, ht=ht, tb=tb, hbuf=hbuf:
                                e.dma_start(
                        out=s_xr6[:, hbuf*TB:(hbuf+1)*TB],
                        in_=xres_spill.ap()[128*ht:128*(ht+1), TB*tb:TB*(tb+1)]),
                        waits=[(dve, p6["xr_free"][hbuf]),
                               (ssp[0], xres_all[0]), (ssp[1], xres_all[1])],
                        inc=lxr[hbuf], inc_n=16)
                    d_p2 = P.op("vector", lambda e, ht=ht, hbuf=hbuf:
                                e.tensor_tensor(
                        s_routed[:, ht*TB:(ht+1)*TB],
                        s_routed[:, ht*TB:(ht+1)*TB],
                        s_xr6[:, hbuf*TB:(hbuf+1)*TB], op=ALU.add),
                        waits=[(lxr[hbuf], c_xr)], inc=dve)
                    p6["xr_free"][hbuf] = d_p2
                    P.op("scalar", lambda e, ht=ht, tb=tb: e.dma_start(
                        out=pre2_spill.ap()[128*ht:128*(ht+1), TB*tb:TB*(tb+1)],
                        in_=s_routed[:, ht*TB:(ht+1)*TB]),
                        waits=[(dve, d_p2)], inc=sp2, inc_n=16)
                p6["routed_free"] = sp2.c
                p6["hb_free"] = pe.c

            pre2_done = sp2.c
            bar6 = barrier(P)
            with nc.Block("ph6") as blk:
                P.emit(blk)

        P = new_phase()
        phase_wait(P, bar6)
        p7ctx = ExitStack()
        with p7ctx:
            def sb(name, shape, dt=F32):
                return p7ctx.enter_context(nc.sbuf_tensor(name, list(shape), dt))

            s_p7 = sb("s_p7", (128, 2 * 8 * CK))
            s_q7 = sb("s_q7", (128, 8 * CK))
            s_out7 = sb("s_out7", (128, 2 * DM))
            s_rs7 = sb("s_rs7", (1, CK))
            s_nm7 = sb("s_nm7", (1, CK))
            s_sclr = sb("s_sclr", (128, 1))
            s_sclm = sb("s_sclm", (128, 1))
            s_r17 = sb("s_r17", (1, CK))
            s_mu7 = sb("s_mu7", (1, CK))
            s_m27 = sb("s_m27", (1, CK))
            s_v7 = sb("s_v7", (1, CK))
            p_t7 = [p7ctx.enter_context(nc.psum_tensor(f"p_t7{i}",
                                                       [128, 128], F32))
                    for i in range(2)]
            p_s17 = p7ctx.enter_context(nc.psum_tensor("p_s17", [1, CK], F32))
            p_s27 = p7ctx.enter_context(nc.psum_tensor("p_s27", [1, CK], F32))
            p_sc7 = p7ctx.enter_context(nc.psum_tensor("p_sc7", [128, 2], F32))

            p7 = {"p7_free": [0, 0], "out_free": [0, 0], "q_free": 0,
                  "tr_free": [0, 0], "scl_free": 0, "ps1_free": 0,
                  "ps2_free": 0}
            for cko in range(CKPC):
                buf = cko % 2
                c_p2 = P.op("sync", lambda e, cko=cko, buf=buf: e.dma_start(
                    out=s_p7[:, buf*8*CK:(buf + 1)*8*CK],
                    in_=pre2_spill.ap()[:, CK*cko:CK*(cko+1)].rearrange(
                        "(dt p) t -> p dt t", p=128)),
                    waits=[(sp2, pre2_done), (pe, p7["p7_free"][buf])],
                    inc=lp2[buf], inc_n=16)
                a_sq = None
                for d in range(8):
                    a_sq = P.op("scalar", lambda e, d=d, buf=buf: e.activation(
                        s_q7[:, d*CK:(d+1)*CK],
                        s_p7[:, (buf*8 + d)*CK:(buf*8 + d+1)*CK], AF.Square),
                        waits=([(lp2[buf], c_p2), (pe, p7["q_free"])]
                               if d == 0 else ()), inc=act)
                for d in range(8):
                    P.op("tensor", lambda e, d=d, buf=buf: e.matmul(
                        p_s17[:, :], s_onc32[:, :],
                        s_p7[:, (buf*8 + d)*CK:(buf*8 + d+1)*CK],
                        start=(d == 0), stop=(d == 7)),
                        waits=([(lp2[buf], c_p2), (dve, p7["ps1_free"])]
                               if d == 0 else ()),
                        inc=pe if d == 7 else None)
                pe_s1 = pe.c
                for d in range(8):
                    P.op("tensor", lambda e, d=d: e.matmul(
                        p_s27[:, :], s_onc32[:, :], s_q7[:, d*CK:(d+1)*CK],
                        start=(d == 0), stop=(d == 7)),
                        waits=([(act, a_sq), (dve, p7["ps2_free"])]
                               if d == 0 else ()),
                        inc=pe if d == 7 else None)
                pe_s2 = pe.c
                p7["q_free"] = pe_s2
                d_mu = P.op("vector", lambda e: e.tensor_scalar(
                    s_mu7[:, :], p_s17[0:1, :], 1.0 / DM, None, op0=ALU.mult),
                    waits=[(pe, pe_s1)], inc=dve)
                p7["ps1_free"] = d_mu
                P.op("vector", lambda e: e.tensor_tensor(
                    s_m27[:, :], s_mu7[:, :], s_mu7[:, :], op=ALU.mult), inc=dve)
                d_v = P.op("vector", lambda e: e.scalar_tensor_tensor(
                    s_v7[:, :], p_s27[0:1, :], 1.0 / DM, s_m27[:, :],
                    op0=ALU.mult, op1=ALU.subtract),
                    waits=[(pe, pe_s2)], inc=dve)
                p7["ps2_free"] = d_v
                P.op("scalar", lambda e: e.activation(
                    s_r17[:, :], s_v7[:, :], AF.Ln, bias=1e-5),
                    waits=[(dve, d_v)], inc=act)
                a_rs = P.op("scalar", lambda e: e.activation(
                    s_rs7[:, :], s_r17[:, :], AF.Exp, scale=-0.5), inc=act)
                d_nm = P.op("vector", lambda e: e.scalar_tensor_tensor(
                    s_nm7[:, :], s_mu7[:, :], -1.0, s_rs7[:, :],
                    op0=ALU.mult, op1=ALU.mult),
                    waits=[(act, a_rs)], inc=dve)
                for tt in range(4):
                    obuf = tt % 2
                    pe_sc1 = P.op("tensor", lambda e, tt=tt: e.transpose(
                        p_sc7[:, 0:1], s_rs7[:, tt*128:(tt+1)*128], s_id[0:1, 0:1]),
                        waits=[(act, a_rs), (dve, d_nm),
                               (dve, p7["scl_free"])], inc=pe)
                    pe_sc2 = P.op("tensor", lambda e, tt=tt: e.transpose(
                        p_sc7[:, 1:2], s_nm7[:, tt*128:(tt+1)*128], s_id[0:1, 0:1]),
                        inc=pe)
                    a_sc1 = P.op("scalar", lambda e: e.activation(
                        s_sclr[:, :], p_sc7[:, 0:1], AF.Copy),
                        waits=[(pe, pe_sc2)], inc=act)
                    a_scl = P.op("scalar", lambda e: e.activation(
                        s_sclm[:, :], p_sc7[:, 1:2], AF.Copy), inc=act)
                    p7["scl_free"] = a_scl
                    for d in range(8):
                        dbuf = d % 2
                        pe_tr = P.op("tensor",
                                     lambda e, d=d, tt=tt, buf=buf, dbuf=dbuf:
                                     e.transpose(
                            p_t7[dbuf][:, :],
                            s_p7[:, (buf*8 + d)*CK + tt*128:
                                 (buf*8 + d)*CK + (tt+1)*128],
                            s_id[:, :]),
                            waits=[(dve, p7["tr_free"][dbuf])], inc=pe)
                        a_o = P.op("vector",
                                   lambda e, d=d, dbuf=dbuf, obuf=obuf:
                                   e.tensor_scalar(
                            s_out7[:, obuf*DM + d*128: obuf*DM + (d+1)*128],
                            p_t7[dbuf][:, :], s_sclr[:, 0:1], s_sclm[:, 0:1],
                            op0=ALU.mult, op1=ALU.add),
                            waits=[(pe, pe_tr), (act, a_scl)] +
                                  ([(so[obuf], p7["out_free"][obuf])]
                                   if d == 0 else []),
                            inc=dve)
                        p7["tr_free"][dbuf] = a_o
                    p7["scl_free"] = dve.c
                    if tt == 3:
                        p7["p7_free"][buf] = pe.c
                    a_last = dve.c
                    P.op("sync", lambda e, cko=cko, tt=tt, obuf=obuf:
                         e.dma_start(
                        out=out_h.ap()[cko*CK + tt*128: cko*CK + (tt+1)*128, :],
                        in_=s_out7[:, obuf*DM:(obuf+1)*DM]),
                        waits=[(dve, a_last)], inc=so[obuf], inc_n=16)
                    p7["out_free"][obuf] = so[obuf].c

            P.op("sync", lambda e: None,
                 waits=[(so[0], so[0].c), (so[1], so[1].c)])
            with nc.Block("ph7") as blk:
                P.emit(blk)

    return nc


def kernel(**inputs):
    f = lambda k: np.asarray(inputs[k], np.float32)
    u = np.ascontiguousarray(f("L0_attn_u").reshape(T, DM))
    conv_w = f("L0_attn_conv_w")[:, 0, :]
    conv_b = f("L0_attn_conv_b").reshape(DM, 1)
    x_proj_w = f("L0_attn_x_proj_w")
    dt_proj_w = f("L0_attn_dt_proj_w")
    dt_proj_b = f("L0_attn_dt_proj_b")
    A_log = f("L0_attn_A_log")
    D_skip = f("L0_attn_D_skip")
    gate_w = f("L0_ffn_gate_w")
    gate_bias = f("L0_ffn_gate_bias")
    w_gate = f("L0_ffn_w_gate")
    w_up = f("L0_ffn_w_up")
    w_down = f("L0_ffn_w_down")
    sh_gate = f("L0_ffn_sh_gate")
    sh_up = f("L0_ffn_sh_up")
    sh_down = f("L0_ffn_sh_down")

    wgT = np.empty((NEX, DM, EI), np.float32)
    wuT = np.empty((NEX, DM, EI), np.float32)
    wdT = np.empty((NEX, EI, DM), np.float32)
    wgT[:NE] = w_gate.transpose(0, 2, 1)
    wuT[:NE] = w_up.transpose(0, 2, 1)
    wdT[:NE] = w_down.transpose(0, 2, 1)
    shgT = sh_gate.T
    shuT = sh_up.T
    shdT = sh_down.T
    wgT[NE] = shgT[:, :EI]; wgT[NE+1] = shgT[:, EI:]
    wuT[NE] = shuT[:, :EI]; wuT[NE+1] = shuT[:, EI:]
    wdT[NE] = shdT[:EI, :]; wdT[NE+1] = shdT[EI:, :]
    wgT = np.ascontiguousarray(wgT)
    wuT = np.ascontiguousarray(wuT)
    wdT = np.ascontiguousarray(wdT)

    selrep = np.zeros((16, 128, 128), np.float32)
    selcon = np.zeros((16, 128, 128), np.float32)
    for g in range(16):
        for n in range(16):
            for j in range(8):
                selrep[g, 8 * g + j, n * 8 + j] = 1.0
                selcon[g, n * 8 + j, 8 * g + j] = 1.0
    selbc = np.zeros((NST, 128), np.float32)
    for n in range(16):
        selbc[n, n * 8:(n + 1) * 8] = 1.0

    shared = {
        "xpwT": np.ascontiguousarray(x_proj_w.T),
        "cw": np.ascontiguousarray(conv_w),
        "cb": conv_b,
        "gatewT": np.ascontiguousarray(gate_w.T),
        "gateb": gate_bias.reshape(1, NE),
        "selrep": selrep, "selcon": selcon, "selbc": selbc,
        "ones_row": np.ones((1, 128), np.float32),
        "ones_col": np.ones((128, 1), np.float32),
        "ident": np.eye(128, dtype=np.float32),
        "wgT": wgT, "wuT": wuT, "wdT": wdT,
    }

    in_maps = []
    for c in range(NCORE):
        t0 = TPC * c
        halo = (np.zeros((3, DM), np.float32) if (t0 % LQ) == 0
                else u[t0-3:t0])
        u_tokT_a = np.ascontiguousarray(
            np.concatenate([halo, u[t0:t0+TPC]], axis=0).T)
        ch0 = CPC * c
        u_chanT_a = np.ascontiguousarray(u[:, ch0:ch0+CPC].T)
        pvec = np.arange(128)
        gvec = np.arange(16)
        alog_a = A_log[ch0 + 8 * gvec[None, :] + (pvec[:, None] % 8),
                       pvec[:, None] // 8].astype(np.float32)
        rr = np.arange(8)
        jj = np.arange(CKPC)
        kk = CKPC * c + jj
        qq = kk // 8
        kin = kk % 8
        idx = ((qq[None, None, :] * 1024 + 128 * rr[None, :, None]
                + pvec[:, None, None]) * 8 + kin[None, None, :]
               ).reshape(128, 8 * CKPC).astype(np.int32)
        m = dict(shared)
        m.update({
            "u_tokT": u_tokT_a,
            "u_chanT": u_chanT_a,
            "dtwT": np.ascontiguousarray(dt_proj_w[ch0:ch0+CPC].T),
            "dtb": dt_proj_b[ch0:ch0+CPC].reshape(CPC, 1),
            "alog": np.ascontiguousarray(alog_a),
            "dcol": D_skip[ch0:ch0+CPC].reshape(CPC, 1),
            "ckidx": np.ascontiguousarray(idx),
        })
        in_maps.append(m)

    nc = bass.Bass("TRN2", target_bir_lowering=False, debug=False)
    nc = _build(nc)
    res = run_bass_kernel_spmd(nc, in_maps, core_ids=list(range(NCORE)))
    out = np.concatenate([np.asarray(res.results[c]["out"], np.float32)
                          for c in range(NCORE)], axis=0)
    return out.reshape(B, LQ, DM)


# version 12
# speedup vs baseline: 1.3128x; 1.0000x over previous
import numpy as np
from contextlib import ExitStack

import concourse.bass as bass
import concourse.mybir as mybir
from concourse.bass_utils import run_bass_kernel_spmd
from concourse.bass_core_rust import IndirectOffsetOnAxis

F32 = mybir.dt.float32
F32R = mybir.dt.float32r
BF16 = mybir.dt.bfloat16
I32 = mybir.dt.int32
AF = mybir.ActivationFunctionType
ALU = mybir.AluOpType
AX = mybir.AxisListType

NCORE = 8
DM = 1024
NST = 16
DTR = 64
B = 2
LQ = 8192
T = B * LQ
TPC = T // NCORE
CPC = DM // NCORE
CK = 512
NCK = T // CK
CKPC = NCK // NCORE
NE = 32
EI = 512
NEX = NE + 2
TB = 1024
SUB = 256
NSUB = TB // SUB
EG = 2
NSLOT = 4

ENGINES = ("sync", "tensor", "vector", "scalar", "gpsimd")


class Sem:
    def __init__(self, ctx, nc, name):
        self.h = ctx.enter_context(nc.semaphore(name))
        self.c = 0


class Plan:
    def __init__(self, self_sems=None):
        self.ops = {e: [] for e in ENGINES}
        self.self_sems = self_sems or {}

    def op(self, eng, emit, waits=(), inc=None, inc_n=1):
        w = [(s.h, v) for (s, v) in waits if v > 0]
        ss = self.self_sems.get(eng)
        if ss is not None and ss.c > 0:
            w.append((ss.h, ss.c))
        if inc is not None:
            inc.c += inc_n
            self.ops[eng].append((w, emit, inc.h, inc_n))
            return inc.c
        self.ops[eng].append((w, emit, None, 0))
        return None

    WMARKS = {e: {} for e in ENGINES}

    def emit(self, block):
        for eng_name in ENGINES:
            lst = self.ops[eng_name]
            if not lst:
                continue

            def body(eng, lst=lst, eng_name=eng_name):
                wm = Plan.WMARKS[eng_name]
                for waits, emit, sem_h, n in lst:
                    for (h, v) in waits:
                        if wm.get(id(h), -1) >= v:
                            continue
                        eng.wait_ge(h, v)
                        wm[id(h)] = v
                    ins = emit(eng)
                    if sem_h is not None:
                        if ins is None:
                            eng.sem_inc(sem_h, n)
                        else:
                            ins.then_inc(sem_h, n)
            getattr(block, eng_name)(body)


def _build(nc):
    dram = {}

    def din(name, shape, dt=F32):
        dram[name] = nc.dram_tensor(name, list(shape), dt, kind="ExternalInput")
        return dram[name]

    u_tokT = din("u_tokT", (DM, TPC + 3), F32R)
    u_chanT = din("u_chanT", (CPC, T), F32R)
    dtwT = din("dtwT", (DTR, CPC), F32R)
    dtb = din("dtb", (CPC, 1), F32)
    alog = din("alog", (128, 16), F32)
    dcol = din("dcol", (CPC, 1), F32)
    xpwT = din("xpwT", (DM, 96), F32)
    cw = din("cw", (DM, 4), F32)
    cb = din("cb", (DM, 1), F32)
    gatewT = din("gatewT", (DM, NE), F32)
    gateb = din("gateb", (1, NE), F32)
    selrep = din("selrep", (16, 128, 128), F32R)
    selcon = din("selcon", (16, 128, 128), F32R)
    selbc = din("selbc", (NST, 128), F32R)
    ones_row = din("ones_row", (1, 128), F32R)
    ones_col = din("ones_col", (128, 1), F32R)
    ident = din("ident", (128, 128), F32)
    ckidx = din("ckidx", (128, 8 * CKPC), I32)
    wgT = din("wgT", (NEX, DM, EI), F32)
    wuT = din("wuT", (NEX, DM, EI), F32)
    wdT = din("wdT", (NEX, EI, DM), F32)
    out_h = nc.dram_tensor("out", [TPC, DM], F32, kind="ExternalOutput")

    cc_prep_in = nc.dram_tensor("cc_prep_in", [96, TPC], F32R, kind="Internal")
    cc_prep_out = nc.dram_tensor("cc_prep_out", [96 * NCORE, TPC], F32R,
                                 kind="Internal")
    cc_y_in = nc.dram_tensor("cc_y_in", [4, CPC, 4096], F32R, kind="Internal")
    cc_y_out = nc.dram_tensor("cc_y_out", [4, DM, 4096], F32R, kind="Internal")
    xres_spill = nc.dram_tensor("xres_spill", [DM, TPC], F32, kind="Internal")
    hb_spill = nc.dram_tensor("hb_spill", [DM, TPC], BF16, kind="Internal")
    pre2_spill = nc.dram_tensor("pre2_spill", [DM, TPC], F32, kind="Internal")
    cflat_spill = nc.dram_tensor("cflat_spill", [NE, TPC], F32R, kind="Internal")

    octx = ExitStack()
    with octx:
        def osb(name, shape, dt=F32):
            return octx.enter_context(nc.sbuf_tensor(name, list(shape), dt))

        def sem(name):
            return Sem(octx, nc, name)

        CONSTS = [0.0, 1.0, -1.0, -0.5, 1e-5, 2.5]
        s_const = osb("s_const", (128, len(CONSTS)))
        for i, v in enumerate(CONSTS):
            nc.const_aps.register(F32, v, s_const[:, i:i + 1])
        s_acol = osb("s_acol", (128, 16))
        s_ainv = osb("s_ainv", (128, 16))
        s_dp1 = osb("s_dp1", (CPC, 1))
        s_dtb = osb("s_dtb", (CPC, 1))
        s_carry = osb("s_carry", (128, 16))
        s_zero = osb("s_zero", (128, 1))
        s_onr = osb("s_onr", (1, 128), F32R)
        s_onc = osb("s_onc", (128, 1), F32R)
        s_onc32 = osb("s_onc32", (128, 1))
        s_onr32 = osb("s_onr32", (1, 128))
        s_id = osb("s_id", (128, 128))
        s_gw = osb("s_gw", (128, 8 * NE))
        s_gbb = osb("s_gbb", (128, NE))
        s_gbr = osb("s_gbr", (1, NE), F32)
        s_ckidx = osb("s_ckidx", (128, 8 * CKPC), I32)
        s_tmp16 = osb("s_tmp16", (128, 16))

        ld0 = sem("ld0")
        ld1 = sem("ld1")
        lu = [sem("lu0"), sem("lu1")]
        ldt = [sem("ldt0"), sem("ldt1")]
        lbc = [sem("lbc0"), sem("lbc1")]
        ys = [sem("ys0"), sem("ys1")]
        lyg = [sem("lyg0"), sem("lyg1")]
        ssp = [sem("ssp0"), sem("ssp1")]
        shb = sem("shb")
        scf = sem("scf")
        stp = [sem("stp0"), sem("stp1")]
        sw = [sem(f"sw{i}") for i in range(NSLOT)]
        scr = [sem("scr0"), sem("scr1")]
        lhb = sem("lhb")
        lxr = [sem("lxr0"), sem("lxr1")]
        sp2 = sem("sp2")
        lp2 = [sem("lp20"), sem("lp21")]
        so = [sem("so0"), sem("so1")]
        cc = sem("cc")
        pe = sem("pe")
        act = sem("act")
        dve = sem("dve")
        pool = sem("pool")
        bar = sem("bar")

        phases = []

        def new_phase():
            P = Plan(self_sems={"vector": dve, "scalar": act, "gpsimd": pool})
            phases.append(P)
            return P

        ALL_DMA = ([ld0, ld1, lu[0], lu[1], ldt[0], ldt[1], lbc[0], lbc[1],
                    ys[0], ys[1], lyg[0], lyg[1], ssp[0], ssp[1], shb, scf,
                    stp[0], stp[1]] + sw + scr + [lhb, lxr[0], lxr[1], sp2,
                    lp2[0], lp2[1], so[0], so[1]])

        def barrier(P):
            for e in ENGINES:
                P.op(e, lambda eng: None,
                     waits=[(s, s.c) for s in ALL_DMA], inc=bar)
            tgt = bar.c
            return tgt

        def phase_wait(P, tgt):
            for e in ENGINES:
                P.op(e, lambda eng: None, waits=[(bar, tgt)])

        def ld16(P, q, s, dst_fn, src_fn, waits=()):
            return P.op(q, lambda e: e.dma_start(out=dst_fn(), in_=src_fn()),
                        waits=waits, inc=s, inc_n=16)

        P = new_phase()
        p1ctx = ExitStack()
        with p1ctx:
            def sb(name, shape, dt=F32):
                return p1ctx.enter_context(nc.sbuf_tensor(name, list(shape), dt))

            for i, v in enumerate(CONSTS):
                P.op("vector", lambda e, i=i, v=v: e.memset(s_const[:, i:i+1], v),
                     inc=dve)
            P.op("vector", lambda e: e.memset(s_zero[:, :], 0.0), inc=dve)
            P.op("vector", lambda e: e.memset(s_carry[:, :], 0.0), inc=dve)
            P.op("vector", lambda e: e.memset(s_onc32[:, :], 1.0), inc=dve)
            P.op("vector", lambda e: e.memset(s_onr32[:, :], 1.0), inc=dve)
            const_ready = dve.c

            ld16(P, "sync", ld0, lambda: s_onr[:, :], lambda: ones_row.ap())
            ld16(P, "sync", ld0, lambda: s_onc[:, :], lambda: ones_col.ap())
            ld16(P, "sync", ld0, lambda: s_id[:, :], lambda: ident.ap())
            ld16(P, "sync", ld0, lambda: s_ckidx[:, :], lambda: ckidx.ap())
            ld16(P, "sync", ld0, lambda: s_gbr[:, :], lambda: gateb.ap())
            for d in range(8):
                ld16(P, "sync", ld0, lambda d=d: s_gw[:, NE*d:NE*(d+1)],
                     lambda d=d: gatewT.ap()[128*d:128*(d+1), :])
            ld16(P, "sync", ld0, lambda: s_tmp16[:, :], lambda: alog.ap())
            s_dcol_t = sb("s_dcol_t", (CPC, 1))
            ld16(P, "sync", ld0, lambda: s_dcol_t[:, :], lambda: dcol.ap())
            ld16(P, "sync", ld0, lambda: s_dtb[:, :], lambda: dtb.ap())
            c_small = ld0.c

            a1 = P.op("scalar", lambda e: e.activation(
                s_acol[:, :], s_tmp16[:, :], AF.Exp),
                waits=[(ld0, c_small), (dve, const_ready)], inc=act)
            a2 = P.op("scalar", lambda e: e.activation(
                s_ainv[:, :], s_tmp16[:, :], AF.Exp, scale=-1.0), inc=act)
            P.op("vector", lambda e: e.tensor_scalar(
                s_acol[:, :], s_acol[:, :], -1.0, None, op0=ALU.mult),
                waits=[(act, a2)], inc=dve)
            ainv_ready = P.op("vector", lambda e: e.tensor_scalar(
                s_ainv[:, :], s_ainv[:, :], -1.0, None, op0=ALU.mult), inc=dve)
            acol_ready = ainv_ready
            dp1_ready = P.op("vector", lambda e: e.tensor_scalar(
                s_dp1[:, :], s_dcol_t[:, :], 1.0, None, op0=ALU.add),
                waits=[(ld0, c_small)], inc=dve)

            UW = TPC + 3
            s_utok = sb("s_utok", (128, 8 * UW), F32R)
            s_xpw = sb("s_xpw", (128, 8 * 96))
            s_cw = sb("s_cw", (128, 8 * 4))
            s_cb = sb("s_cb", (128, 8))
            s_wk = sb("s_wk", (128, 8 * 4 * 96), F32R)
            s_c96 = sb("s_c96", (96, 1))
            s_proj = sb("s_proj", (96, 2 * CK), F32R)
            pp = p1ctx.enter_context(nc.psum_tensor("pp", [96, CK], F32))
            pc96 = p1ctx.enter_context(nc.psum_tensor("pc96", [96, 1], F32))

            for d in range(8):
                ld16(P, "sync", ld1, lambda d=d: s_utok[:, UW*d:UW*(d+1)],
                     lambda d=d: u_tokT.ap()[128*d:128*(d+1), :])
                ld16(P, "sync", ld1, lambda d=d: s_xpw[:, 96*d:96*(d+1)],
                     lambda d=d: xpwT.ap()[128*d:128*(d+1), :])
                ld16(P, "sync", ld1, lambda d=d: s_cw[:, 4*d:4*(d+1)],
                     lambda d=d: cw.ap()[128*d:128*(d+1), :])
                ld16(P, "sync", ld1, lambda d=d: s_cb[:, d:d+1],
                     lambda d=d: cb.ap()[128*d:128*(d+1), :])
            c_p1 = ld1.c

            for d in range(8):
                for k in range(4):
                    P.op("vector", lambda e, d=d, k=k: e.tensor_scalar(
                        s_wk[:, d*384 + k*96: d*384 + (k+1)*96],
                        s_xpw[:, 96*d:96*(d+1)], s_cw[:, 4*d+k:4*d+k+1], None,
                        op0=ALU.mult),
                        waits=[(ld1, c_p1)] if (d == 0 and k == 0) else (),
                        inc=dve)
            wk_ready = dve.c

            for d in range(8):
                P.op("tensor", lambda e, d=d: e.matmul(
                    pc96[:, :], s_xpw[:, 96*d:96*(d+1)], s_cb[:, d:d+1],
                    start=(d == 0), stop=(d == 7)),
                    waits=[(dve, wk_ready)] if d == 0 else (),
                    inc=pe if d == 7 else None)
            c96_mm = pe.c
            c96_ready = P.op("scalar", lambda e: e.copy(s_c96[:, :], pc96[:, :]),
                             waits=[(pe, c96_mm)], inc=act)

            prev_evac = 0
            for ck_i in range(CKPC):
                buf = ck_i % 2
                for d in range(8):
                    for k in range(4):
                        col0 = UW*d + ck_i*CK + k
                        w = [(dve, prev_evac)] if (d == 0 and k == 0) else ()
                        P.op("tensor", lambda e, d=d, k=k, col0=col0: e.matmul(
                            pp[:, :], s_wk[:, d*384 + k*96: d*384 + (k+1)*96],
                            s_utok[:, col0:col0+CK],
                            start=(d == 0 and k == 0), stop=(d == 7 and k == 3)),
                            waits=w, inc=pe if (d == 7 and k == 3) else None)
                mm_done = pe.c
                prev_evac = P.op("vector", lambda e, buf=buf: e.tensor_scalar(
                    s_proj[:, CK*buf:CK*(buf+1)], pp[:, :], s_c96[:, :1], None,
                    op0=ALU.add),
                    waits=[(pe, mm_done), (act, c96_ready),
                           (stp[buf], stp[buf].c)], inc=dve)
                P.op("sync", lambda e, ck_i=ck_i, buf=buf: e.dma_start(
                    out=cc_prep_in.ap()[:, CK*ck_i:CK*(ck_i+1)],
                    in_=s_proj[:, CK*buf:CK*(buf+1)]),
                    waits=[(dve, prev_evac)], inc=stp[buf], inc_n=16)

            cc_prep_done = P.op("gpsimd", lambda e: e.collective_compute(
                "AllGather", ALU.bypass, replica_groups=[list(range(NCORE))],
                ins=[cc_prep_in.ap().opt()], outs=[cc_prep_out.ap().opt()]),
                waits=[(stp[0], stp[0].c), (stp[1], stp[1].c)], inc=cc)
            bar1 = barrier(P)
            with nc.Block("ph1") as blk:
                P.emit(blk)

        P = new_phase()
        phase_wait(P, bar1)
        p3ctx = ExitStack()
        with p3ctx:
            def sb(name, shape, dt=F32):
                return p3ctx.enter_context(nc.sbuf_tensor(name, list(shape), dt))

            s_u3 = sb("s_u3", (128, 2 * CK), F32R)
            s_dt3 = sb("s_dt3", (64, 2 * CK), F32R)
            s_bt3 = sb("s_bt3", (NST, 2 * CK), F32R)
            s_ct3 = sb("s_ct3", (NST, 2 * CK), F32R)
            s_selr = sb("s_selr", (128, 16 * 128), F32R)
            s_selc = sb("s_selc", (128, 16 * 128), F32R)
            s_selb = sb("s_selb", (NST, 128), F32R)
            s_dtw = sb("s_dtw", (DTR, CPC), F32R)
            s_sp = sb("s_sp", (128, CK))
            s_delta = sb("s_delta", (128, 2 * CK), F32R)
            RDA, RW, RT = 6, 4, 8
            s_dA = sb("s_dA", (128, RDA * CK))
            s_w1 = sb("s_w1", (128, RW * CK))
            s_w2 = sb("s_w2", (128, RW * CK))
            s_hh = sb("s_hh", (128, 32 * CK))
            s_t1 = sb("s_t1", (128, RT * CK), F32R)
            s_brep = sb("s_brep", (128, 2 * CK))
            s_crep = sb("s_crep", (128, 2 * CK))
            s_y3 = sb("s_y3", (128, 2 * CK), F32R)
            s_tp = sb("s_tp", (128, 2 * CK))
            psh = p3ctx.enter_context(nc.psum_tensor("psh", [128, CK], F32))
            pdr = [p3ctx.enter_context(nc.psum_tensor(f"pdr{i}", [128, CK], F32))
                   for i in range(2)]
            pur = [p3ctx.enter_context(nc.psum_tensor(f"pur{i}", [128, CK], F32))
                   for i in range(3)]
            py3 = [p3ctx.enter_context(nc.psum_tensor(f"py3{i}", [128, CK], F32))
                   for i in range(2)]

            for g in range(16):
                ld16(P, "sync", ld1, lambda g=g: s_selr[:, 128*g:128*(g+1)],
                     lambda g=g: selrep.ap()[g, :, :])
                ld16(P, "sync", ld1, lambda g=g: s_selc[:, 128*g:128*(g+1)],
                     lambda g=g: selcon.ap()[g, :, :])
            ld16(P, "sync", ld1, lambda: s_selb[:, :], lambda: selbc.ap())
            ld16(P, "sync", ld1, lambda: s_dtw[:, :], lambda: dtwT.ap())
            sel_ready = ld1.c

            ev = {
                "w2_free": [0] * RW,
                "t1_free": [0] * RT,
                "u_free": [0, 0],
                "dt_free": [0, 0],
                "bc_free": [0, 0],
                "delta_free": [0, 0],
                "brep_free": [0, 0],
                "crep_free": [(0, 0), (0, 0)],
                "psh_free_act": 0,
                "pdr_free": [0] * 2,
                "pur_free": [0] * 3,
                "py_free": [0, 0],
                "y_free": [0, 0],
                "w1s_free": [0] * RW,
                "loads": {},
                "spq": [],
                "a": {},
            }
            dA_free = [0] * RDA
            CLAG = 4
            pend_con = []

            def hh_col(ck, g):
                return ((ck % 2) * 16 + g) * CK

            def emit_loads(Pp, ck):
                buf = ck % 2
                rk, cko = ck // CKPC, ck % CKPC
                c_u = Pp.op("sync", lambda e, ck=ck, buf=buf: e.dma_start(
                    out=s_u3[:, CK*buf:CK*(buf+1)],
                    in_=u_chanT.ap()[:, CK*ck:CK*(ck+1)]),
                    waits=[(dve, ev["u_free"][buf])], inc=lu[buf], inc_n=16)
                c_dt = Pp.op("sync", lambda e, rk=rk, cko=cko, buf=buf:
                             e.dma_start(
                    out=s_dt3[:, CK*buf:CK*(buf+1)],
                    in_=cc_prep_out.ap()[96*rk:96*rk+64, CK*cko:CK*(cko+1)]),
                    waits=[(cc, cc_prep_done), (pe, ev["dt_free"][buf])],
                    inc=ldt[buf], inc_n=16)
                Pp.op("sync", lambda e, rk=rk, cko=cko, buf=buf: e.dma_start(
                    out=s_bt3[:, CK*buf:CK*(buf+1)],
                    in_=cc_prep_out.ap()[96*rk+64:96*rk+80, CK*cko:CK*(cko+1)]),
                    waits=[(pe, ev["bc_free"][buf])], inc=lbc[buf], inc_n=16)
                c_c = Pp.op("sync", lambda e, rk=rk, cko=cko, buf=buf:
                            e.dma_start(
                    out=s_ct3[:, CK*buf:CK*(buf+1)],
                    in_=cc_prep_out.ap()[96*rk+80:96*rk+96, CK*cko:CK*(cko+1)]),
                    inc=lbc[buf], inc_n=16)
                ev["loads"][ck] = (c_u, c_dt, c_c)

            def emit_prep1(Pp, ck):
                buf = ck % 2
                (c_u, c_dt, c_c) = ev["loads"][ck]
                pe_d = Pp.op("tensor", lambda e, buf=buf: e.matmul(
                    psh[:, :], s_dtw[:, :], s_dt3[:, CK*buf:CK*(buf+1)],
                    start=True, stop=True),
                    waits=[(ldt[buf], c_dt), (ld1, sel_ready),
                           (act, ev["psh_free_act"])], inc=pe)
                ev["dt_free"][buf] = pe.c
                a_sp = Pp.op("scalar", lambda e: e.activation(
                    s_sp[:, :], psh[:, :], AF.Exp, bias=s_dtb[:, :1]),
                    waits=[(pe, pe_d)], inc=act)
                ev["a"].setdefault(ck, {})["a_sp"] = a_sp

            def emit_prep2(Pp, ck):
                buf = ck % 2
                (c_u, c_dt, c_c) = ev["loads"][ck]
                a_sp = ev["a"][ck]["a_sp"]
                a_dl = Pp.op("scalar", lambda e, buf=buf: e.activation(
                    s_delta[:, CK*buf:CK*(buf+1)], s_sp[:, :], AF.Ln, bias=1.0),
                    waits=[(pe, ev["delta_free"][buf])], inc=act)
                pe_b = Pp.op("tensor", lambda e, buf=buf: e.matmul(
                    psh[:, :], s_selb[:, :], s_bt3[:, CK*buf:CK*(buf+1)],
                    start=True, stop=True),
                    waits=[(lbc[buf], c_c), (act, a_sp)], inc=pe)
                a_b = Pp.op("scalar", lambda e, buf=buf: e.activation(
                    s_brep[:, CK*buf:CK*(buf+1)], psh[:, :], AF.Copy),
                    waits=[(pe, pe_b), (pool, ev["brep_free"][buf])], inc=act)
                ev["a"][ck].update(a_dl=a_dl, a_b=a_b)

            def emit_prep3(Pp, ck):
                buf = ck % 2
                a_b = ev["a"][ck]["a_b"]
                pe_c = Pp.op("tensor", lambda e, buf=buf: e.matmul(
                    psh[:, :], s_selb[:, :], s_ct3[:, CK*buf:CK*(buf+1)],
                    start=True, stop=True),
                    waits=[(act, a_b)], inc=pe)
                ev["bc_free"][buf] = pe.c
                a_c = Pp.op("scalar", lambda e, buf=buf: e.activation(
                    s_crep[:, CK*buf:CK*(buf+1)], psh[:, :], AF.Copy),
                    waits=[(pe, pe_c), (dve, ev["crep_free"][buf][0]),
                           (pool, ev["crep_free"][buf][1])], inc=act)
                ev["psh_free_act"] = a_c
                ev["a"][ck]["a_c"] = a_c

            def emit_con(Pp, item):
                g, gt, ck, d_t1, t1_sem = item
                pyb = ck % 2
                pe_con = Pp.op("tensor", lambda e, g=g, gt=gt, pyb=pyb: e.matmul(
                    py3[pyb][:, :], s_selc[:, 128*g:128*(g+1)],
                    s_t1[:, CK*gt:CK*(gt+1)],
                    start=(g == 0), stop=(g == 15)),
                    waits=[(t1_sem, d_t1)] +
                          ([(dve, ev["py_free"][pyb])] if g == 0 else []),
                    inc=pe)
                ev["t1_free"][gt] = pe_con
                if g == 15:
                    buf = ck % 2
                    d_y = Pp.op("vector", lambda e, buf=buf, pyb=pyb:
                                e.scalar_tensor_tensor(
                        s_y3[:, CK*buf:CK*(buf+1)],
                        s_u3[:, CK*buf:CK*(buf+1)].bitcast(F32),
                        s_dp1[:, :1], py3[pyb][:, :],
                        op0=ALU.mult, op1=ALU.add),
                        waits=[(pe, pe_con), (dve, dp1_ready),
                               (ys[buf], ev["y_free"][buf])], inc=dve)
                    ev["u_free"][buf] = d_y
                    ev["py_free"][pyb] = d_y
                    Pp.op("gpsimd", lambda e, ck=ck, buf=buf: e.dma_start(
                        out=cc_y_in.ap()[ck // 8][:, (ck % 8)*CK:
                                                  (ck % 8 + 1)*CK],
                        in_=s_y3[:, CK*buf:CK*(buf+1)]),
                        waits=[(dve, d_y)], inc=ys[buf], inc_n=16)
                    ev["y_free"][buf] = ys[buf].c
                    if ck % 8 == 7:
                        q = ck // 8
                        Pp.op("gpsimd", lambda e, q=q: e.collective_compute(
                            "AllGather", ALU.bypass,
                            replica_groups=[list(range(NCORE))],
                            ins=[cc_y_in.ap()[q].opt()],
                            outs=[cc_y_out.ap()[q].opt()]),
                            waits=[(ys[0], ys[0].c), (ys[1], ys[1].c)], inc=cc)

            pend_scan = []

            def emit_scan_t1(Pp, it):
                g, gw, gda, gt = it["g"], it["gw"], it["gda"], it["gt"]
                buf, ckk = it["buf"], it["ckk"]
                if ckk in (0, NCK // 2):
                    init_ap = s_zero[:, 0:1]
                else:
                    pc = hh_col(ckk - 1, g)
                    init_ap = s_hh[:, pc + CK - 1: pc + CK]
                hc = hh_col(ckk, g)
                d_sc = Pp.op("vector", lambda e, hc=hc, gda=gda, gw=gw,
                             init_ap=init_ap: e.tensor_tensor_scan(
                    s_hh[:, hc:hc + CK], s_dA[:, CK*gda:CK*(gda+1)],
                    s_w2[:, CK*gw:CK*gw+CK],
                    init_ap, op0=ALU.mult, op1=ALU.add),
                    waits=[(pool, it["p_w2"])], inc=dve)
                ev["w2_free"][gw] = d_sc
                dA_free[gda] = d_sc
                d_t1 = Pp.op("vector", lambda e, hc=hc, gt=gt, g=g,
                             buf=buf: e.scalar_tensor_tensor(
                    s_t1[:, CK*gt:CK*(gt+1)], s_hh[:, hc:hc + CK],
                    s_ainv[:, g:g+1], s_crep[:, CK*buf:CK*(buf+1)],
                    op0=ALU.mult, op1=ALU.mult),
                    waits=[(act, it["a_c"]), (pe, ev["t1_free"][gt]),
                           (dve, ainv_ready)],
                    inc=dve)
                if g >= 14:
                    ev["crep_free"][buf] = (d_t1, 0)
                pend_con.append((g, gt, ckk, d_t1, dve))

            emit_loads(P, 0)
            emit_prep1(P, 0)
            emit_prep2(P, 0)
            emit_prep3(P, 0)
            for ckk in range(NCK):
                buf = ckk % 2
                a_dl = ev["a"][ckk]["a_dl"]
                a_c = ev["a"][ckk]["a_c"]
                (c_u, c_dt, c_c) = ev["loads"][ckk]
                for g in range(16):
                    gw = g % RW
                    gda = g % RDA
                    gt = (ckk * 16 + g) % RT
                    gp2 = g % 2
                    gp3 = g % 3
                    if len(pend_con) > CLAG:
                        emit_con(P, pend_con.pop(0))
                    if ckk + 1 < NCK:
                        if g == 6:
                            emit_loads(P, ckk + 1)
                        elif g == 8:
                            emit_prep1(P, ckk + 1)
                        elif g == 10:
                            emit_prep2(P, ckk + 1)
                        elif g == 12:
                            emit_prep3(P, ckk + 1)
                    pe_dr = P.op("tensor", lambda e, g=g, buf=buf, gp2=gp2:
                                 e.matmul(
                        pdr[gp2][:, :], s_selr[:, 128*g:128*(g+1)],
                        s_delta[:, CK*buf:CK*(buf+1)], start=True, stop=True),
                        waits=[(act, a_dl), (act, ev["pdr_free"][gp2])], inc=pe)
                    if g == 15:
                        ev["delta_free"][buf] = pe.c
                    pe_ur = P.op("tensor", lambda e, g=g, buf=buf, gp3=gp3:
                                 e.matmul(
                        pur[gp3][:, :], s_selr[:, 128*g:128*(g+1)],
                        s_u3[:, CK*buf:CK*(buf+1)], start=True, stop=True),
                        waits=[(lu[buf], c_u), (dve, ev["pur_free"][gp3])],
                        inc=pe)
                    a_dA = P.op("scalar", lambda e, g=g, gp2=gp2, gda=gda:
                                e.activation(
                        s_dA[:, CK*gda:CK*(gda+1)], pdr[gp2][:, :], AF.Exp,
                        scale=s_acol[:, g:g+1]),
                        waits=[(pe, pe_dr), (dve, dA_free[gda]),
                               (dve, acol_ready)], inc=act)
                    ev["pdr_free"][gp2] = a_dA
                    d_w1 = P.op("vector", lambda e, gw=gw, gda=gda, gp3=gp3:
                                e.scalar_tensor_tensor(
                        s_w1[:, CK*gw:CK*(gw+1)], s_dA[:, CK*gda:CK*(gda+1)],
                        -1.0, pur[gp3][:, :], op0=ALU.add, op1=ALU.mult),
                        waits=[(act, a_dA), (pe, pe_ur),
                               (pool, ev["w1s_free"][gw])], inc=dve)
                    ev["pur_free"][gp3] = d_w1
                    p_w2 = P.op("gpsimd", lambda e, gw=gw, buf=buf:
                                e.tensor_tensor(
                        s_w2[:, CK*gw:CK*(gw+1)], s_w1[:, CK*gw:CK*(gw+1)],
                        s_brep[:, CK*buf:CK*(buf+1)], op=ALU.mult),
                        waits=[(dve, d_w1), (act, ev["a"][ckk]["a_b"]),
                               (dve, ev["w2_free"][gw])], inc=pool)
                    ev["w1s_free"][gw] = p_w2
                    if g >= 14:
                        ev["brep_free"][buf] = p_w2
                    pend_scan.append(dict(
                        g=g, gw=gw, gda=gda, gt=gt, buf=buf, ckk=ckk,
                        p_w2=p_w2, a_c=a_c))
                    if len(pend_scan) > 1:
                        emit_scan_t1(P, pend_scan.pop(0))
            while pend_scan:
                emit_scan_t1(P, pend_scan.pop(0))
            while pend_con:
                emit_con(P, pend_con.pop(0))
            cc_y_done = cc.c
            bar3 = barrier(P)
            with nc.Block("ph3") as blk:
                P.emit(blk)

        P = new_phase()
        phase_wait(P, bar3)
        p5ctx = ExitStack()
        with p5ctx:
            def sb(name, shape, dt=F32):
                return p5ctx.enter_context(nc.sbuf_tensor(name, list(shape), dt))

            s_yg = sb("s_yg", (128, 2 * 8 * CK), F32R)
            s_ysq = sb("s_ysq", (128, 8 * CK), F32R)
            s_h5 = sb("s_h5", (128, 2 * 8 * CK))
            s_hb5 = sb("s_hb5", (128, 8 * CK), BF16)
            s_r1 = sb("s_r1", (1, CK))
            s_mu = sb("s_mu", (1, CK))
            s_m2 = sb("s_m2", (1, CK))
            s_var = sb("s_var", (1, CK))
            s_rstd = sb("s_rstd", (1, CK), F32R)
            s_nmu = sb("s_nmu", (1, CK), F32R)
            s_sc5 = sb("s_sc5", (128, NE))
            s_sc5b = sb("s_sc5b", (128, NE))
            s_gsc = sb("s_gsc", (128, 8))
            s_gtmp = sb("s_gtmp", (128, 8))
            s_gm = sb("s_gm", (128, 1))
            s_msk = sb("s_msk", (128, NE))
            s_mskd = sb("s_mskd", (128, NE))
            s_thr = sb("s_thr", (128, 1))
            s_wsum = sb("s_wsum", (128, 1))
            s_comb = sb("s_comb", (128, NE))
            s_combT = sb("s_combT", (NE, CK), F32R)
            p_s1 = p5ctx.enter_context(nc.psum_tensor("p_s1", [1, CK], F32))
            p_s2 = p5ctx.enter_context(nc.psum_tensor("p_s2", [1, CK], F32))
            p_bc1 = p5ctx.enter_context(nc.psum_tensor("p_bc1", [128, CK], F32))
            p_bc2 = p5ctx.enter_context(nc.psum_tensor("p_bc2", [128, CK], F32))
            p_lg = p5ctx.enter_context(nc.psum_tensor("p_lg", [128, NE], F32))
            p_tr5 = p5ctx.enter_context(nc.psum_tensor("p_tr5", [NE, 128], F32))

            pe_gb = P.op("tensor", lambda e: e.matmul(
                p_bc1[:, 0:NE], s_onr32[:, :], s_gbr[:, :], start=True,
                stop=True), inc=pe)
            gbb_ready = P.op("scalar", lambda e: e.activation(
                s_gbb[:, :], p_bc1[:, 0:NE], AF.Copy),
                waits=[(pe, pe_gb)], inc=act)

            p5 = {"yg_free": [0, 0], "h5_free": [0, 0], "hb_free": 0,
                  "h5_rt": [0, 0],
                  "ps1_free": 0, "ps2_free": 0, "bc1_free": gbb_ready,
                  "bc2_free": 0, "lg_free": 0, "tr_free": 0, "combT_free": 0}
            def emit_routing(cko, buf, d_h):
                    for tt in range(4):
                        for d in range(8):
                            P.op("tensor", lambda e, d=d, tt=tt, buf=buf: e.matmul(
                                p_lg[:, :],
                                s_h5[:, (buf*8 + d)*CK + tt*128:
                                     (buf*8 + d)*CK + (tt+1)*128],
                                s_gw[:, NE*d:NE*(d+1)], start=(d == 0), stop=(d == 7)),
                                waits=([(dve, d_h), (act, p5["lg_free"])]
                                       if d == 0 else ()),
                                inc=pe if d == 7 else None)
                        pe_lg = pe.c
                        a_e = P.op("scalar", lambda e: e.activation(
                            s_sc5[:, :], p_lg[:, :], AF.Exp, scale=-1.0),
                            waits=[(pe, pe_lg), (dve, 0)], inc=act)
                        p5["lg_free"] = a_e
                        P.op("vector", lambda e: e.tensor_scalar(
                            s_sc5[:, :], s_sc5[:, :], 1.0, None, op0=ALU.add),
                            waits=[(act, a_e)], inc=dve)
                        P.op("vector", lambda e: e.reciprocal(
                            s_sc5[:, :], s_sc5[:, :]), inc=dve)
                        P.op("vector", lambda e: e.tensor_tensor(
                            s_sc5b[:, :], s_sc5[:, :], s_gbb[:, :], op=ALU.add),
                            waits=[(act, gbb_ready)], inc=dve)
                        pairs = [(0, 1), (0, 2), (0, 3), (1, 2), (1, 3), (2, 3)]
                        for pi, (i1, i2) in enumerate(pairs):
                            dst = s_gsc if pi == 0 else s_gtmp
                            P.op("vector", lambda e, i1=i1, i2=i2, dst=dst:
                                 e.tensor_tensor(
                                dst[:, :], s_sc5b[:, i1::4], s_sc5b[:, i2::4],
                                op=ALU.add), inc=dve)
                            if pi > 0:
                                P.op("vector", lambda e: e.tensor_tensor(
                                    s_gsc[:, :], s_gsc[:, :], s_gtmp[:, :],
                                    op=ALU.max), inc=dve)
                        P.op("vector", lambda e: e.tensor_copy(
                            s_gtmp[:, :], s_gsc[:, :]), inc=dve)
                        for it in range(4):
                            P.op("vector", lambda e: e.reduce_max(
                                s_gm[:, :], s_gtmp[:, :], axis=AX.X), inc=dve)
                            if it < 3:
                                P.op("vector", lambda e: e.tensor_scalar(
                                    s_msk[:, 0:8], s_gtmp[:, :], s_gm[:, :1], None,
                                    op0=ALU.is_ge), inc=dve)
                                P.op("vector", lambda e: e.scalar_tensor_tensor(
                                    s_gtmp[:, :], s_msk[:, 0:8], -1e30, s_gtmp[:, :],
                                    op0=ALU.mult, op1=ALU.add), inc=dve)
                        P.op("vector", lambda e: e.tensor_scalar(
                            s_gtmp[:, :], s_gsc[:, :], s_gm[:, :1], None,
                            op0=ALU.is_ge), inc=dve)
                        for j in range(4):
                            P.op("vector", lambda e, j=j: e.tensor_copy(
                                s_msk[:, j::4], s_gtmp[:, :]), inc=dve)
                        P.op("vector", lambda e: e.tensor_tensor(
                            s_mskd[:, :], s_sc5b[:, :], s_msk[:, :], op=ALU.mult),
                            inc=dve)
                        P.op("vector", lambda e: e.tensor_copy(
                            s_comb[:, :], s_mskd[:, :]), inc=dve)
                        for it in range(8):
                            P.op("vector", lambda e: e.reduce_max(
                                s_thr[:, :], s_comb[:, :], axis=AX.X), inc=dve)
                            if it < 7:
                                P.op("vector", lambda e: e.tensor_scalar(
                                    s_msk[:, :], s_comb[:, :], s_thr[:, :1], None,
                                    op0=ALU.is_ge), inc=dve)
                                P.op("vector", lambda e: e.scalar_tensor_tensor(
                                    s_comb[:, :], s_msk[:, :], -1e30, s_comb[:, :],
                                    op0=ALU.mult, op1=ALU.add), inc=dve)
                        P.op("vector", lambda e: e.tensor_scalar(
                            s_msk[:, :], s_mskd[:, :], s_thr[:, :1], None,
                            op0=ALU.is_ge), inc=dve)
                        P.op("vector", lambda e: e.tensor_tensor(
                            s_comb[:, :], s_msk[:, :], s_sc5[:, :], op=ALU.mult),
                            inc=dve)
                        P.op("vector", lambda e: e.reduce_sum(
                            s_wsum[:, :], s_comb[:, :], axis=AX.X), inc=dve)
                        P.op("vector", lambda e: e.tensor_scalar(
                            s_wsum[:, :], s_wsum[:, :], 1e-20, None, op0=ALU.add),
                            inc=dve)
                        P.op("vector", lambda e: e.reciprocal(
                            s_wsum[:, :], s_wsum[:, :]), inc=dve)
                        P.op("vector", lambda e: e.tensor_scalar(
                            s_wsum[:, :], s_wsum[:, :], 2.5, None, op0=ALU.mult),
                            inc=dve)
                        d_cb = P.op("vector", lambda e: e.tensor_scalar(
                            s_comb[:, :], s_comb[:, :], s_wsum[:, :1], None,
                            op0=ALU.mult), inc=dve)
                        pe_tr = P.op("tensor", lambda e: e.transpose(
                            p_tr5[:, :], s_comb[:, :], s_id[:, :]),
                            waits=[(dve, d_cb), (act, p5["tr_free"])], inc=pe)
                        p5["tr_free"] = P.op("scalar", lambda e, tt=tt: e.activation(
                            s_combT[:, tt*128:(tt+1)*128], p_tr5[:, :], AF.Copy),
                            waits=[(pe, pe_tr), (scf, p5["combT_free"])], inc=act)

                    a_combT = act.c
                    P.op("scalar", lambda e, cko=cko: e.dma_start(
                        out=cflat_spill.ap()[:, CK*cko:CK*(cko+1)],
                        in_=s_combT[:, :]),
                        waits=[(act, a_combT)], inc=scf, inc_n=16)
                    p5["combT_free"] = scf.c
                    p5["h5_rt"][buf] = dve.c

            rt_pend = []
            for cko in range(CKPC):
                buf = cko % 2
                c_yg = None
                for r in range(8):
                    c_yg = P.op("gpsimd",
                                lambda e, r=r, cko=cko, buf=buf: e.indirect_dma_start(
                        out=s_yg[:, (buf*8 + r)*CK:(buf*8 + r + 1)*CK],
                        out_offset=None,
                        in_=cc_y_out.ap().rearrange(
                            "q p (k f) -> (q p k) f", f=CK),
                        in_offset=IndirectOffsetOnAxis(
                            s_ckidx[:, r*CKPC+cko:r*CKPC+cko+1], 0)),
                        waits=[(cc, cc_y_done)] +
                              ([(dve, p5["yg_free"][buf])] if r == 0 else []),
                        inc=lyg[buf], inc_n=16)
                a_sq = None
                for d in range(8):
                    a_sq = P.op("scalar", lambda e, d=d, buf=buf: e.activation(
                        s_ysq[:, d*CK:(d+1)*CK],
                        s_yg[:, (buf*8 + d)*CK:(buf*8 + d + 1)*CK].bitcast(F32),
                        AF.Square),
                        waits=[(lyg[buf], c_yg)] if d == 0 else (), inc=act)
                for d in range(8):
                    P.op("tensor", lambda e, d=d, buf=buf: e.matmul(
                        p_s1[:, :], s_onc[:, :],
                        s_yg[:, (buf*8 + d)*CK:(buf*8 + d + 1)*CK],
                        start=(d == 0), stop=(d == 7)),
                        waits=([(lyg[buf], c_yg), (dve, p5["ps1_free"])]
                               if d == 0 else ()),
                        inc=pe if d == 7 else None)
                pe_s1 = pe.c
                for d in range(8):
                    P.op("tensor", lambda e, d=d: e.matmul(
                        p_s2[:, :], s_onc[:, :], s_ysq[:, d*CK:(d+1)*CK],
                        start=(d == 0), stop=(d == 7)),
                        waits=([(act, a_sq), (dve, p5["ps2_free"])]
                               if d == 0 else ()),
                        inc=pe if d == 7 else None)
                pe_s2 = pe.c
                d_mu = P.op("vector", lambda e: e.tensor_scalar(
                    s_mu[:, :], p_s1[0:1, :], 1.0 / DM, None, op0=ALU.mult),
                    waits=[(pe, pe_s1)], inc=dve)
                p5["ps1_free"] = d_mu
                P.op("vector", lambda e: e.tensor_tensor(
                    s_m2[:, :], s_mu[:, :], s_mu[:, :], op=ALU.mult), inc=dve)
                d_var = P.op("vector", lambda e: e.scalar_tensor_tensor(
                    s_var[:, :], p_s2[0:1, :], 1.0 / DM, s_m2[:, :],
                    op0=ALU.mult, op1=ALU.subtract),
                    waits=[(pe, pe_s2)], inc=dve)
                p5["ps2_free"] = d_var
                P.op("scalar", lambda e: e.activation(
                    s_r1[:, :], s_var[:, :], AF.Ln, bias=1e-5),
                    waits=[(dve, d_var)], inc=act)
                a_rstd = P.op("scalar", lambda e: e.activation(
                    s_rstd[:, :], s_r1[:, :], AF.Exp, scale=-0.5), inc=act)
                d_nmu = P.op("vector", lambda e: e.scalar_tensor_tensor(
                    s_nmu[:, :], s_mu[:, :], -1.0, s_rstd[:, :].bitcast(F32),
                    op0=ALU.mult, op1=ALU.mult),
                    waits=[(act, a_rstd)], inc=dve)
                pe_b1 = P.op("tensor", lambda e: e.matmul(
                    p_bc1[:, :], s_onr[:, :], s_rstd[:, :], start=True, stop=True),
                    waits=[(act, a_rstd), (act, p5["bc1_free"] if cko == 0 else 0),
                           (dve, p5["bc1_free"] if cko > 0 else 0)], inc=pe)
                pe_b2 = P.op("tensor", lambda e: e.matmul(
                    p_bc2[:, :], s_onr[:, :], s_nmu[:, :], start=True, stop=True),
                    waits=[(dve, d_nmu), (dve, p5["bc2_free"])], inc=pe)
                d_h = None
                for d in range(8):
                    P.op("vector", lambda e, d=d, buf=buf: e.tensor_tensor(
                        s_h5[:, (buf*8 + d)*CK:(buf*8 + d + 1)*CK],
                        s_yg[:, (buf*8 + d)*CK:(buf*8 + d + 1)*CK].bitcast(F32),
                        p_bc1[:, :], op=ALU.mult),
                        waits=[(pe, pe_b1)] +
                              ([(ssp[buf], p5["h5_free"][buf]),
                                (dve, p5["h5_rt"][buf])]
                               if d == 0 else []), inc=dve)
                    d_h = P.op("vector", lambda e, d=d, buf=buf: e.tensor_tensor(
                        s_h5[:, (buf*8 + d)*CK:(buf*8 + d + 1)*CK],
                        s_h5[:, (buf*8 + d)*CK:(buf*8 + d + 1)*CK],
                        p_bc2[:, :], op=ALU.add),
                        waits=[(pe, pe_b2)] if d == 0 else (), inc=dve)
                p5["yg_free"][buf] = d_h
                p5["bc1_free"] = d_h
                p5["bc2_free"] = d_h
                d_hb = None
                for d in range(8):
                    d_hb = P.op("vector", lambda e, d=d, buf=buf: e.tensor_copy(
                        s_hb5[:, d*CK:(d+1)*CK],
                        s_h5[:, (buf*8 + d)*CK:(buf*8 + d + 1)*CK]),
                        waits=[(shb, p5["hb_free"])] if d == 0 else (),
                        inc=dve)
                sp1 = P.op("scalar", lambda e, cko=cko, buf=buf: e.dma_start(
                    out=xres_spill.ap()[:, CK*cko:CK*(cko+1)].rearrange(
                        "(dt p) f -> p dt f", p=128),
                    in_=s_h5[:, buf*8*CK:(buf*8 + 8)*CK]),
                    waits=[(dve, d_h)], inc=ssp[buf], inc_n=16)
                P.op("scalar", lambda e, cko=cko: e.dma_start(
                    out=hb_spill.ap()[:, CK*cko:CK*(cko+1)].rearrange(
                        "(dt p) f -> p dt f", p=128),
                    in_=s_hb5[:, :]),
                    waits=[(dve, d_hb)], inc=shb, inc_n=16)
                p5["hb_free"] = shb.c
                p5["h5_free"][buf] = ssp[buf].c

                rt_pend.append((cko, buf, d_h))
                if rt_pend and len(rt_pend) > 1:
                    emit_routing(*rt_pend.pop(0))

            while rt_pend:
                emit_routing(*rt_pend.pop(0))
            cflat_ready = scf.c
            hb_all = shb.c
            xres_all = [ssp[0].c, ssp[1].c]
            bar5 = barrier(P)
            with nc.Block("ph5") as blk:
                P.emit(blk)

        P = new_phase()
        phase_wait(P, bar5)
        p6ctx = ExitStack()
        with p6ctx:
            def sb(name, shape, dt=F32):
                return p6ctx.enter_context(nc.sbuf_tensor(name, list(shape), dt))

            s_wg = sb("s_wg", (128, NSLOT * 8 * EI), BF16)
            s_wu = sb("s_wu", (128, NSLOT * 8 * EI), BF16)
            s_wd = sb("s_wd", (128, NSLOT * 4 * DM), BF16)
            s_hb6 = sb("s_hb6", (128, 8 * TB), BF16)
            s_routed = sb("s_routed", (128, 8 * TB))
            s_cbc = sb("s_cbc", (128, 2 * EG * TB))
            s_crow = sb("s_crow", (1, EG * TB), F32R)
            s_sg6 = sb("s_sg6", (128, 4 * SUB))
            s_x6 = sb("s_x6", (128, 2 * 4 * SUB), BF16)
            s_xr6 = sb("s_xr6", (128, 2 * TB))
            pg = [p6ctx.enter_context(nc.psum_tensor(f"pg{i}", [128, 512], F32))
                  for i in range(2)]
            pu = [p6ctx.enter_context(nc.psum_tensor(f"pu{i}", [128, 512], F32))
                  for i in range(2)]
            pr6 = [p6ctx.enter_context(nc.psum_tensor(f"pr{i}", [128, 512], F32))
                   for i in range(4)]

            def wslot(base, slot, n):
                return base[:, slot * n:(slot + 1) * n]

            p6 = {"crow_free": [0, 0],
                  "slot_free": [0] * NSLOT, "slot_ready": [0] * NSLOT,
                  "cbc_free": [0, 0], "cbc_ready": [(0, 0), (0, 0)],
                  "sg_free": 0, "x_free": [0, 0],
                  "routed_free": 0, "pr_free": 0, "pg_free": [0, 0],
                  "pu_free": [0, 0], "hb_free": 0, "xr_free": [0, 0]}

            for tb in range(2):
                c_hb = P.op("sync", lambda e, tb=tb: e.dma_start(
                    out=s_hb6[:, :],
                    in_=hb_spill.ap()[:, TB*tb:TB*(tb+1)].rearrange(
                        "(kd p) t -> p kd t", p=128)),
                    waits=[(shb, hb_all), (pe, p6["hb_free"])],
                    inc=lhb, inc_n=16)
                d_ms = None
                for ht in range(8):
                    d_ms = P.op("vector", lambda e, ht=ht: e.memset(
                        s_routed[:, ht*TB:(ht+1)*TB], 0.0),
                        waits=[(sp2, p6["routed_free"])] if ht == 0 else (),
                        inc=dve)

                pend_dn = []

                def emit_down(Pp, item):
                    (j_, sub_, sl_, xslot_, d_x_, need_ms) = item
                    w_extra = []
                    if j_ == 0:
                        w_extra.append((dve, p6["pr_free"]))
                        if need_ms:
                            w_extra.append((dve, d_ms))
                    for ht in range(8):
                        for ki in range(4):
                            w = []
                            if ht == 0 and ki == 0:
                                w = [(dve, d_x_)] + w_extra
                            Pp.op("tensor",
                                  lambda e, ht=ht, ki=ki, sl=sl_, j=j_,
                                  xs=xslot_: e.matmul(
                                pr6[ht // 2][:, (ht % 2)*SUB:
                                             (ht % 2 + 1)*SUB],
                                wslot(s_wd, sl, 4*DM)[
                                    :, ki*DM + ht*128:
                                    ki*DM + (ht+1)*128],
                                s_x6[:, xs*TB + ki*SUB:
                                     xs*TB + (ki+1)*SUB],
                                start=(ki == 0 and j == 0
                                       and ht % 2 == 0),
                                stop=(ki == 3 and j == EG - 1
                                      and ht % 2 == 1)),
                                waits=w,
                                inc=pe if (ki == 3 and ht == 7)
                                else None)
                    dn_done = pe.c
                    p6["x_free"][xslot_] = dn_done
                    p6["slot_free"][sl_] = dn_done
                    if j_ == EG - 1:
                        for q in range(4):
                            P.op("vector", lambda e, q=q, sub=sub_:
                                 e.tensor_tensor(
                                s_routed[:, (2*q)*TB + sub*SUB:
                                         (2*q)*TB + sub*SUB + SUB],
                                pr6[q][:, 0:SUB],
                                s_routed[:, (2*q)*TB + sub*SUB:
                                         (2*q)*TB + sub*SUB + SUB],
                                op=ALU.add),
                                waits=[(pe, dn_done)], inc=dve)
                            P.op("vector", lambda e, q=q, sub=sub_:
                                 e.tensor_tensor(
                                s_routed[:, (2*q+1)*TB + sub*SUB:
                                         (2*q+1)*TB + sub*SUB + SUB],
                                pr6[q][:, SUB:2*SUB],
                                s_routed[:, (2*q+1)*TB + sub*SUB:
                                         (2*q+1)*TB + sub*SUB + SUB],
                                op=ALU.add), inc=dve)
                        p6["pr_free"] = dve.c


                def build_cbc(eg2):
                    cs = eg2 % 2
                    egs2 = [eg2 * EG + jj for jj in range(EG)]
                    cbc_dve = 0
                    cbc_act = 0
                    for j, ei in enumerate(egs2):
                        if ei < NE:
                            c_cr = P.op("scalar", lambda e, ei=ei, tb=tb, j=j,
                                        cs=cs: e.dma_start(
                                out=s_crow[:, j*TB:(j+1)*TB],
                                in_=cflat_spill.ap()[ei:ei+1, TB*tb:TB*(tb+1)]),
                                waits=[(scf, cflat_ready),
                                       (pe, p6["crow_free"][j])],
                                inc=scr[j], inc_n=16)
                            pe_o1 = P.op("tensor", lambda e, j=j: e.matmul(
                                pg[0][:, :], s_onr[:, :],
                                s_crow[:, j*TB:j*TB + 512],
                                start=True, stop=True),
                                waits=[(scr[j], c_cr),
                                       (act, p6["pg_free"][0]),
                                       (dve, p6["cbc_free"][cs])], inc=pe)
                            pe_o2 = P.op("tensor", lambda e, j=j: e.matmul(
                                pg[1][:, :], s_onr[:, :],
                                s_crow[:, j*TB + 512:(j+1)*TB],
                                start=True, stop=True),
                                waits=[(act, p6["pg_free"][1])], inc=pe)
                            p6["crow_free"][j] = pe_o2
                            a1 = P.op("scalar", lambda e, j=j, cs=cs: e.activation(
                                s_cbc[:, cs*EG*TB + j*TB:cs*EG*TB + j*TB + 512],
                                pg[0][:, :], AF.Copy),
                                waits=[(pe, pe_o1)], inc=act)
                            a2 = P.op("scalar", lambda e, j=j, cs=cs: e.activation(
                                s_cbc[:, cs*EG*TB + j*TB + 512:
                                      cs*EG*TB + (j+1)*TB],
                                pg[1][:, :], AF.Copy),
                                waits=[(pe, pe_o2)], inc=act)
                            p6["pg_free"][0] = a1
                            p6["pg_free"][1] = a2
                            cbc_act = a2
                        else:
                            cbc_dve = P.op("vector", lambda e, j=j, cs=cs:
                                           e.memset(
                                s_cbc[:, cs*EG*TB + j*TB:
                                      cs*EG*TB + (j+1)*TB], 1.0),
                                waits=[(dve, p6["cbc_free"][cs])], inc=dve)
                    p6["cbc_ready"][cs] = (cbc_act, cbc_dve)

                build_cbc(0)
                for eg_i in range(NEX // EG):
                    egs = [eg_i * EG + j for j in range(EG)]
                    slots = [ei % NSLOT for ei in egs]
                    for ei, sl in zip(egs, slots):
                        P.op("gpsimd", lambda e, ei=ei, sl=sl: e.dma_start(
                            out=wslot(s_wg, sl, 8 * EI),
                            in_=wgT.ap()[ei].rearrange("(k p) i -> p k i",
                                                       p=128)),
                            waits=[(pe, p6["slot_free"][sl])],
                            inc=sw[sl], inc_n=16)
                        P.op("gpsimd", lambda e, ei=ei, sl=sl: e.dma_start(
                            out=wslot(s_wu, sl, 8 * EI),
                            in_=wuT.ap()[ei].rearrange("(k p) i -> p k i",
                                                       p=128)),
                            inc=sw[sl], inc_n=16)
                        P.op("gpsimd", lambda e, ei=ei, sl=sl: e.dma_start(
                            out=wslot(s_wd, sl, 4 * DM),
                            in_=wdT.ap()[ei].rearrange("(k p) h -> p k h",
                                                       p=128)),
                            inc=sw[sl], inc_n=16)
                        p6["slot_ready"][sl] = sw[sl].c
                    cs = eg_i % 2
                    cbc_act, cbc_dve = p6["cbc_ready"][cs]
                    for sub in range(NSUB):
                        for j, ei in enumerate(egs):
                            sl = slots[j]
                            xslot = (sub * EG + j) % 2
                            for it in range(4):
                                for kd in range(8):
                                    w = []
                                    if it == 0 and kd == 0:
                                        w = [(sw[sl], p6["slot_ready"][sl]),
                                             (lhb, c_hb),
                                             (act, p6["pg_free"][0]),
                                             (act, p6["pg_free"][1])]
                                        if cbc_act:
                                            w.append((act, cbc_act))
                                        if cbc_dve:
                                            w.append((dve, cbc_dve))
                                    P.op("tensor",
                                         lambda e, it=it, kd=kd, sl=sl, sub=sub:
                                         e.matmul(
                                        pg[it // 2][:, (it % 2)*SUB:
                                                    (it % 2 + 1)*SUB],
                                        wslot(s_wg, sl, 8*EI)[
                                            :, kd*EI + it*128: kd*EI + (it+1)*128],
                                        s_hb6[:, kd*TB + sub*SUB:
                                              kd*TB + (sub+1)*SUB],
                                        start=(kd == 0 and it % 2 == 0),
                                        stop=(kd == 7 and it % 2 == 1)),
                                        waits=w,
                                        inc=pe if (kd == 7 and it % 2 == 1)
                                        else None)
                            pe_g = pe.c
                            for it in range(4):
                                for kd in range(8):
                                    w = []
                                    if it == 0 and kd == 0:
                                        w = [(dve, p6["pu_free"][0]),
                                             (dve, p6["pu_free"][1])]
                                    P.op("tensor",
                                         lambda e, it=it, kd=kd, sl=sl, sub=sub:
                                         e.matmul(
                                        pu[it // 2][:, (it % 2)*SUB:
                                                    (it % 2 + 1)*SUB],
                                        wslot(s_wu, sl, 8*EI)[
                                            :, kd*EI + it*128: kd*EI + (it+1)*128],
                                        s_hb6[:, kd*TB + sub*SUB:
                                              kd*TB + (sub+1)*SUB],
                                        start=(kd == 0 and it % 2 == 0),
                                        stop=(kd == 7 and it % 2 == 1)),
                                        waits=w,
                                        inc=pe if (kd == 7 and it % 2 == 1)
                                        else None)
                            pe_u = pe.c
                            if pend_dn:
                                emit_down(P, pend_dn.pop(0))
                            a_sg1 = P.op("scalar", lambda e: e.activation(
                                s_sg6[:, 0:512], pg[0][:, :], AF.Silu),
                                waits=[(pe, pe_g), (dve, p6["sg_free"])],
                                inc=act)
                            a_sg2 = P.op("scalar", lambda e: e.activation(
                                s_sg6[:, 512:1024], pg[1][:, :], AF.Silu),
                                inc=act)
                            p6["pg_free"][0] = a_sg1
                            p6["pg_free"][1] = a_sg2
                            d_t1 = P.op("vector", lambda e: e.tensor_tensor(
                                s_sg6[:, 0:512], pu[0][:, :], s_sg6[:, 0:512],
                                op=ALU.mult),
                                waits=[(pe, pe_u), (act, a_sg1)], inc=dve)
                            d_t2 = P.op("vector", lambda e: e.tensor_tensor(
                                s_sg6[:, 512:1024], pu[1][:, :],
                                s_sg6[:, 512:1024], op=ALU.mult),
                                waits=[(act, a_sg2)], inc=dve)
                            p6["pu_free"][0] = d_t1
                            p6["pu_free"][1] = d_t2
                            d_x = None
                            for it in range(4):
                                d_x = P.op("vector",
                                           lambda e, it=it, j=j, sub=sub,
                                           xs=xslot, cs=cs: e.tensor_tensor(
                                    s_x6[:, xs*TB + it*SUB:
                                         xs*TB + (it+1)*SUB],
                                    s_sg6[:, it*SUB:(it+1)*SUB],
                                    s_cbc[:, cs*EG*TB + j*TB + sub*SUB:
                                          cs*EG*TB + j*TB + sub*SUB + SUB],
                                    op=ALU.mult),
                                    waits=([(dve, d_t2),
                                            (pe, p6["x_free"][xslot])]
                                           if it == 0 else ()), inc=dve)
                            p6["sg_free"] = d_x
                            pend_dn.append((j, sub, sl, xslot, d_x,
                                            eg_i == 0 and sub == 0))
                    p6["cbc_free"][eg_i % 2] = dve.c
                    if eg_i + 1 < NEX // EG:
                        build_cbc(eg_i + 1)
                if True:
                    while pend_dn:
                        emit_down(P, pend_dn.pop(0))

                for ht in range(8):
                    hbuf = ht % 2
                    c_xr = P.op("sync", lambda e, ht=ht, tb=tb, hbuf=hbuf:
                                e.dma_start(
                        out=s_xr6[:, hbuf*TB:(hbuf+1)*TB],
                        in_=xres_spill.ap()[128*ht:128*(ht+1), TB*tb:TB*(tb+1)]),
                        waits=[(dve, p6["xr_free"][hbuf]),
                               (ssp[0], xres_all[0]), (ssp[1], xres_all[1])],
                        inc=lxr[hbuf], inc_n=16)
                    d_p2 = P.op("vector", lambda e, ht=ht, hbuf=hbuf:
                                e.tensor_tensor(
                        s_routed[:, ht*TB:(ht+1)*TB],
                        s_routed[:, ht*TB:(ht+1)*TB],
                        s_xr6[:, hbuf*TB:(hbuf+1)*TB], op=ALU.add),
                        waits=[(lxr[hbuf], c_xr)], inc=dve)
                    p6["xr_free"][hbuf] = d_p2
                    P.op("scalar", lambda e, ht=ht, tb=tb: e.dma_start(
                        out=pre2_spill.ap()[128*ht:128*(ht+1), TB*tb:TB*(tb+1)],
                        in_=s_routed[:, ht*TB:(ht+1)*TB]),
                        waits=[(dve, d_p2)], inc=sp2, inc_n=16)
                p6["routed_free"] = sp2.c
                p6["hb_free"] = pe.c

            pre2_done = sp2.c
            bar6 = barrier(P)
            with nc.Block("ph6") as blk:
                P.emit(blk)

        P = new_phase()
        phase_wait(P, bar6)
        p7ctx = ExitStack()
        with p7ctx:
            def sb(name, shape, dt=F32):
                return p7ctx.enter_context(nc.sbuf_tensor(name, list(shape), dt))

            s_p7 = sb("s_p7", (128, 2 * 8 * CK))
            s_q7 = sb("s_q7", (128, 8 * CK))
            s_out7 = sb("s_out7", (128, 2 * DM))
            s_rs7 = sb("s_rs7", (1, CK))
            s_nm7 = sb("s_nm7", (1, CK))
            s_sclr = sb("s_sclr", (128, 1))
            s_sclm = sb("s_sclm", (128, 1))
            s_r17 = sb("s_r17", (1, CK))
            s_mu7 = sb("s_mu7", (1, CK))
            s_m27 = sb("s_m27", (1, CK))
            s_v7 = sb("s_v7", (1, CK))
            p_t7 = [p7ctx.enter_context(nc.psum_tensor(f"p_t7{i}",
                                                       [128, 128], F32))
                    for i in range(2)]
            p_s17 = p7ctx.enter_context(nc.psum_tensor("p_s17", [1, CK], F32))
            p_s27 = p7ctx.enter_context(nc.psum_tensor("p_s27", [1, CK], F32))
            p_sc7 = p7ctx.enter_context(nc.psum_tensor("p_sc7", [128, 2], F32))

            p7 = {"p7_free": [0, 0], "out_free": [0, 0], "q_free": 0,
                  "tr_free": [0, 0], "scl_free": 0, "ps1_free": 0,
                  "ps2_free": 0}
            for cko in range(CKPC):
                buf = cko % 2
                c_p2 = P.op("sync", lambda e, cko=cko, buf=buf: e.dma_start(
                    out=s_p7[:, buf*8*CK:(buf + 1)*8*CK],
                    in_=pre2_spill.ap()[:, CK*cko:CK*(cko+1)].rearrange(
                        "(dt p) t -> p dt t", p=128)),
                    waits=[(sp2, pre2_done), (pe, p7["p7_free"][buf])],
                    inc=lp2[buf], inc_n=16)
                a_sq = None
                for d in range(8):
                    a_sq = P.op("scalar", lambda e, d=d, buf=buf: e.activation(
                        s_q7[:, d*CK:(d+1)*CK],
                        s_p7[:, (buf*8 + d)*CK:(buf*8 + d+1)*CK], AF.Square),
                        waits=([(lp2[buf], c_p2), (pe, p7["q_free"])]
                               if d == 0 else ()), inc=act)
                for d in range(8):
                    P.op("tensor", lambda e, d=d, buf=buf: e.matmul(
                        p_s17[:, :], s_onc32[:, :],
                        s_p7[:, (buf*8 + d)*CK:(buf*8 + d+1)*CK],
                        start=(d == 0), stop=(d == 7)),
                        waits=([(lp2[buf], c_p2), (dve, p7["ps1_free"])]
                               if d == 0 else ()),
                        inc=pe if d == 7 else None)
                pe_s1 = pe.c
                for d in range(8):
                    P.op("tensor", lambda e, d=d: e.matmul(
                        p_s27[:, :], s_onc32[:, :], s_q7[:, d*CK:(d+1)*CK],
                        start=(d == 0), stop=(d == 7)),
                        waits=([(act, a_sq), (dve, p7["ps2_free"])]
                               if d == 0 else ()),
                        inc=pe if d == 7 else None)
                pe_s2 = pe.c
                p7["q_free"] = pe_s2
                d_mu = P.op("vector", lambda e: e.tensor_scalar(
                    s_mu7[:, :], p_s17[0:1, :], 1.0 / DM, None, op0=ALU.mult),
                    waits=[(pe, pe_s1)], inc=dve)
                p7["ps1_free"] = d_mu
                P.op("vector", lambda e: e.tensor_tensor(
                    s_m27[:, :], s_mu7[:, :], s_mu7[:, :], op=ALU.mult), inc=dve)
                d_v = P.op("vector", lambda e: e.scalar_tensor_tensor(
                    s_v7[:, :], p_s27[0:1, :], 1.0 / DM, s_m27[:, :],
                    op0=ALU.mult, op1=ALU.subtract),
                    waits=[(pe, pe_s2)], inc=dve)
                p7["ps2_free"] = d_v
                P.op("scalar", lambda e: e.activation(
                    s_r17[:, :], s_v7[:, :], AF.Ln, bias=1e-5),
                    waits=[(dve, d_v)], inc=act)
                a_rs = P.op("scalar", lambda e: e.activation(
                    s_rs7[:, :], s_r17[:, :], AF.Exp, scale=-0.5), inc=act)
                d_nm = P.op("vector", lambda e: e.scalar_tensor_tensor(
                    s_nm7[:, :], s_mu7[:, :], -1.0, s_rs7[:, :],
                    op0=ALU.mult, op1=ALU.mult),
                    waits=[(act, a_rs)], inc=dve)
                for tt in range(4):
                    obuf = tt % 2
                    pe_sc1 = P.op("tensor", lambda e, tt=tt: e.transpose(
                        p_sc7[:, 0:1], s_rs7[:, tt*128:(tt+1)*128], s_id[0:1, 0:1]),
                        waits=[(act, a_rs), (dve, d_nm),
                               (dve, p7["scl_free"])], inc=pe)
                    pe_sc2 = P.op("tensor", lambda e, tt=tt: e.transpose(
                        p_sc7[:, 1:2], s_nm7[:, tt*128:(tt+1)*128], s_id[0:1, 0:1]),
                        inc=pe)
                    a_sc1 = P.op("scalar", lambda e: e.activation(
                        s_sclr[:, :], p_sc7[:, 0:1], AF.Copy),
                        waits=[(pe, pe_sc2)], inc=act)
                    a_scl = P.op("scalar", lambda e: e.activation(
                        s_sclm[:, :], p_sc7[:, 1:2], AF.Copy), inc=act)
                    p7["scl_free"] = a_scl
                    for d in range(8):
                        dbuf = d % 2
                        pe_tr = P.op("tensor",
                                     lambda e, d=d, tt=tt, buf=buf, dbuf=dbuf:
                                     e.transpose(
                            p_t7[dbuf][:, :],
                            s_p7[:, (buf*8 + d)*CK + tt*128:
                                 (buf*8 + d)*CK + (tt+1)*128],
                            s_id[:, :]),
                            waits=[(dve, p7["tr_free"][dbuf])], inc=pe)
                        a_o = P.op("vector",
                                   lambda e, d=d, dbuf=dbuf, obuf=obuf:
                                   e.tensor_scalar(
                            s_out7[:, obuf*DM + d*128: obuf*DM + (d+1)*128],
                            p_t7[dbuf][:, :], s_sclr[:, 0:1], s_sclm[:, 0:1],
                            op0=ALU.mult, op1=ALU.add),
                            waits=[(pe, pe_tr), (act, a_scl)] +
                                  ([(so[obuf], p7["out_free"][obuf])]
                                   if d == 0 else []),
                            inc=dve)
                        p7["tr_free"][dbuf] = a_o
                    p7["scl_free"] = dve.c
                    if tt == 3:
                        p7["p7_free"][buf] = pe.c
                    a_last = dve.c
                    P.op("sync", lambda e, cko=cko, tt=tt, obuf=obuf:
                         e.dma_start(
                        out=out_h.ap()[cko*CK + tt*128: cko*CK + (tt+1)*128, :],
                        in_=s_out7[:, obuf*DM:(obuf+1)*DM]),
                        waits=[(dve, a_last)], inc=so[obuf], inc_n=16)
                    p7["out_free"][obuf] = so[obuf].c

            P.op("sync", lambda e: None,
                 waits=[(so[0], so[0].c), (so[1], so[1].c)])
            with nc.Block("ph7") as blk:
                P.emit(blk)

    return nc


def kernel(**inputs):
    f = lambda k: np.asarray(inputs[k], np.float32)
    u = np.ascontiguousarray(f("L0_attn_u").reshape(T, DM))
    conv_w = f("L0_attn_conv_w")[:, 0, :]
    conv_b = f("L0_attn_conv_b").reshape(DM, 1)
    x_proj_w = f("L0_attn_x_proj_w")
    dt_proj_w = f("L0_attn_dt_proj_w")
    dt_proj_b = f("L0_attn_dt_proj_b")
    A_log = f("L0_attn_A_log")
    D_skip = f("L0_attn_D_skip")
    gate_w = f("L0_ffn_gate_w")
    gate_bias = f("L0_ffn_gate_bias")
    w_gate = f("L0_ffn_w_gate")
    w_up = f("L0_ffn_w_up")
    w_down = f("L0_ffn_w_down")
    sh_gate = f("L0_ffn_sh_gate")
    sh_up = f("L0_ffn_sh_up")
    sh_down = f("L0_ffn_sh_down")

    wgT = np.empty((NEX, DM, EI), np.float32)
    wuT = np.empty((NEX, DM, EI), np.float32)
    wdT = np.empty((NEX, EI, DM), np.float32)
    wgT[:NE] = w_gate.transpose(0, 2, 1)
    wuT[:NE] = w_up.transpose(0, 2, 1)
    wdT[:NE] = w_down.transpose(0, 2, 1)
    shgT = sh_gate.T
    shuT = sh_up.T
    shdT = sh_down.T
    wgT[NE] = shgT[:, :EI]; wgT[NE+1] = shgT[:, EI:]
    wuT[NE] = shuT[:, :EI]; wuT[NE+1] = shuT[:, EI:]
    wdT[NE] = shdT[:EI, :]; wdT[NE+1] = shdT[EI:, :]
    wgT = np.ascontiguousarray(wgT)
    wuT = np.ascontiguousarray(wuT)
    wdT = np.ascontiguousarray(wdT)

    selrep = np.zeros((16, 128, 128), np.float32)
    selcon = np.zeros((16, 128, 128), np.float32)
    for g in range(16):
        for n in range(16):
            for j in range(8):
                selrep[g, 8 * g + j, n * 8 + j] = 1.0
                selcon[g, n * 8 + j, 8 * g + j] = 1.0
    selbc = np.zeros((NST, 128), np.float32)
    for n in range(16):
        selbc[n, n * 8:(n + 1) * 8] = 1.0

    shared = {
        "xpwT": np.ascontiguousarray(x_proj_w.T),
        "cw": np.ascontiguousarray(conv_w),
        "cb": conv_b,
        "gatewT": np.ascontiguousarray(gate_w.T),
        "gateb": gate_bias.reshape(1, NE),
        "selrep": selrep, "selcon": selcon, "selbc": selbc,
        "ones_row": np.ones((1, 128), np.float32),
        "ones_col": np.ones((128, 1), np.float32),
        "ident": np.eye(128, dtype=np.float32),
        "wgT": wgT, "wuT": wuT, "wdT": wdT,
    }

    in_maps = []
    for c in range(NCORE):
        t0 = TPC * c
        halo = (np.zeros((3, DM), np.float32) if (t0 % LQ) == 0
                else u[t0-3:t0])
        u_tokT_a = np.ascontiguousarray(
            np.concatenate([halo, u[t0:t0+TPC]], axis=0).T)
        ch0 = CPC * c
        u_chanT_a = np.ascontiguousarray(u[:, ch0:ch0+CPC].T)
        pvec = np.arange(128)
        gvec = np.arange(16)
        alog_a = A_log[ch0 + 8 * gvec[None, :] + (pvec[:, None] % 8),
                       pvec[:, None] // 8].astype(np.float32)
        rr = np.arange(8)
        jj = np.arange(CKPC)
        kk = CKPC * c + jj
        qq = kk // 8
        kin = kk % 8
        idx = ((qq[None, None, :] * 1024 + 128 * rr[None, :, None]
                + pvec[:, None, None]) * 8 + kin[None, None, :]
               ).reshape(128, 8 * CKPC).astype(np.int32)
        m = dict(shared)
        m.update({
            "u_tokT": u_tokT_a,
            "u_chanT": u_chanT_a,
            "dtwT": np.ascontiguousarray(dt_proj_w[ch0:ch0+CPC].T),
            "dtb": dt_proj_b[ch0:ch0+CPC].reshape(CPC, 1),
            "alog": np.ascontiguousarray(alog_a),
            "dcol": D_skip[ch0:ch0+CPC].reshape(CPC, 1),
            "ckidx": np.ascontiguousarray(idx),
        })
        in_maps.append(m)

    nc = bass.Bass("TRN2", target_bir_lowering=False, debug=False)
    nc = _build(nc)
    res = run_bass_kernel_spmd(nc, in_maps, core_ids=list(range(NCORE)))
    out = np.concatenate([np.asarray(res.results[c]["out"], np.float32)
                          for c in range(NCORE)], axis=0)
    return out.reshape(B, LQ, DM)
